# version 09
# baseline (speedup 1.0000x reference)
import numpy as np
import ml_dtypes
import concourse.bass as bass
import concourse.mybir as mybir
from concourse.bass_utils import run_bass_kernel_spmd

F32 = mybir.dt.float32
BF16 = mybir.dt.bfloat16
U8 = mybir.dt.uint8
AF = mybir.ActivationFunctionType
ALU = mybir.AluOpType
AX = mybir.AxisListType.X

MAGIC = 12582912.0
NEGBIG = -1e30
LNEPS = 1e-5
QCLIP = 1e-5
E = 1024
H = 4096
ECH = 8
HCH = 32
SBUF_CAP = 212_256


class Prog:

    ENGINES = ("SP", "PE", "ACT", "DVE", "POOL", "PD",
               "AD0", "AD1", "AD2", "AD3")

    def __init__(self):
        self.ops = []
        self.res = {}
        self.seed = {}
        self.last_dma = {}
        self.last_xbar = None
        self.n_act_dma = 0
        self.dma_stream_of = []
        import os
        self.serial = set(os.environ.get("KSYNC", "").split(",")) - {""}

    def _touch(self, k, depset):
        if k in self.seed:
            depset.update(self.seed.pop(k))
        r = self.res.get(k)
        if r is None:
            r = self.res[k] = {"w": None, "r": []}
        return r

    def op(self, engine, emit, reads=(), writes=(), deps=(), dma=False,
           xbar=False):
        idx = len(self.ops)
        depset = set(deps)
        for k in reads:
            r = self._touch(k, depset)
            if r["w"] is not None:
                depset.add(r["w"])
            r["r"].append(idx)
        for k in writes:
            r = self._touch(k, depset)
            if r["w"] is not None:
                depset.add(r["w"])
            depset.update(r["r"])
            r["w"] = idx
            r["r"] = []
        if dma:
            stream = engine
            if engine == "ACT":
                stream = f"AD{self.n_act_dma % 4}"
                self.n_act_dma += 1
            if stream in self.last_dma:
                depset.add(self.last_dma[stream])
            if self.last_xbar is not None:
                depset.add(self.last_xbar)
            if xbar:
                for q, d in self.last_dma.items():
                    depset.add(d)
                self.last_xbar = idx
            self.last_dma[stream] = idx
            self._dma_stream = stream
        depset.discard(idx)
        if engine == "PE":
            depset = {d for d in depset if self.ops[d][0] != "PE"}
        if idx > 0 and ("all" in self.serial or engine in self.serial):
            depset.add(idx - 1)
        if dma:
            st = self._dma_stream
            if engine == "POOL":
                st = "PD"
            elif engine == "SP":
                st = "SP"
            self.dma_stream_of.append(st)
        else:
            self.dma_stream_of.append(None)
        self.ops.append((engine, emit, dma, depset))
        return idx

    def fence(self, new_keys, old_keys):
        dep = set()
        for k in old_keys:
            r = self.res.get(k)
            if r is not None:
                if r["w"] is not None:
                    dep.add(r["w"])
                dep.update(r["r"])
        for k in new_keys:
            self.seed[k] = set(self.seed.get(k, ())) | dep

    def build(self, nc, block, sems):
        needed = set()
        for _, _, _, deps in self.ops:
            needed.update(deps)
        incs = []
        cums = []
        cum = {e: 0 for e in self.ENGINES}
        streams = []
        for i, (eng, _, dma, _) in enumerate(self.ops):
            inc = 16 if dma else (1 if i in needed else 0)
            incs.append(inc)
            stream = self.dma_stream_of[i] if dma else eng
            streams.append(stream)
            cum[stream] += inc
            cums.append(cum[stream])
        per = {e: [] for e in self.ENGINES}
        for i, (eng, emit, dma, deps) in enumerate(self.ops):
            per[eng].append((i, emit, incs[i], deps))
        engobj = {"SP": nc.sync, "PE": nc.tensor, "ACT": nc.scalar,
                  "DVE": nc.vector, "POOL": nc.gpsimd}
        openg = streams

        def make_section(ename):
            ops = per[ename]
            eo = engobj[ename]

            def section(_):
                waited = {f: 0 for f in self.ENGINES}
                for i, emit, inc, deps in ops:
                    want = {}
                    for d in deps:
                        f = openg[d]
                        v = cums[d]
                        if v > want.get(f, 0):
                            want[f] = v
                    for f, v in want.items():
                        if waited[f] < v:
                            eo.wait_ge(sems[f], v)
                            waited[f] = v
                    inst = emit(eo)
                    if inc:
                        inst.then_inc(sems[streams[i]], inc)

            return section

        for ename, sec in (("SP", block.sync), ("PE", block.tensor),
                           ("ACT", block.scalar), ("DVE", block.vector),
                           ("POOL", block.gpsimd)):
            if per[ename]:
                sec(make_section(ename))


class Mem:

    def __init__(self, nc, cap):
        self.nc = nc
        self.base = (nc.sbuf_base + 31) // 32 * 32
        self.cap = min(cap, nc.sbuf_top - self.base)

    def allocate(self):
        pass

    def at(self, off, name, shape, dtype):
        per = int(np.prod(shape[1:])) * mybir.dt.size(dtype)
        assert off % 32 == 0 and off + per <= self.cap, (name, off, per)
        return self.nc.alloc_sbuf_tensor_at(name, list(shape), dtype,
                                            offset=self.base + off)


def _ceil(a, m):
    return (a + m - 1) // m * m


def build_program(T=2048, NB=2, debug=False):
    nc = bass.Bass("TRN2", target_bir_lowering=False)
    NT = T // 128
    NCHUNK = T // 256
    p = Prog()

    dram = {}

    def din(name, shape, dtype=F32):
        dram[name] = nc.dram_tensor(name, shape, dtype, kind="ExternalInput").ap()
        return dram[name]

    def dout(name, shape, dtype=F32):
        dram[name] = nc.dram_tensor(name, shape, dtype, kind="ExternalOutput").ap()
        return dram[name]

    def dscr(name, shape, dtype):
        kind = "ExternalOutput" if debug else "Internal"
        dram[name] = nc.dram_tensor(name, shape, dtype, kind=kind).ap()
        return dram[name]

    x_d = din("x", [NB, T, E])
    wq_d = din("wq", [E, E]); wk_d = din("wk", [E, E]); wv_d = din("wv", [E, E])
    bq_d = din("bq", [E]); bk_d = din("bk", [E]); bv_d = din("bv", [E])
    w1_d = din("w1", [H, E]); w2_d = din("w2", [E, H])
    cw0_d = din("cw0", [H]); cw1_d = din("cw1", [H]); cw2_d = din("cw2", [H])
    cb_d = din("cb", [H])
    out_d = dout("out", [NB, T, E])

    s_xbf = dscr("s_xbf", [NB, T, E], BF16)
    s_xqT = dscr("s_xqT", [NB, ECH, 128, T + 16], BF16)
    s_xln = dscr("s_xln", [NB, T, E], F32)
    s_m1 = dscr("s_m1", [NB, T + 2], F32)
    s_wq = dscr("s_wq", [E, E], BF16)
    s_wk = dscr("s_wk", [E, E], BF16)
    s_wv = dscr("s_wv", [E, E], BF16)
    s_w1i = dscr("s_w1i", [H, E], BF16)
    s_w2i = dscr("s_w2i", [E, H], BF16)
    if debug:
        d_s2seg = dscr("d_s2seg", [1, 256], F32)
        d_sbc = dscr("d_sbc", [128, 256], F32)
        d_fbc = dscr("d_fbc", [128, 258], F32)
        d_g = dscr("d_g", [HCH, 128, 256], F32)
        d_hq = dscr("d_hq", [HCH, 128, 256], BF16)
        d_mx = dscr("d_mx", [128, 256], F32)
        d_mn = dscr("d_mn", [128, 256], F32)

    mem = Mem(nc, SBUF_CAP)
    mem.allocate()

    off = 0
    def P_(name, shape, dtype):
        nonlocal off
        h = mem.at(off, name, shape, dtype)
        per = int(np.prod(shape[1:])) * mybir.dt.size(dtype)
        off = _ceil(off + per, 32)
        return h

    iden_bf = P_("iden_bf", [128, 128], BF16)
    iden_f = P_("iden_f", [128, 128], F32)
    maskneg = P_("maskneg", [128, 128], F32)
    ones1 = P_("ones1", [1, 128], F32)
    onescol = P_("onescol", [128, 1], F32)
    neghalf = P_("neghalf", [128, 1], F32)
    w0c = P_("w0c", [128, 32], F32)
    w1c = P_("w1c", [128, 32], F32)
    w2c = P_("w2c", [128, 32], F32)
    cbc = P_("cbc", [128, 32], F32)
    bqc = P_("bqc", [128, 8], F32)
    bkc = P_("bkc", [128, 8], F32)
    misc = P_("misc", [128, 64], F32)
    m1ct = P_("m1ct", [128, NT], F32)
    mrow = P_("mrow", [NT, 128], F32)
    mw = P_("mw", [1, 8], F32)
    mwb = P_("mwb", [128, 2], F32)
    zpad = P_("zpad", [128, 8], BF16)
    PERS_END = off
    OB = _ceil(PERS_END, 32)
    AVAIL = SBUF_CAP - OB

    MS, NEGM, VS1, VS2, VT, RSTD, NEGMR, M1, M1C, R1, S1V, BIASV, RCOL = range(13)
    MRAW = 14
    CMAX0 = 16
    CSUM0 = 20
    COLSUM = 24
    MX0 = 28
    RWB = 44

    def mc(i):
        return misc[:, i:i + 1]

    _ap_off, _po = [], 0
    for jt in range(NT):
        _ap_off.append(_po)
        _po = _ceil(_po + (T - jt * 128) * 2, 32)
    A_P = [mem.at(OB + _ap_off[jt], f"A_P{jt}", [128, T - jt * 128], BF16)
           for jt in range(NT)]
    A_xT = [mem.at(OB + e * 4096, f"A_xT{e}", [128, T], BF16) for e in range(ECH)]
    A_WvT = [mem.at(OB + 32768 + e * 2048, f"A_WvT{e}", [128, E], BF16)
             for e in range(ECH)]
    A_BVbc = mem.at(OB + 49152, "A_BVbc", [128, E], F32)
    A_brow = mem.at(OB + 53248, "A_brow", [32, 128], F32)
    A_brow2 = mem.at(OB + 53760, "A_brow2", [1, E], F32)
    A2 = OB + 65536
    A_qT = [mem.at(A2 + e * 4096, f"A_qT{e}", [128, T], BF16) for e in range(ECH)]
    A_kT = [mem.at(A2 + 32768 + e * 4096, f"A_kT{e}", [128, T], BF16)
            for e in range(ECH)]
    A_v = [mem.at(A2 + 65536 + t * 2048, f"A_v{t}", [128, E], BF16)
           for t in range(NT)]
    A_WqT = [mem.at(A2 + 98304 + e * 2048, f"A_WqT{e}", [128, E], BF16)
             for e in range(ECH)]
    A_WkT = [mem.at(A2 + 114688 + e * 2048, f"A_WkT{e}", [128, E], BF16)
             for e in range(ECH)]
    AT = A2 + 131072
    A_x32 = [mem.at(AT + i * 4096, f"A_x32_{i}", [128, E], F32) for i in range(2)]
    A_xbf = [mem.at(AT + 8192 + i * 2048, f"A_xbf{i}", [128, E], BF16)
             for i in range(2)]
    assert AT + 12288 <= mem.cap
    PV_xres = mem.at(A2 + 0, "PV_xres", [128, E], F32)
    PV_xln = [mem.at(A2 + 4096 + i * 4096, f"PV_xln{i}", [128, E], F32)
              for i in range(2)]
    PV_xqstg = mem.at(A2 + 12288, "PV_xqstg", [128, E], BF16)
    PV_xqbf = mem.at(A2 + 14336, "PV_xqbf", [128, E], BF16)
    PV_xsrc = [mem.at(A2 + 16384 + i * 4096, f"PV_xsrc{i}", [128, E], F32)
               for i in range(2)]

    PRS = OB + 57856
    PR_in = [mem.at(PRS + i * 1024, f"PR_in{i}", [128, 256], F32)
             for i in range(4)]
    PR_ob = [mem.at(PRS + 4096 + i * 512, f"PR_ob{i}", [128, 256], BF16)
             for i in range(2)]
    PR_ap = mem.at(PRS + 5120, "PR_ap", [128, 256], F32)

    F_W1qT = [mem.at(OB + e * 8192, f"F_W1qT{e}", [128, H], BF16)
              for e in range(ECH)]
    F_W2qT = [mem.at(OB + 65536 + c * 2048, f"F_W2qT{c}", [128, E], BF16)
              for c in range(HCH)]
    F_g = [[mem.at(OB + 131072 + pr * 16384 + s * 512, f"Fg{pr}_{s}",
                   [128, 256], BF16) for s in range(HCH)] for pr in range(2)]
    FX = OB + 163840
    F_xq = [[mem.at(FX + pr * 4352 + e * 544, f"F_xq{pr}_{e}", [128, 258], BF16)
             for e in range(ECH)] for pr in range(2)]
    F_xlnt = [[mem.at(FX + 8704 + (pr * 2 + i) * 4096, f"F_xlnt{pr}_{i}",
                      [128, E], F32) for i in range(2)] for pr in range(2)]
    F_hs = [mem.at(FX + 25088 + i * 544, f"F_hs{i}", [128, 258], BF16)
            for i in range(2)]
    F_u = [mem.at(FX + 26176 + i * 1024, f"F_u{i}", [128, 256], F32)
           for i in range(2)]
    F_Mx = mem.at(FX + 28224, "F_Mx", [128, 256], BF16)
    F_Mn = mem.at(FX + 28736, "F_Mn", [128, 256], BF16)
    F_fbc = [mem.at(FX + 29248 + i * 1056, f"F_fbc{i}", [128, 258], F32)
             for i in range(2)]
    F_sbc = mem.at(FX + 31360, "F_sbc", [128, 256], F32)
    F_fw = [mem.at(FX + 32384 + i * 1056, f"F_fw{i}", [1, 258], F32)
            for i in range(2)]
    F_s2seg = mem.at(FX + 34496, "F_s2seg", [1, 256], F32)
    F_ev = [mem.at(FX + 35520 + i * 2048, f"F_ev{i}", [128, 512], F32)
            for i in range(2)]
    F_lnt = mem.at(FX + 39616, "F_lnt", [128, 512], F32)
    F_D = [[mem.at(FX + 41664 + (k * 2 + pr) * 256, f"F_D{k}_{pr}",
                   [128, 128], BF16) for pr in range(2)] for k in range(3)]
    assert FX + 43200 <= mem.cap, FX + 43200

    import contextlib
    pstack = contextlib.ExitStack()
    B = [pstack.enter_context(nc.psum_tensor(f"bank{i}", [128, 512], F32))
         for i in range(6)]
    X6 = pstack.enter_context(nc.psum_tensor("xb6", [128, 1024], BF16))
    X7 = pstack.enter_context(nc.psum_tensor("xb7", [128, 1024], BF16))

    p.op("POOL", lambda e: e.iota(out=iden_f[:], pattern=[[1, 128]], base=0,
                                  channel_multiplier=-1,
                                  allow_small_or_imprecise_dtypes=True),
         writes=["idenf"])
    p.op("DVE", lambda e: e.tensor_scalar(out=maskneg[:], in0=iden_f[:],
                                          scalar1=0.0, scalar2=NEGBIG,
                                          op0=ALU.is_lt, op1=ALU.mult),
         reads=["idenf"], writes=["maskneg"])
    p.op("DVE", lambda e: e.tensor_scalar(out=iden_f[:], in0=iden_f[:],
                                          scalar1=0.0, scalar2=None,
                                          op0=ALU.is_equal),
         reads=["idenf"], writes=["idenf"])
    p.op("DVE", lambda e: e.tensor_copy(out=iden_bf[:], in_=iden_f[:]),
         reads=["idenf"], writes=["idenb"])
    p.op("POOL", lambda e: e.memset(ones1[:], 1.0), writes=["ones1"])
    p.op("POOL", lambda e: e.memset(onescol[:], 1.0), writes=["onescol"])
    p.op("POOL", lambda e: e.memset(neghalf[:], -0.5), writes=["neghalf"])
    p.op("POOL", lambda e: e.memset(zpad[:], 0.0), writes=["zpad"])
    p.op("POOL", lambda e: e.memset(misc[:, 60:64], 0.0), writes=["msczero"])
    p.op("POOL", lambda e: e.memset(misc[:, 47:48], MAGIC), writes=["magc"])

    for b in range(NB):
        for e8 in range(ECH):
            p.op("POOL", lambda e, b=b, e8=e8: e.dma_start(
                out=s_xqT[b, e8, :, 0:8], in_=zpad[:]),
                reads=["zpad"], writes=[f"sxq_zh{b}_{e8}"], dma=True)
            p.op("POOL", lambda e, b=b, e8=e8: e.dma_start(
                out=s_xqT[b, e8, :, T + 8:T + 16], in_=zpad[:]),
                reads=["zpad"], writes=[f"sxq_zt{b}_{e8}"], dma=True)
    for b in range(NB):
        p.op("POOL", lambda e, b=b: e.dma_start(
            out=s_m1[b, 0:1].rearrange("(o a) -> o a", o=1), in_=misc[0:1, 60:61]),
            reads=["msczero"], writes=[f"sm1z{b}"], dma=True)
        p.op("POOL", lambda e, b=b: e.dma_start(
            out=s_m1[b, T + 1:T + 2].rearrange("(o a) -> o a", o=1), in_=misc[0:1, 61:62]),
            reads=["msczero"], writes=[f"sm1z2{b}"], dma=True)

    def load_col_params(d_ap, n, dst, key, bankslice, ident_n):
        p.op("SP", lambda e: e.dma_start(
            out=A_brow[0:n, :], in_=d_ap.rearrange("(a b) -> a b", b=128)),
            writes=["brow"], dma=True)
        p.op("PE", lambda e: e.transpose(out=bankslice, in_=A_brow[0:n, :],
                                         identity=iden_f[0:n, 0:n]),
             reads=["brow", "idenf"], writes=["B5"])
        p.op("DVE", lambda e: e.tensor_copy(out=dst, in_=bankslice),
             reads=["B5"], writes=[key])

    for d_ap, dst, key in ((cw0_d, w0c, "w0c"), (cw1_d, w1c, "w1c"),
                           (cw2_d, w2c, "w2c"), (cb_d, cbc, "cbc")):
        load_col_params(d_ap, 32, dst[:, 0:32], key, B[5][0:128, 0:32], 32)
    for d_ap, dst, key in ((bq_d, bqc, "bqc"), (bk_d, bkc, "bkc")):
        load_col_params(d_ap, 8, dst[:, 0:8], key, B[5][0:128, 0:8], 8)
    for wd, sd, nm in ((wq_d, s_wq, "wq"), (wk_d, s_wk, "wk"), (wv_d, s_wv, "wv")):
        for i in range(ECH):
            sl = slice(i * 128, (i + 1) * 128)
            p.op("SP", lambda e, wd=wd, sl=sl, i=i: e.dma_start(
                out=A_x32[i % 2][:], in_=wd[sl, :]),
                writes=[f"x32_{i % 2}"], dma=True)
            p.op("DVE", lambda e, i=i: e.tensor_copy(
                out=A_xbf[i % 2][:], in_=A_x32[i % 2][:]),
                reads=[f"x32_{i % 2}"], writes=[f"xbf{i % 2}"])
            p.op("POOL", lambda e, sd=sd, sl=sl, i=i: e.dma_start(
                out=sd[sl, :], in_=A_xbf[i % 2][:]),
                reads=[f"xbf{i % 2}"], writes=[f"scr_{nm}_{i}"], dma=True)

    def attn(b):
        for it in range(NT):
            sl = slice(it * 128, (it + 1) * 128)
            p.op("SP", lambda e, sl=sl, it=it: e.dma_start(
                out=A_x32[it % 2][:], in_=x_d[b, sl, :]),
                writes=[f"x32_{it % 2}"], dma=True)
            p.op("DVE", lambda e, it=it: e.tensor_copy(
                out=A_xbf[it % 2][:], in_=A_x32[it % 2][:]),
                reads=[f"x32_{it % 2}"], writes=[f"xbf{it % 2}"])
            p.op("POOL", lambda e, sl=sl, it=it: e.dma_start(
                out=s_xbf[b, sl, :], in_=A_xbf[it % 2][:]),
                reads=[f"xbf{it % 2}"], writes=[f"sxbf{it}"], dma=True)
            drain(4)
        p.fence([f"xT{e}" for e in range(ECH)]
                + [f"WvT{e}" for e in range(ECH)] + ["BVbc", "bvrow"],
                [f"P{jt}" for jt in range(NT)])
        p.op("SP", lambda e: e.dma_start(out=A_brow2[:],
                                         in_=bv_d.rearrange("(o a) -> o a", o=1)),
             writes=["bvrow"], dma=True)
        for h in range(2):
            p.op("PE", lambda e, h=h: e.matmul(
                out=B[h][:], lhsT=ones1[:],
                rhs=A_brow2[:, h * 512:(h + 1) * 512], start=True, stop=True),
                reads=["bvrow", "ones1"], writes=[f"B{h}"])
            p.op("DVE", lambda e, h=h: e.tensor_copy(
                out=A_BVbc[:, h * 512:(h + 1) * 512], in_=B[h][:]),
                reads=[f"B{h}"], writes=["BVbc"])
        for e8 in range(ECH):
            p.op("SP", lambda e, e8=e8: e.dma_start(
                out=A_xT[e8][:], in_=s_xbf[b, :, e8 * 128:(e8 + 1) * 128],
                transpose=True),
                reads=[f"sxbf{it}" for it in range(NT)],
                writes=[f"xT{e8}"], dma=True, xbar=True)
        wlist = [(s_wv, A_WvT, "WvT", "wv")]
        if b == 0:
            wlist += [(s_wq, A_WqT, "WqT", "wq"), (s_wk, A_WkT, "WkT", "wk")]
        for sd, dstl, nm, srck in wlist:
            for e8 in range(ECH):
                p.op("SP", lambda e, sd=sd, dstl=dstl, e8=e8: e.dma_start(
                    out=dstl[e8][:], in_=sd[:, e8 * 128:(e8 + 1) * 128],
                    transpose=True),
                    reads=[f"scr_{srck}_{i}" for i in range(ECH)],
                    writes=[f"{nm}{e8}"], dma=True, xbar=True)

        for it in range(NT):
            g = it % 2
            bk0, bk1 = 2 * g, 2 * g + 1
            for h, bk in ((0, bk0), (1, bk1)):
                for e8 in range(ECH):
                    p.op("PE", lambda e, it=it, h=h, bk=bk, e8=e8: e.matmul(
                        out=B[bk][:],
                        lhsT=A_xT[e8][:, it * 128:(it + 1) * 128],
                        rhs=A_WvT[e8][:, h * 512:(h + 1) * 512],
                        start=(e8 == 0), stop=(e8 == ECH - 1)),
                        reads=[f"xT{e8}", f"WvT{e8}"], writes=[f"B{bk}"])
                p.op("DVE", lambda e, it=it, h=h, bk=bk: e.tensor_tensor(
                    out=A_v[it][:, h * 512:(h + 1) * 512], in0=B[bk][:],
                    in1=A_BVbc[:, h * 512:(h + 1) * 512], op=ALU.add),
                    reads=[f"B{bk}", "BVbc"], writes=[f"v{it}"])
                drain(4)

        icw = min(512, T)
        for dstl, wTl, bc, nm in ((A_qT, A_WqT, bqc, "q"), (A_kT, A_WkT, bkc, "k")):
            wkey = "WqT" if nm == "q" else "WkT"
            for fc in range(ECH):
                for ic in range(T // icw):
                    bk = (fc * (T // icw) + ic) % 4
                    for e8 in range(ECH):
                        p.op("PE", lambda e, fc=fc, ic=ic, bk=bk, e8=e8,
                             wTl=wTl: e.matmul(
                            out=B[bk][:, 0:icw],
                            lhsT=wTl[e8][:, fc * 128:(fc + 1) * 128],
                            rhs=A_xT[e8][:, ic * icw:(ic + 1) * icw],
                            start=(e8 == 0), stop=(e8 == ECH - 1)),
                            reads=[f"xT{e8}", f"{wkey}{e8}"], writes=[f"B{bk}"])
                    p.op("ACT", lambda e, fc=fc, ic=ic, bk=bk, dstl=dstl,
                         bc=bc: e.activation(
                        out=dstl[fc][:, ic * icw:(ic + 1) * icw],
                        in_=B[bk][:, 0:icw],
                        func=AF.Identity, bias=bc[:, fc:fc + 1], scale=1.0),
                        reads=[f"B{bk}", "b" + nm + "c"],
                        writes=[f"{nm}T{fc}"])
                    drain(4)

        p.fence([f"P{jt}" for jt in range(NT)],
                [f"xT{e}" for e in range(ECH)] + [f"WvT{e}" for e in range(ECH)]
                + ["BVbc", "BVbc_use", "brow", "bvrow"])
        gchunk = 0
        for jt in range(NT):
            c0 = jt * 128
            bounds = [c0]
            nxt = (c0 // 512 + 1) * 512
            while nxt < T:
                bounds.append(nxt)
                nxt += 512
            bounds.append(T)
            chunks = [(bounds[i], bounds[i + 1] - bounds[i])
                      for i in range(len(bounds) - 1)]
            nch = len(chunks)
            banks = []
            for ci in range(nch):
                banks.append(gchunk % 6)
                gchunk += 1
            for e8 in range(ECH):
                for ci, (cs, w) in enumerate(chunks):
                    bk = banks[ci]
                    p.op("PE", lambda e, e8=e8, jt=jt, cs=cs, w=w, bk=bk: e.matmul(
                        out=B[bk][:, 0:w],
                        lhsT=A_kT[e8][:, jt * 128:(jt + 1) * 128],
                        rhs=A_qT[e8][:, cs:cs + w],
                        start=(e8 == 0), stop=(e8 == ECH - 1)),
                        reads=[f"kT{e8}", f"qT{e8}"], writes=[f"B{bk}"])
            bk0 = banks[0]
            p.op("DVE", lambda e, bk0=bk0: e.tensor_tensor(
                out=B[bk0][:, 0:128], in0=B[bk0][:, 0:128], in1=maskneg[:],
                op=ALU.add),
                reads=[f"B{bk0}", "maskneg"], writes=[f"B{bk0}"])
            for ci, (cs, w) in enumerate(chunks):
                bk = banks[ci]
                p.op("DVE", lambda e, bk=bk, w=w, ci=ci: e.tensor_reduce(
                    out=misc[:, CMAX0 + ci:CMAX0 + ci + 1], in_=B[bk][:, 0:w],
                    axis=AX, op=ALU.max),
                    reads=[f"B{bk}"], writes=[f"cmax{ci}"])
            p.op("DVE", lambda e, nch=nch: e.tensor_reduce(
                out=mc(MRAW), in_=misc[:, CMAX0:CMAX0 + nch], axis=AX,
                op=ALU.max),
                reads=[f"cmax{ci}" for ci in range(nch)], writes=["Mraw"])
            p.op("DVE", lambda e: e.tensor_scalar(
                out=mc(BIASV), in0=mc(MRAW), scalar1=-1.0 / 32.0, scalar2=None,
                op0=ALU.mult),
                reads=["Mraw"], writes=["biasv"])
            for ci, (cs, w) in enumerate(chunks):
                bk = banks[ci]
                p.op("ACT", lambda e, bk=bk, w=w, jt=jt, cs=cs, c0=c0,
                     ci=ci: e.activation(
                    out=A_P[jt][:, cs - c0:cs - c0 + w], in_=B[bk][:, 0:w],
                    func=AF.Exp, bias=mc(BIASV), scale=1.0 / 32.0,
                    accum_out=misc[:, CSUM0 + ci:CSUM0 + ci + 1]),
                    reads=[f"B{bk}", "biasv"],
                    writes=[f"P{jt}", f"csum{ci}", f"B{bk}_free"])
            p.op("DVE", lambda e, nch=nch: e.tensor_reduce(
                out=mc(COLSUM), in_=misc[:, CSUM0:CSUM0 + nch], axis=AX,
                op=ALU.add),
                reads=[f"csum{ci}" for ci in range(nch)], writes=["colsum"])
            p.op("DVE", lambda e: e.reciprocal(out=mc(RCOL), in_=mc(COLSUM)),
                 reads=["colsum"], writes=["rcol"])
            p.op("DVE", lambda e, jt=jt: e.tensor_scalar(
                out=A_v[jt][:], in0=A_v[jt][:], scalar1=mc(RCOL), scalar2=None,
                op0=ALU.mult),
                reads=["rcol", f"v{jt}"], writes=[f"v{jt}"])
            drain(8)

        p.fence(["xres", "xln0", "xln1", "xqstg", "xqbf", "pvx0", "pvx1"],
                [f"qT{e}" for e in range(ECH)])
        for it in range(NT):
            sl = slice(it * 128, (it + 1) * 128)
            g = it % 2
            p.op("SP", lambda e, sl=sl, g=g: e.dma_start(
                out=PV_xsrc[g][:], in_=x_d[b, sl, :]),
                writes=[f"pvx{g}"], dma=True)
            for h in range(2):
                bk = 2 * g + h
                for jt in range(it + 1):
                    p.op("PE", lambda e, it=it, jt=jt, h=h, bk=bk: e.matmul(
                        out=B[bk][:],
                        lhsT=A_P[jt][:, it * 128 - jt * 128:
                                      it * 128 - jt * 128 + 128],
                        rhs=A_v[jt][:, h * 512:(h + 1) * 512],
                        start=(jt == 0), stop=(jt == it)),
                        reads=[f"P{jt}", f"v{jt}"], writes=[f"B{bk}"])
                p.op("DVE", lambda e, h=h, bk=bk, g=g: e.tensor_tensor(
                    out=PV_xres[:, h * 512:(h + 1) * 512], in0=B[bk][:],
                    in1=PV_xsrc[g][:, h * 512:(h + 1) * 512], op=ALU.add),
                    reads=[f"B{bk}", f"pvx{g}"], writes=["xres"])
            p.op("DVE", lambda e: e.tensor_reduce(
                out=mc(MS), in_=PV_xres[:], axis=AX, op=ALU.add),
                reads=["xres"], writes=["ms"])
            p.op("DVE", lambda e: e.tensor_scalar(
                out=mc(NEGM), in0=mc(MS), scalar1=-1.0 / E, scalar2=None,
                op0=ALU.mult), reads=["ms"], writes=["negm"])
            for h, vs in ((0, VS1), (1, VS2)):
                bk = 4 + h
                p.op("ACT", lambda e, h=h, vs=vs, bk=bk: e.activation(
                    out=B[bk][:], in_=PV_xres[:, h * 512:(h + 1) * 512],
                    func=AF.Square, bias=mc(NEGM), scale=1.0,
                    accum_out=misc[:, vs:vs + 1]),
                    reads=["xres", "negm"], writes=[f"B{bk}", f"vs{h}"])
            p.op("DVE", lambda e: e.tensor_tensor(
                out=mc(VT), in0=mc(VS1), in1=mc(VS2), op=ALU.add),
                reads=["vs0", "vs1"], writes=["vt"])
            p.op("DVE", lambda e: e.tensor_scalar(
                out=mc(VT), in0=mc(VT), scalar1=1.0 / E, scalar2=LNEPS,
                op0=ALU.mult, op1=ALU.add), reads=["vt"], writes=["vt"])
            p.op("POOL", lambda e: e.tensor_tensor(
                out=mc(RSTD), in0=mc(VT), in1=neghalf[:], op=ALU.pow),
                reads=["vt", "neghalf"], writes=["rstd"])
            p.op("DVE", lambda e: e.tensor_scalar(
                out=mc(NEGMR), in0=mc(NEGM), scalar1=mc(RSTD), scalar2=None,
                op0=ALU.mult), reads=["negm", "rstd"], writes=["negmr"])
            p.op("ACT", lambda e, g=g: e.activation(
                out=PV_xln[g][:], in_=PV_xres[:], func=AF.Identity,
                scale=mc(RSTD), bias=mc(NEGMR)),
                reads=["xres", "rstd", "negmr"], writes=[f"xln{g}"])
            p.op("POOL", lambda e, sl=sl, g=g: e.dma_start(
                out=s_xln[b, sl, :], in_=PV_xln[g][:]),
                reads=[f"xln{g}"], writes=[f"sxln{b}_{it}"], dma=True)
            p.op("DVE", lambda e, g=g: e.tensor_reduce(
                out=mc(M1), in_=PV_xln[g][:], axis=AX, op=ALU.max,
                apply_absolute_value=True),
                reads=[f"xln{g}"], writes=["m1"])
            p.op("DVE", lambda e: e.tensor_scalar(
                out=mc(M1C), in0=mc(M1), scalar1=QCLIP, scalar2=None,
                op0=ALU.max), reads=["m1"], writes=["m1c"])
            p.op("DVE", lambda e, it=it: e.tensor_copy(
                out=m1ct[:, it:it + 1], in_=mc(M1C)),
                reads=["m1c"], writes=["m1ct"])
            p.op("DVE", lambda e: e.reciprocal(out=mc(R1), in_=mc(M1C)),
                 reads=["m1c"], writes=["r1"])
            p.op("DVE", lambda e: e.tensor_scalar(
                out=mc(S1V), in0=mc(R1), scalar1=127.0, scalar2=None,
                op0=ALU.mult), reads=["r1"], writes=["s1v"])
            p.op("DVE", lambda e, g=g: e.tensor_scalar(
                out=PV_xres[:], in0=PV_xln[g][:], scalar1=mc(S1V),
                scalar2=MAGIC, op0=ALU.mult, op1=ALU.add),
                reads=[f"xln{g}", "s1v"], writes=["xres"])
            p.op("DVE", lambda e: e.tensor_scalar(
                out=PV_xqbf[:], in0=PV_xres[:], scalar1=MAGIC, scalar2=None,
                op0=ALU.subtract), reads=["xres"], writes=["xqbf"])
            for e8 in range(ECH):
                xb = X6 if e8 % 2 == 0 else X7
                xk = "X6" if e8 % 2 == 0 else "X7"
                p.op("PE", lambda e, e8=e8, xb=xb: e.transpose(
                    out=xb[:, 0:128], in_=PV_xqbf[:, e8 * 128:(e8 + 1) * 128],
                    identity=iden_bf[:]),
                    reads=["xqbf", "idenb"], writes=[xk])
                p.op("ACT", lambda e, e8=e8, xb=xb: e.copy(
                    out=PV_xqstg[:, e8 * 128:(e8 + 1) * 128], in_=xb[:, 0:128]),
                    reads=[xk], writes=["xqstg"])
            p.op("POOL", lambda e, it=it: e.dma_start(
                out=s_xqT[b, :, :, 8 + it * 128:8 + (it + 1) * 128]
                .rearrange("e p t -> p e t"),
                in_=PV_xqstg[:].rearrange("p (e t) -> p e t", e=ECH)),
                reads=["xqstg"], writes=[f"sxq{b}_{it}"], dma=True)
            drain(12)
        p.op("PE", lambda e: e.transpose(
            out=B[5][0:NT, 0:128], in_=m1ct[:], identity=iden_f[:]),
            reads=["m1ct", "idenf"], writes=["B5"])
        p.op("DVE", lambda e: e.tensor_copy(out=mrow[:], in_=B[5][0:NT, 0:128]),
             reads=["B5"], writes=["mrow"])
        p.op("POOL", lambda e: e.dma_start(
            out=s_m1[b, 1:T + 1].rearrange("(i p) -> i p", p=128),
            in_=mrow[:]),
            reads=["mrow"], writes=[f"sm1_{b}"], dma=True)

    PMS, RWB1, RWB2 = 46, 44, 45
    prep_q = []

    def _pload(g, wd, rows, cols):
        up = g % 4
        p.op("ACT", lambda e, up=up, wd=wd, rows=rows, cols=cols: e.dma_start(
            out=PR_in[up][:], in_=wd[rows, cols]),
            writes=[f"pin{up}"], dma=True)

    def _p1c(g):
        up = g % 4
        p.op("DVE", lambda e, up=up, g=g: e.tensor_reduce(
            out=PR_ap[:, g:g + 1], in_=PR_in[up][:], axis=AX, op=ALU.add,
            apply_absolute_value=True),
            reads=[f"pin{up}"], writes=[f"pap{g}"])

    def _pmean(m):
        base = 128 * m
        p.op("DVE", lambda e, base=base: e.tensor_reduce(
            out=misc[:, PMS:PMS + 1], in_=PR_ap[:, base:base + 128], axis=AX,
            op=ALU.add),
            reads=[f"pap{base + u}" for u in range(128)], writes=["pms"])
        p.op("PE", lambda e: e.matmul(
            out=B[5][0:1, 0:1], lhsT=misc[:, PMS:PMS + 1], rhs=onescol[:],
            start=True, stop=True),
            reads=["pms", "onescol"], writes=["B5"])
        p.op("DVE", lambda e, m=m: e.tensor_scalar(
            out=mw[:, 2 * m:2 * m + 1], in0=B[5][0:1, 0:1],
            scalar1=1.0 / (H * E), scalar2=QCLIP, op0=ALU.mult, op1=ALU.max),
            reads=["B5"], writes=[f"mw{m}"])
        p.op("DVE", lambda e, m=m: e.reciprocal(
            out=mw[:, 2 * m + 1:2 * m + 2], in_=mw[:, 2 * m:2 * m + 1]),
            reads=[f"mw{m}"], writes=[f"rw{m}"])
        p.op("PE", lambda e, m=m: e.matmul(
            out=B[5][:, 0:2], lhsT=ones1[:], rhs=mw[:, 2 * m:2 * m + 2],
            start=True, stop=True),
            reads=[f"mw{m}", f"rw{m}", "ones1"], writes=["B5"])
        p.op("DVE", lambda e, m=m: e.tensor_copy(
            out=mwb[:, m:m + 1], in_=B[5][:, 0:1]),
            reads=["B5"], writes=[f"mwb{m}"])
        p.op("DVE", lambda e, m=m: e.tensor_copy(
            out=misc[:, RWB1 + m:RWB1 + m + 1], in_=B[5][:, 1:2]),
            reads=["B5"], writes=[f"rwb{m + 1}"])

    def _p2c(g, m, sd, rows, cols, skey):
        rwbk = f"rwb{m + 1}"
        rwbc = RWB1 + m
        up = g % 4
        ob = (g // 1) % 2
        p.op("ACT", lambda e, up=up, rwbc=rwbc: e.activation(
            out=PR_in[up][:], in_=PR_in[up][:], func=AF.Identity,
            scale=misc[:, rwbc:rwbc + 1], bias=misc[:, 47:48]),
            reads=[f"pin{up}", rwbk, "magc"], writes=[f"pin{up}"])
        p.op("DVE", lambda e, up=up: e.tensor_scalar(
            out=PR_in[up][:], in0=PR_in[up][:], scalar1=MAGIC, scalar2=1.0,
            op0=ALU.subtract, op1=ALU.min),
            reads=[f"pin{up}"], writes=[f"pin{up}"])
        p.op("DVE", lambda e, up=up, ob=ob: e.tensor_scalar(
            out=PR_ob[ob][:], in0=PR_in[up][:], scalar1=-1.0, scalar2=None,
            op0=ALU.max),
            reads=[f"pin{up}"], writes=[f"pob{ob}"])
        p.op("ACT", lambda e, ob=ob, sd=sd, rows=rows, cols=cols: e.dma_start(
            out=sd[rows, cols], in_=PR_ob[ob][:]),
            reads=[f"pob{ob}"], writes=[skey], dma=True)

    def build_prep_queue():
        w1u = [(i, c) for i in range(32) for c in range(4)]
        w2u = [(i, c) for i in range(8) for c in range(16)]
        units = []
        for u, (i, c) in enumerate(w1u):
            rows = slice(i * 128, (i + 1) * 128)
            cols = slice(c * 256, (c + 1) * 256)
            units.append(((w1_d, rows, cols),
                          lambda g, u=u: _p1c(g)))
        units.append((None, lambda g: _pmean(0)))
        for u, (i, c) in enumerate(w2u):
            rows = slice(i * 128, (i + 1) * 128)
            cols = slice(c * 256, (c + 1) * 256)
            units.append(((w2_d, rows, cols),
                          lambda g, u=u: _p1c(g)))
        units.append((None, lambda g: _pmean(1)))
        for u, (i, c) in enumerate(w1u):
            rows = slice(i * 128, (i + 1) * 128)
            cols = slice(c * 256, (c + 1) * 256)
            units.append(((w1_d, rows, cols),
                          lambda g, i=i, c=c, rows=rows, cols=cols: _p2c(
                              g, 0, s_w1i, rows, cols, f"sw1i{i}_{c}")))
        for u, (i, c) in enumerate(w2u):
            rows = slice(i * 128, (i + 1) * 128)
            cols = slice(c * 256, (c + 1) * 256)
            units.append(((w2_d, rows, cols),
                          lambda g, i=i, c=c, rows=rows, cols=cols: _p2c(
                              g, 1, s_w2i, rows, cols, f"sw2i{i}_{c}")))
        loads = [u for u in units if u[0] is not None]
        gidx = 0
        pend = []
        for la, comp in units:
            if la is None:
                while pend:
                    g2, c2 = pend.pop(0)
                    prep_q.append(lambda g2=g2, c2=c2: c2(g2))
                prep_q.append(lambda comp=comp: comp(-1))
                continue
            g = gidx
            gidx += 1
            prep_q.append(lambda la=la, g=g: _pload(g, *la))
            pend.append((g, comp))
            if len(pend) >= 4:
                g2, c2 = pend.pop(0)
                prep_q.append(lambda g2=g2, c2=c2: c2(g2))
        for g2, c2 in pend:
            prep_q.append(lambda g2=g2, c2=c2: c2(g2))

    def drain(n=1):
        for _ in range(n):
            if prep_q:
                prep_q.pop(0)()

    NGC = NB * NCHUNK

    def ffn_entry():
        newk = ([f"W1qT{e}" for e in range(ECH)]
                + [f"W2qT{c}" for c in range(HCH)]
                + [f"Fg{pr}_{s}" for pr in range(2) for s in range(HCH)]
                + [f"F_xq{pr}_{e}" for pr in range(2) for e in range(ECH)]
                + [f"fxlnt{pr}_{t}" for pr in range(2) for t in range(2)]
                + [f"fw{pr}" for pr in range(2)] + [f"fbc{pr}" for pr in range(2)]
                + ["sbc", "fs2seg", "Mx", "Mn", "flnt"]
                + [f"hs{i}" for i in range(2)] + [f"u{i}" for i in range(2)]
                + [f"fev{i}" for i in range(2)]
                + [f"D{k}_{pr}" for k in range(3) for pr in range(2)]
                + ["fcm0", "fcm1", "fcn0", "fcn1", "fm2", "fr2", "fs2v", "frs2"])
        oldk = ([f"P{jt}" for jt in range(NT)] + [f"v{t}" for t in range(NT)]
                + [f"qT{e}" for e in range(ECH)] + [f"kT{e}" for e in range(ECH)]
                + [f"xT{e}" for e in range(ECH)] + [f"WqT{e}" for e in range(ECH)]
                + [f"WkT{e}" for e in range(ECH)] + [f"WvT{e}" for e in range(ECH)]
                + ["xres", "xln0", "xln1", "xqstg", "xqbf", "pvx0", "pvx1",
                   "BVbc", "bvrow", "brow", "x32_0", "x32_1", "xbf0", "xbf1"]
                + [f"pap{c}" for c in range(256)]
                + [f"pin{i}" for i in range(4)]
                + ["pob0", "pob1", "pms", "m1ct", "mrow"]
                + [f"cmax{i}" for i in range(4)] + [f"csum{i}" for i in range(4)]
                + ["colsum", "msczero", "Mraw"])
        p.fence(newk, oldk)
        for e8 in range(ECH):
            p.op("SP", lambda e, e8=e8: e.dma_start(
                out=F_W1qT[e8][:], in_=s_w1i[:, e8 * 128:(e8 + 1) * 128],
                transpose=True),
                reads=[f"sw1i{i}_{c}" for i in range(32) for c in range(4)],
                writes=[f"W1qT{e8}"], dma=True, xbar=True)
        for c in range(HCH):
            p.op("SP", lambda e, c=c: e.dma_start(
                out=F_W2qT[c][:], in_=s_w2i[:, c * 128:(c + 1) * 128],
                transpose=True),
                reads=[f"sw2i{i}_{cc}" for i in range(8) for cc in range(16)],
                writes=[f"W2qT{c}"], dma=True, xbar=True)

    def ffn_loads(gc):
        b, ct = divmod(gc, NCHUNK)
        t0 = ct * 256
        par = gc % 2
        for e8 in range(ECH):
            p.op("SP", lambda e, e8=e8, b=b, t0=t0, par=par: e.dma_start(
                out=F_xq[par][e8][:],
                in_=s_xqT[b, e8, :, 8 + t0 - 1:8 + t0 + 257]),
                reads=[f"sxq{b}_{it}" for it in range(max(0, t0 // 128 - 1),
                                                      min(NT, t0 // 128 + 3))]
                + [f"sxq_zh{b}_{e8}", f"sxq_zt{b}_{e8}"],
                writes=[f"F_xq{par}_{e8}"], dma=True)
        p.op("SP", lambda e, b=b, t0=t0, par=par: e.dma_start(
            out=F_fw[par][:],
            in_=s_m1[b, t0:t0 + 258].rearrange("(o a) -> o a", o=1)),
            reads=[f"sm1_{b}", f"sm1z{b}", f"sm1z2{b}"],
            writes=[f"fw{par}"], dma=True)

    def ffn_loads_xlnt(gc):
        b, ct = divmod(gc, NCHUNK)
        t0 = ct * 256
        par = gc % 2
        for tb in range(2):
            sl = slice(t0 + tb * 128, t0 + (tb + 1) * 128)
            p.op("SP", lambda e, sl=sl, tb=tb, b=b, par=par: e.dma_start(
                out=F_xlnt[par][tb][:], in_=s_xln[b, sl, :]),
                reads=[f"sxln{b}_{sl.start // 128}"],
                writes=[f"fxlnt{par}_{tb}"], dma=True)

    def ffn_fbc(gc):
        par = gc % 2
        p.op("DVE", lambda e, par=par: e.tensor_scalar(
            out=F_fw[par][:], in0=F_fw[par][:], scalar1=mw[0:1, 0:1],
            scalar2=1.0 / 127.0, op0=ALU.mult, op1=ALU.mult),
            reads=[f"fw{par}", "mw0"], writes=[f"fw{par}"])
        p.op("PE", lambda e, par=par: e.matmul(
            out=B[3][:, 0:258], lhsT=ones1[:], rhs=F_fw[par][:],
            start=True, stop=True),
            reads=[f"fw{par}", "ones1"], writes=["B3s"])
        p.op("DVE", lambda e, par=par: e.tensor_copy(
            out=F_fbc[par][:], in_=B[3][:, 0:258]),
            reads=["B3s"], writes=[f"fbc{par}"])

    def ffn_w1(gc, s):
        par = gc % 2
        bk = s % 2
        for e8 in range(ECH):
            p.op("PE", lambda e, s=s, e8=e8, bk=bk, par=par: e.matmul(
                out=B[bk][:, 0:258],
                lhsT=F_W1qT[e8][:, s * 128:(s + 1) * 128],
                rhs=F_xq[par][e8][:],
                start=(e8 == 0), stop=(e8 == ECH - 1)),
                reads=[f"W1qT{e8}", f"F_xq{par}_{e8}"], writes=[f"B{bk}"])
        p.op("DVE", lambda e, s=s, bk=bk, par=par: e.tensor_tensor(
            out=F_hs[s % 2][:], in0=B[bk][:, 0:258], in1=F_fbc[par][:],
            op=ALU.mult),
            reads=[f"B{bk}", f"fbc{par}"], writes=[f"hs{s % 2}"])

    def ffn_dbuild(s):
        for k, wc, wk in ((0, w0c, "w0c"), (1, w1c, "w1c"), (2, w2c, "w2c")):
            p.op("ACT", lambda e, k=k, s=s, wc=wc: e.activation(
                out=F_D[k][s % 2][:], in_=iden_f[:], func=AF.Identity,
                scale=wc[:, s:s + 1]),
                reads=["idenf", wk], writes=[f"D{k}_{s % 2}"])

    def ffn_conv(gc, s):
        par = gc % 2
        for k in range(3):
            p.op("PE", lambda e, k=k, s=s: e.matmul(
                out=B[2][:, 0:256], lhsT=F_D[k][s % 2][:],
                rhs=F_hs[s % 2][:, k:k + 256],
                start=(k == 0), stop=(k == 2)),
                reads=[f"D{k}_{s % 2}", f"hs{s % 2}"], writes=["B2c"])
        p.op("ACT", lambda e, s=s, par=par: e.activation(
            out=F_g[par][s][:], in_=B[2][:, 0:256], func=AF.Silu,
            bias=cbc[:, s:s + 1], scale=1.0),
            reads=["B2c", "cbc"], writes=[f"Fg{par}_{s}"])
        if s == 0:
            p.op("DVE", lambda e, par=par: e.tensor_copy(
                out=F_Mx[:], in_=F_g[par][0][:]),
                reads=[f"Fg{par}_0"], writes=["Mx"])
            p.op("DVE", lambda e, par=par: e.tensor_copy(
                out=F_Mn[:], in_=F_g[par][0][:]),
                reads=[f"Fg{par}_0"], writes=["Mn"])
        else:
            p.op("DVE", lambda e, s=s, par=par: e.tensor_tensor(
                out=F_Mx[:], in0=F_Mx[:], in1=F_g[par][s][:], op=ALU.max),
                reads=["Mx", f"Fg{par}_{s}"], writes=["Mx"])
            p.op("DVE", lambda e, s=s, par=par: e.tensor_tensor(
                out=F_Mn[:], in0=F_Mn[:], in1=F_g[par][s][:], op=ALU.min),
                reads=["Mn", f"Fg{par}_{s}"], writes=["Mn"])

    def ffn_scales_a(pgc):
        for i, key in ((0, "Mx"), (1, "Mn")):
            for tb in range(2):
                src = F_Mx if i == 0 else F_Mn
                p.op("PE", lambda e, i=i, tb=tb, src=src, key=key: e.transpose(
                    out=X6[:, (i * 2 + tb) * 128:(i * 2 + tb + 1) * 128],
                    in_=src[:, tb * 128:(tb + 1) * 128], identity=iden_bf[:]),
                    reads=[key, "idenb"], writes=["X6"])
        for tb in range(2):
            p.op("DVE", lambda e, tb=tb: e.tensor_reduce(
                out=misc[:, 16 + tb:17 + tb],
                in_=X6[:, tb * 128:(tb + 1) * 128], axis=AX, op=ALU.max),
                reads=["X6"], writes=[f"fcm{tb}"])
            p.op("DVE", lambda e, tb=tb: e.tensor_reduce(
                out=misc[:, 18 + tb:19 + tb],
                in_=X6[:, (2 + tb) * 128:(3 + tb) * 128], axis=AX, op=ALU.min,
                negate=True),
                reads=["X6"], writes=[f"fcn{tb}"])
        p.op("DVE", lambda e: e.tensor_tensor(
            out=misc[:, 20:22], in0=misc[:, 16:18], in1=misc[:, 18:20],
            op=ALU.max),
            reads=["fcm0", "fcm1", "fcn0", "fcn1"], writes=["fm2"])
        p.op("DVE", lambda e: e.tensor_scalar(
            out=misc[:, 20:22], in0=misc[:, 20:22], scalar1=QCLIP, scalar2=None,
            op0=ALU.max), reads=["fm2"], writes=["fm2"])
        p.op("DVE", lambda e: e.tensor_scalar(
            out=misc[:, 24:26], in0=misc[:, 20:22], scalar1=mwb[:, 1:2],
            scalar2=1.0 / 127.0, op0=ALU.mult, op1=ALU.mult),
            reads=["fm2", "mwb1"], writes=["frs2"])
        p.op("DVE", lambda e: e.reciprocal(out=misc[:, 22:24], in_=misc[:, 20:22]),
             reads=["fm2"], writes=["fr2"])
        p.op("DVE", lambda e: e.tensor_scalar(
            out=misc[:, 30:31], in0=misc[:, 22:23], scalar1=127.0, scalar2=None,
            op0=ALU.mult), reads=["fr2"], writes=["fs2v"])
        p.op("DVE", lambda e: e.tensor_scalar(
            out=misc[:, 62:63], in0=misc[:, 23:24], scalar1=127.0, scalar2=None,
            op0=ALU.mult), reads=["fr2"], writes=["fs2v"])

    def ffn_scales_b(pgc):
        import os as _os2
        _ksb = _os2.environ.get("KSB", "tcc")
        p.op("PE", lambda e: e.matmul(
            out=B[3][0:33, 384:512], lhsT=misc[:, 30:63], rhs=iden_f[:],
            start=True, stop=True, is_transpose=True),
            reads=["fs2v", "idenf"], writes=["B3s"])
        for half, row in ((0, 0), (1, 32))[:_ksb.count("c")]:
            p.op("DVE", lambda e, half=half, row=row: e.tensor_copy(
                out=F_s2seg[:, half * 128:(half + 1) * 128],
                in_=B[3][row:row + 1, 384:512]),
                reads=["B3s"], writes=["fs2seg"])

    def ffn_scales_c(pgc):
        p.op("PE", lambda e: e.matmul(
            out=B[3][:, 0:256], lhsT=ones1[:], rhs=F_s2seg[:],
            start=True, stop=True),
            reads=["fs2seg", "ones1"], writes=["B3s"])
        p.op("DVE", lambda e: e.tensor_copy(out=F_sbc[:], in_=B[3][:, 0:256]),
             reads=["B3s"], writes=["sbc"])

    def ffn_round(pgc, cs):
        par = pgc % 2
        up = cs % 2
        p.op("DVE", lambda e, cs=cs, par=par, up=up: e.tensor_tensor(
            out=F_u[up][:], in0=F_g[par][cs][:], in1=F_sbc[:], op=ALU.mult),
            reads=[f"Fg{par}_{cs}", "sbc"], writes=[f"u{up}"])
        p.op("DVE", lambda e, cs=cs, par=par, up=up: e.tensor_scalar(
            out=F_g[par][cs][:], in0=F_u[up][:], scalar1=MAGIC, scalar2=MAGIC,
            op0=ALU.add, op1=ALU.subtract),
            reads=[f"u{up}"], writes=[f"Fg{par}_{cs}"])

    def ffn_w2mm(pgc, grp, cs):
        par = pgc % 2
        tb, h = grp // 2, grp % 2
        bk = 4 + (grp % 2)
        p.op("PE", lambda e, par=par, tb=tb, h=h, cs=cs, bk=bk: e.matmul(
            out=B[bk][:],
            lhsT=F_g[par][cs][:, tb * 128:(tb + 1) * 128],
            rhs=F_W2qT[cs][:, h * 512:(h + 1) * 512],
            start=(cs == 0), stop=(cs == HCH - 1)),
            reads=[f"Fg{par}_{cs}", f"W2qT{cs}"], writes=[f"B{bk}"])

    def ffn_w2evac(pgc, grp):
        par = pgc % 2
        tb, h = grp // 2, grp % 2
        bk = 4 + (grp % 2)
        ev = grp % 2
        p.op("ACT", lambda e, bk=bk, ev=ev, tb=tb: e.activation(
            out=F_ev[ev][:], in_=B[bk][:], func=AF.Identity,
            scale=misc[:, 24 + tb:25 + tb]),
            reads=[f"B{bk}", "frs2"], writes=[f"fev{ev}"])
        p.op("DVE", lambda e, par=par, tb=tb, h=h, ev=ev: e.tensor_tensor(
            out=F_xlnt[par][tb][:, h * 512:(h + 1) * 512],
            in0=F_ev[ev][:], in1=F_xlnt[par][tb][:, h * 512:(h + 1) * 512],
            op=ALU.add),
            reads=[f"fev{ev}", f"fxlnt{par}_{tb}"], writes=[f"fxlnt{par}_{tb}"])

    def ffn_ln2(pgc, tb):
        b, ct = divmod(pgc, NCHUNK)
        par = pgc % 2
        t0 = ct * 256
        sl = slice(t0 + tb * 128, t0 + (tb + 1) * 128)
        xt = F_xlnt[par][tb]
        xk = f"fxlnt{par}_{tb}"
        p.op("DVE", lambda e, xt=xt: e.tensor_reduce(
            out=mc(MS), in_=xt[:], axis=AX, op=ALU.add),
            reads=[xk], writes=["ms"])
        p.op("DVE", lambda e: e.tensor_scalar(
            out=mc(NEGM), in0=mc(MS), scalar1=-1.0 / E, scalar2=None,
            op0=ALU.mult), reads=["ms"], writes=["negm"])
        for h, vs in ((0, VS1), (1, VS2)):
            p.op("ACT", lambda e, xt=xt, h=h, vs=vs: e.activation(
                out=F_lnt[:], in_=xt[:, h * 512:(h + 1) * 512],
                func=AF.Square, bias=mc(NEGM), scale=1.0,
                accum_out=misc[:, vs:vs + 1]),
                reads=[xk, "negm"], writes=["flnt", f"vs{h}"])
        p.op("DVE", lambda e: e.tensor_tensor(
            out=mc(VT), in0=mc(VS1), in1=mc(VS2), op=ALU.add),
            reads=["vs0", "vs1"], writes=["vt"])
        p.op("DVE", lambda e: e.tensor_scalar(
            out=mc(VT), in0=mc(VT), scalar1=1.0 / E, scalar2=LNEPS,
            op0=ALU.mult, op1=ALU.add), reads=["vt"], writes=["vt"])
        p.op("POOL", lambda e: e.tensor_tensor(
            out=mc(RSTD), in0=mc(VT), in1=neghalf[:], op=ALU.pow),
            reads=["vt", "neghalf"], writes=["rstd"])
        p.op("DVE", lambda e: e.tensor_scalar(
            out=mc(NEGMR), in0=mc(NEGM), scalar1=mc(RSTD), scalar2=None,
            op0=ALU.mult), reads=["negm", "rstd"], writes=["negmr"])
        p.op("ACT", lambda e, xt=xt: e.activation(
            out=xt[:], in_=xt[:], func=AF.Identity,
            scale=mc(RSTD), bias=mc(NEGMR)),
            reads=[xk, "rstd", "negmr"], writes=[xk])
        st = p.op("POOL", lambda e, sl=sl, xt=xt, b=b: e.dma_start(
            out=out_d[b, sl, :], in_=xt[:]),
            reads=[xk], writes=[f"outst{b}_{sl.start}"], dma=True)
        out_stores.append(st)

    def ffn_chunk(gc):
        if gc + 1 < NGC:
            ffn_loads(gc + 1)
        ffn_fbc(gc)
        ffn_dbuild(0)
        have_prev = gc > 0
        pgc = gc - 1
        import os as _os
        _kd = _os.environ.get("KDEF", "full")
        _lv = {"none": 0, "scales": 1, "round": 2, "w2": 3, "evac": 4,
               "full": 9}[_kd]
        W2S = {0: 12, 1: 14, 2: 20, 3: 22}
        for s in range(HCH):
            ffn_w1(gc, s)
            if s >= 1:
                ffn_conv(gc, s - 1)
            if s + 1 < HCH:
                ffn_dbuild(s + 1)
            if have_prev:
                if _lv >= 1:
                    _ksc = _os.environ.get("KSC", "abc")
                    if s == 0 and "a" in _ksc:
                        ffn_scales_a(pgc)
                    elif s == 2 and "b" in _ksc:
                        ffn_scales_b(pgc)
                    elif s == 3 and "c" in _ksc:
                        ffn_scales_c(pgc)
                if _lv >= 2 and 4 <= s <= 11:
                    for cs in range(4 * (s - 4), 4 * (s - 3)):
                        ffn_round(pgc, cs)
                if _lv >= 4:
                    if s == 20:
                        ffn_w2evac(pgc, 0)
                    elif s == 22:
                        ffn_w2evac(pgc, 1)
                    elif s == 28:
                        ffn_w2evac(pgc, 2)
                    elif s == 30:
                        ffn_w2evac(pgc, 3)
                if _lv >= 5:
                    if s == 23:
                        ffn_ln2(pgc, 0)
                    elif s == 31:
                        ffn_ln2(pgc, 1)
                if s == 31 and gc + 1 < NGC:
                    ffn_loads_xlnt(gc + 1)
                if _lv >= 3:
                    for grp, s0 in W2S.items():
                        if s0 <= s < s0 + 8:
                            for cs in range(4 * (s - s0), 4 * (s - s0 + 1)):
                                ffn_w2mm(pgc, grp, cs)
        if not have_prev and gc + 1 < NGC:
            ffn_loads_xlnt(gc + 1)
        ffn_conv(gc, HCH - 1)

    def ffn_tail():
        pgc = NGC - 1
        ffn_scales_a(pgc)
        ffn_scales_b(pgc)
        ffn_scales_c(pgc)
        for cs in range(HCH):
            ffn_round(pgc, cs)
        for grp in range(4):
            for cs in range(HCH):
                ffn_w2mm(pgc, grp, cs)
            ffn_w2evac(pgc, grp)
            if grp == 1:
                ffn_ln2(pgc, 0)
        ffn_ln2(pgc, 1)

    out_stores = []
    build_prep_queue()
    for b in range(NB):
        attn(b)
    while prep_q:
        drain(1)
    ffn_entry()
    ffn_loads(0)
    ffn_loads_xlnt(0)
    for gc in range(NGC):
        ffn_chunk(gc)
    ffn_tail()
    p.op("POOL", lambda e: e.nop(), deps=out_stores)

    import contextlib as _ctx
    with _ctx.ExitStack() as st:
        block = st.enter_context(nc.Block())
        sems = {e: st.enter_context(nc.semaphore("sem" + e)) for e in Prog.ENGINES}
        p.build(nc, block, sems)
    pstack.close()
    return nc


def kernel(L0_attn_X, L0_attn_Wq, L0_attn_bq, L0_attn_Wk, L0_attn_bk,
           L0_attn_Wv, L0_attn_bv, L0_ffn_w1, L0_ffn_conv_w, L0_ffn_conv_b,
           L0_ffn_w2):
    X = np.ascontiguousarray(np.asarray(L0_attn_X, dtype=np.float32))
    cw = np.asarray(L0_ffn_conv_w, dtype=np.float32).reshape(4096, 3)
    common = {
        "wq": np.ascontiguousarray(np.asarray(L0_attn_Wq, np.float32)),
        "wk": np.ascontiguousarray(np.asarray(L0_attn_Wk, np.float32)),
        "wv": np.ascontiguousarray(np.asarray(L0_attn_Wv, np.float32)),
        "bq": np.ascontiguousarray(np.asarray(L0_attn_bq, np.float32)),
        "bk": np.ascontiguousarray(np.asarray(L0_attn_bk, np.float32)),
        "bv": np.ascontiguousarray(np.asarray(L0_attn_bv, np.float32)),
        "w1": np.ascontiguousarray(np.asarray(L0_ffn_w1, np.float32)),
        "w2": np.ascontiguousarray(np.asarray(L0_ffn_w2, np.float32)),
        "cw0": np.ascontiguousarray(cw[:, 0]),
        "cw1": np.ascontiguousarray(cw[:, 1]),
        "cw2": np.ascontiguousarray(cw[:, 2]),
        "cb": np.ascontiguousarray(np.asarray(L0_ffn_conv_b, np.float32)),
    }
    n_cores = 8
    nb = X.shape[0] // n_cores
    nc = build_program(T=X.shape[1], NB=nb)
    in_maps = []
    for c in range(n_cores):
        m = dict(common)
        m["x"] = np.ascontiguousarray(X[c * nb:(c + 1) * nb])
        in_maps.append(m)
    res = run_bass_kernel_spmd(nc, in_maps, core_ids=list(range(n_cores)))
    out = np.concatenate([np.asarray(r["out"]) for r in res.results], axis=0)
    return out.astype(np.float32)


# version 10
# speedup vs baseline: 1.1364x; 1.1364x over previous
import numpy as np
import ml_dtypes
import concourse.bass as bass
import concourse.mybir as mybir
from concourse.bass_utils import run_bass_kernel_spmd

F32 = mybir.dt.float32
BF16 = mybir.dt.bfloat16
U8 = mybir.dt.uint8
AF = mybir.ActivationFunctionType
ALU = mybir.AluOpType
AX = mybir.AxisListType.X

MAGIC = 12582912.0
NEGBIG = -1e30
LNEPS = 1e-5
QCLIP = 1e-5
E = 1024
H = 4096
ECH = 8
HCH = 32
SBUF_CAP = 212_256


class Prog:

    ENGINES = ("SP", "PE", "ACT", "DVE", "POOL", "PD")

    def __init__(self):
        self.ops = []
        self.res = {}
        self.seed = {}
        self.last_dma = {}
        self.last_xbar = None
        import os
        self.serial = set(os.environ.get("KSYNC", "").split(",")) - {""}

    def _touch(self, k, depset):
        if k in self.seed:
            depset.update(self.seed.pop(k))
        r = self.res.get(k)
        if r is None:
            r = self.res[k] = {"w": None, "r": []}
        return r

    def op(self, engine, emit, reads=(), writes=(), deps=(), dma=False,
           xbar=False):
        idx = len(self.ops)
        depset = set(deps)
        for k in reads:
            r = self._touch(k, depset)
            if r["w"] is not None:
                depset.add(r["w"])
            r["r"].append(idx)
        for k in writes:
            r = self._touch(k, depset)
            if r["w"] is not None:
                depset.add(r["w"])
            depset.update(r["r"])
            r["w"] = idx
            r["r"] = []
        if dma:
            if engine in self.last_dma:
                depset.add(self.last_dma[engine])
            if self.last_xbar is not None:
                depset.add(self.last_xbar)
            if xbar:
                for q, d in self.last_dma.items():
                    depset.add(d)
                self.last_xbar = idx
            self.last_dma[engine] = idx
        depset.discard(idx)
        if engine == "PE":
            depset = {d for d in depset if self.ops[d][0] != "PE"}
        if idx > 0 and ("all" in self.serial or engine in self.serial):
            depset.add(idx - 1)
        self.ops.append((engine, emit, dma, depset))
        return idx

    def fence(self, new_keys, old_keys):
        dep = set()
        for k in old_keys:
            r = self.res.get(k)
            if r is not None:
                if r["w"] is not None:
                    dep.add(r["w"])
                dep.update(r["r"])
        for k in new_keys:
            self.seed[k] = set(self.seed.get(k, ())) | dep

    def build(self, nc, block, sems):
        needed = set()
        for _, _, _, deps in self.ops:
            needed.update(deps)
        incs = []
        cums = []
        cum = {e: 0 for e in self.ENGINES}
        streams = []
        for i, (eng, _, dma, _) in enumerate(self.ops):
            inc = 16 if dma else (1 if i in needed else 0)
            incs.append(inc)
            stream = "PD" if (dma and eng == "POOL") else eng
            streams.append(stream)
            cum[stream] += inc
            cums.append(cum[stream])
        per = {e: [] for e in self.ENGINES}
        for i, (eng, emit, dma, deps) in enumerate(self.ops):
            per[eng].append((i, emit, incs[i], deps))
        engobj = {"SP": nc.sync, "PE": nc.tensor, "ACT": nc.scalar,
                  "DVE": nc.vector, "POOL": nc.gpsimd}
        openg = streams

        def make_section(ename):
            ops = per[ename]
            eo = engobj[ename]

            def section(_):
                waited = {f: 0 for f in self.ENGINES}
                for i, emit, inc, deps in ops:
                    want = {}
                    for d in deps:
                        f = openg[d]
                        v = cums[d]
                        if v > want.get(f, 0):
                            want[f] = v
                    for f, v in want.items():
                        if waited[f] < v:
                            eo.wait_ge(sems[f], v)
                            waited[f] = v
                    inst = emit(eo)
                    if inc:
                        inst.then_inc(sems[streams[i]], inc)

            return section

        for ename, sec in (("SP", block.sync), ("PE", block.tensor),
                           ("ACT", block.scalar), ("DVE", block.vector),
                           ("POOL", block.gpsimd)):
            if per[ename]:
                sec(make_section(ename))


class Mem:

    def __init__(self, nc, cap):
        self.nc = nc
        self.base = (nc.sbuf_base + 31) // 32 * 32
        self.cap = min(cap, nc.sbuf_top - self.base)

    def allocate(self):
        pass

    def at(self, off, name, shape, dtype):
        per = int(np.prod(shape[1:])) * mybir.dt.size(dtype)
        assert off % 32 == 0 and off + per <= self.cap, (name, off, per)
        return self.nc.alloc_sbuf_tensor_at(name, list(shape), dtype,
                                            offset=self.base + off)


def _ceil(a, m):
    return (a + m - 1) // m * m


def build_program(T=2048, NB=2, debug=False):
    nc = bass.Bass("TRN2", target_bir_lowering=False)
    NT = T // 128
    NCHUNK = T // 256
    p = Prog()

    dram = {}

    def din(name, shape, dtype=F32):
        dram[name] = nc.dram_tensor(name, shape, dtype, kind="ExternalInput").ap()
        return dram[name]

    def dout(name, shape, dtype=F32):
        dram[name] = nc.dram_tensor(name, shape, dtype, kind="ExternalOutput").ap()
        return dram[name]

    def dscr(name, shape, dtype):
        kind = "ExternalOutput" if debug else "Internal"
        dram[name] = nc.dram_tensor(name, shape, dtype, kind=kind).ap()
        return dram[name]

    x_d = din("x", [NB, T, E])
    wq_d = din("wq", [E, E]); wk_d = din("wk", [E, E]); wv_d = din("wv", [E, E])
    bq_d = din("bq", [E]); bk_d = din("bk", [E]); bv_d = din("bv", [E])
    w1_d = din("w1", [H, E]); w2_d = din("w2", [E, H])
    cw0_d = din("cw0", [H]); cw1_d = din("cw1", [H]); cw2_d = din("cw2", [H])
    cb_d = din("cb", [H])
    out_d = dout("out", [NB, T, E])

    s_xbf = dscr("s_xbf", [NB, T, E], BF16)
    s_xqT = dscr("s_xqT", [NB, ECH, 128, T + 16], BF16)
    s_xln = dscr("s_xln", [NB, T, E], F32)
    s_m1 = dscr("s_m1", [NB, T + 2], F32)
    s_wq = dscr("s_wq", [E, E], BF16)
    s_wk = dscr("s_wk", [E, E], BF16)
    s_wv = dscr("s_wv", [E, E], BF16)
    s_w1i = dscr("s_w1i", [H, E], BF16)
    s_w2i = dscr("s_w2i", [E, H], BF16)
    if debug:
        d_s2seg = dscr("d_s2seg", [1, 256], F32)
        d_sbc = dscr("d_sbc", [128, 256], F32)
        d_fbc = dscr("d_fbc", [128, 258], F32)
        d_g = dscr("d_g", [HCH, 128, 256], F32)
        d_hq = dscr("d_hq", [HCH, 128, 256], BF16)
        d_mx = dscr("d_mx", [128, 256], F32)
        d_mn = dscr("d_mn", [128, 256], F32)

    mem = Mem(nc, SBUF_CAP)
    mem.allocate()

    off = 0
    def P_(name, shape, dtype):
        nonlocal off
        h = mem.at(off, name, shape, dtype)
        per = int(np.prod(shape[1:])) * mybir.dt.size(dtype)
        off = _ceil(off + per, 32)
        return h

    iden_bf = P_("iden_bf", [128, 128], BF16)
    iden_f = P_("iden_f", [128, 128], F32)
    maskneg = P_("maskneg", [128, 128], F32)
    ones1 = P_("ones1", [1, 128], F32)
    onescol = P_("onescol", [128, 1], F32)
    neghalf = P_("neghalf", [128, 1], F32)
    w0c = P_("w0c", [128, 32], F32)
    w1c = P_("w1c", [128, 32], F32)
    w2c = P_("w2c", [128, 32], F32)
    cbc = P_("cbc", [128, 32], F32)
    bqc = P_("bqc", [128, 8], F32)
    bkc = P_("bkc", [128, 8], F32)
    misc = P_("misc", [128, 64], F32)
    m1ct = P_("m1ct", [128, NT], F32)
    mrow = P_("mrow", [NT, 128], F32)
    mw = P_("mw", [1, 8], F32)
    mwb = P_("mwb", [128, 2], F32)
    zpad = P_("zpad", [128, 8], BF16)
    PERS_END = off
    OB = _ceil(PERS_END, 32)
    AVAIL = SBUF_CAP - OB

    MS, NEGM, VS1, VS2, VT, RSTD, NEGMR, M1, M1C, R1, S1V, BIASV, RCOL = range(13)
    MRAW = 14
    CMAX0 = 16
    CSUM0 = 20
    COLSUM = 24
    MX0 = 28
    RWB = 44

    def mc(i):
        return misc[:, i:i + 1]

    A_P = [mem.at(OB + jt * 4096, f"A_P{jt}", [128, T], BF16) for jt in range(NT)]
    A_xT = [mem.at(OB + e * 4096, f"A_xT{e}", [128, T], BF16) for e in range(ECH)]
    A_WvT = [mem.at(OB + 32768 + e * 2048, f"A_WvT{e}", [128, E], BF16)
             for e in range(ECH)]
    A_BVbc = mem.at(OB + 49152, "A_BVbc", [128, E], F32)
    A_brow = mem.at(OB + 53248, "A_brow", [32, 128], F32)
    A_brow2 = mem.at(OB + 53760, "A_brow2", [1, E], F32)
    A2 = OB + 65536
    A_qT = [mem.at(A2 + e * 4096, f"A_qT{e}", [128, T], BF16) for e in range(ECH)]
    A_kT = [mem.at(A2 + 32768 + e * 4096, f"A_kT{e}", [128, T], BF16)
            for e in range(ECH)]
    A_v = [mem.at(A2 + 65536 + t * 2048, f"A_v{t}", [128, E], BF16)
           for t in range(NT)]
    A_WqT = [mem.at(A2 + 98304 + e * 2048, f"A_WqT{e}", [128, E], BF16)
             for e in range(ECH)]
    A_WkT = [mem.at(A2 + 114688 + e * 2048, f"A_WkT{e}", [128, E], BF16)
             for e in range(ECH)]
    AT = A2 + 131072
    A_x32 = [mem.at(AT + i * 4096, f"A_x32_{i}", [128, E], F32) for i in range(2)]
    A_xbf = [mem.at(AT + 8192 + i * 2048, f"A_xbf{i}", [128, E], BF16)
             for i in range(2)]
    assert AT + 12288 <= mem.cap
    PV_xres = mem.at(A2 + 0, "PV_xres", [128, E], F32)
    PV_xln = [mem.at(A2 + 4096 + i * 4096, f"PV_xln{i}", [128, E], F32)
              for i in range(2)]
    PV_xqstg = mem.at(A2 + 12288, "PV_xqstg", [128, E], BF16)
    PV_xqbf = mem.at(A2 + 14336, "PV_xqbf", [128, E], BF16)
    PV_xsrc = [mem.at(A2 + 16384 + i * 4096, f"PV_xsrc{i}", [128, E], F32)
               for i in range(2)]

    PR_w1 = [mem.at(OB + i * 4096, f"PR_w1_{i}", [128, E], F32) for i in range(32)]
    PR_w2 = [mem.at(OB + i * 16384, f"PR_w2_{i}", [128, H], F32) for i in range(8)]
    PR_t1 = mem.at(OB + 131072, "PR_t1", [128, H], F32)
    PR_t2 = mem.at(OB + 147456, "PR_t2", [128, H], F32)
    PR_obf = [mem.at(OB + 163840 + i * 8192, f"PR_obf{i}", [128, H], BF16)
              for i in range(2)]
    PR_ap = mem.at(OB + 180224, "PR_ap", [128, 32], F32)

    F_W1qT = [mem.at(OB + e * 8192, f"F_W1qT{e}", [128, H], BF16)
              for e in range(ECH)]
    F_W2qT = [mem.at(OB + 65536 + c * 2048, f"F_W2qT{c}", [128, E], BF16)
              for c in range(HCH)]
    F_g = [[mem.at(OB + 131072 + pr * 16384 + s * 512, f"Fg{pr}_{s}",
                   [128, 256], BF16) for s in range(HCH)] for pr in range(2)]
    FX = OB + 163840
    F_xq = [[mem.at(FX + pr * 4352 + e * 544, f"F_xq{pr}_{e}", [128, 258], BF16)
             for e in range(ECH)] for pr in range(2)]
    F_xlnt = [[mem.at(FX + 8704 + (pr * 2 + i) * 4096, f"F_xlnt{pr}_{i}",
                      [128, E], F32) for i in range(2)] for pr in range(2)]
    F_hs = [mem.at(FX + 25088 + i * 544, f"F_hs{i}", [128, 258], BF16)
            for i in range(2)]
    F_u = [mem.at(FX + 26176 + i * 1024, f"F_u{i}", [128, 256], F32)
           for i in range(2)]
    F_Mx = mem.at(FX + 28224, "F_Mx", [128, 256], BF16)
    F_Mn = mem.at(FX + 28736, "F_Mn", [128, 256], BF16)
    F_fbc = [mem.at(FX + 29248 + i * 1056, f"F_fbc{i}", [128, 258], F32)
             for i in range(2)]
    F_sbc = mem.at(FX + 31360, "F_sbc", [128, 256], F32)
    F_fw = [mem.at(FX + 32384 + i * 1056, f"F_fw{i}", [1, 258], F32)
            for i in range(2)]
    F_s2seg = mem.at(FX + 34496, "F_s2seg", [1, 256], F32)
    F_ev = [mem.at(FX + 35520 + i * 2048, f"F_ev{i}", [128, 512], F32)
            for i in range(2)]
    F_lnt = mem.at(FX + 39616, "F_lnt", [128, 512], F32)
    F_D = [[mem.at(FX + 41664 + (k * 2 + pr) * 256, f"F_D{k}_{pr}",
                   [128, 128], BF16) for pr in range(2)] for k in range(3)]
    assert FX + 43200 <= mem.cap, FX + 43200

    import contextlib
    pstack = contextlib.ExitStack()
    B = [pstack.enter_context(nc.psum_tensor(f"bank{i}", [128, 512], F32))
         for i in range(6)]
    X6 = pstack.enter_context(nc.psum_tensor("xb6", [128, 1024], BF16))
    X7 = pstack.enter_context(nc.psum_tensor("xb7", [128, 1024], BF16))

    p.op("POOL", lambda e: e.iota(out=iden_f[:], pattern=[[1, 128]], base=0,
                                  channel_multiplier=-1,
                                  allow_small_or_imprecise_dtypes=True),
         writes=["idenf"])
    p.op("DVE", lambda e: e.tensor_scalar(out=maskneg[:], in0=iden_f[:],
                                          scalar1=0.0, scalar2=NEGBIG,
                                          op0=ALU.is_lt, op1=ALU.mult),
         reads=["idenf"], writes=["maskneg"])
    p.op("DVE", lambda e: e.tensor_scalar(out=iden_f[:], in0=iden_f[:],
                                          scalar1=0.0, scalar2=None,
                                          op0=ALU.is_equal),
         reads=["idenf"], writes=["idenf"])
    p.op("DVE", lambda e: e.tensor_copy(out=iden_bf[:], in_=iden_f[:]),
         reads=["idenf"], writes=["idenb"])
    p.op("POOL", lambda e: e.memset(ones1[:], 1.0), writes=["ones1"])
    p.op("POOL", lambda e: e.memset(onescol[:], 1.0), writes=["onescol"])
    p.op("POOL", lambda e: e.memset(neghalf[:], -0.5), writes=["neghalf"])
    p.op("POOL", lambda e: e.memset(zpad[:], 0.0), writes=["zpad"])
    p.op("POOL", lambda e: e.memset(misc[:, 60:64], 0.0), writes=["msczero"])

    for b in range(NB):
        for e8 in range(ECH):
            p.op("POOL", lambda e, b=b, e8=e8: e.dma_start(
                out=s_xqT[b, e8, :, 0:8], in_=zpad[:]),
                reads=["zpad"], writes=[f"sxq_zh{b}_{e8}"], dma=True)
            p.op("POOL", lambda e, b=b, e8=e8: e.dma_start(
                out=s_xqT[b, e8, :, T + 8:T + 16], in_=zpad[:]),
                reads=["zpad"], writes=[f"sxq_zt{b}_{e8}"], dma=True)
    for b in range(NB):
        p.op("POOL", lambda e, b=b: e.dma_start(
            out=s_m1[b, 0:1].rearrange("(o a) -> o a", o=1), in_=misc[0:1, 60:61]),
            reads=["msczero"], writes=[f"sm1z{b}"], dma=True)
        p.op("POOL", lambda e, b=b: e.dma_start(
            out=s_m1[b, T + 1:T + 2].rearrange("(o a) -> o a", o=1), in_=misc[0:1, 61:62]),
            reads=["msczero"], writes=[f"sm1z2{b}"], dma=True)

    def load_col_params(d_ap, n, dst, key, bankslice, ident_n):
        p.op("SP", lambda e: e.dma_start(
            out=A_brow[0:n, :], in_=d_ap.rearrange("(a b) -> a b", b=128)),
            writes=["brow"], dma=True)
        p.op("PE", lambda e: e.transpose(out=bankslice, in_=A_brow[0:n, :],
                                         identity=iden_f[0:n, 0:n]),
             reads=["brow", "idenf"], writes=["B5"])
        p.op("DVE", lambda e: e.tensor_copy(out=dst, in_=bankslice),
             reads=["B5"], writes=[key])

    for d_ap, dst, key in ((cw0_d, w0c, "w0c"), (cw1_d, w1c, "w1c"),
                           (cw2_d, w2c, "w2c"), (cb_d, cbc, "cbc")):
        load_col_params(d_ap, 32, dst[:, 0:32], key, B[5][0:128, 0:32], 32)
    for d_ap, dst, key in ((bq_d, bqc, "bqc"), (bk_d, bkc, "bkc")):
        load_col_params(d_ap, 8, dst[:, 0:8], key, B[5][0:128, 0:8], 8)
    for wd, sd, nm in ((wq_d, s_wq, "wq"), (wk_d, s_wk, "wk"), (wv_d, s_wv, "wv")):
        for i in range(ECH):
            sl = slice(i * 128, (i + 1) * 128)
            p.op("SP", lambda e, wd=wd, sl=sl, i=i: e.dma_start(
                out=A_x32[i % 2][:], in_=wd[sl, :]),
                writes=[f"x32_{i % 2}"], dma=True)
            p.op("DVE", lambda e, i=i: e.tensor_copy(
                out=A_xbf[i % 2][:], in_=A_x32[i % 2][:]),
                reads=[f"x32_{i % 2}"], writes=[f"xbf{i % 2}"])
            p.op("POOL", lambda e, sd=sd, sl=sl, i=i: e.dma_start(
                out=sd[sl, :], in_=A_xbf[i % 2][:]),
                reads=[f"xbf{i % 2}"], writes=[f"scr_{nm}_{i}"], dma=True)

    def attn(b):
        for it in range(NT):
            sl = slice(it * 128, (it + 1) * 128)
            p.op("SP", lambda e, sl=sl, it=it: e.dma_start(
                out=A_x32[it % 2][:], in_=x_d[b, sl, :]),
                writes=[f"x32_{it % 2}"], dma=True)
            p.op("DVE", lambda e, it=it: e.tensor_copy(
                out=A_xbf[it % 2][:], in_=A_x32[it % 2][:]),
                reads=[f"x32_{it % 2}"], writes=[f"xbf{it % 2}"])
            p.op("POOL", lambda e, sl=sl, it=it: e.dma_start(
                out=s_xbf[b, sl, :], in_=A_xbf[it % 2][:]),
                reads=[f"xbf{it % 2}"], writes=[f"sxbf{it}"], dma=True)
        p.fence([f"xT{e}" for e in range(ECH)]
                + [f"WvT{e}" for e in range(ECH)] + ["BVbc", "bvrow"],
                [f"P{jt}" for jt in range(NT)])
        p.op("SP", lambda e: e.dma_start(out=A_brow2[:],
                                         in_=bv_d.rearrange("(o a) -> o a", o=1)),
             writes=["bvrow"], dma=True)
        for h in range(2):
            p.op("PE", lambda e, h=h: e.matmul(
                out=B[h][:], lhsT=ones1[:],
                rhs=A_brow2[:, h * 512:(h + 1) * 512], start=True, stop=True),
                reads=["bvrow", "ones1"], writes=[f"B{h}"])
            p.op("DVE", lambda e, h=h: e.tensor_copy(
                out=A_BVbc[:, h * 512:(h + 1) * 512], in_=B[h][:]),
                reads=[f"B{h}"], writes=["BVbc"])
        for e8 in range(ECH):
            p.op("SP", lambda e, e8=e8: e.dma_start(
                out=A_xT[e8][:], in_=s_xbf[b, :, e8 * 128:(e8 + 1) * 128],
                transpose=True),
                reads=[f"sxbf{it}" for it in range(NT)],
                writes=[f"xT{e8}"], dma=True, xbar=True)
        wlist = [(s_wv, A_WvT, "WvT", "wv")]
        if b == 0:
            wlist += [(s_wq, A_WqT, "WqT", "wq"), (s_wk, A_WkT, "WkT", "wk")]
        for sd, dstl, nm, srck in wlist:
            for e8 in range(ECH):
                p.op("SP", lambda e, sd=sd, dstl=dstl, e8=e8: e.dma_start(
                    out=dstl[e8][:], in_=sd[:, e8 * 128:(e8 + 1) * 128],
                    transpose=True),
                    reads=[f"scr_{srck}_{i}" for i in range(ECH)],
                    writes=[f"{nm}{e8}"], dma=True, xbar=True)

        for it in range(NT):
            g = it % 2
            bk0, bk1 = 2 * g, 2 * g + 1
            for h, bk in ((0, bk0), (1, bk1)):
                for e8 in range(ECH):
                    p.op("PE", lambda e, it=it, h=h, bk=bk, e8=e8: e.matmul(
                        out=B[bk][:],
                        lhsT=A_xT[e8][:, it * 128:(it + 1) * 128],
                        rhs=A_WvT[e8][:, h * 512:(h + 1) * 512],
                        start=(e8 == 0), stop=(e8 == ECH - 1)),
                        reads=[f"xT{e8}", f"WvT{e8}"], writes=[f"B{bk}"])
                p.op("DVE", lambda e, it=it, h=h, bk=bk: e.tensor_tensor(
                    out=A_v[it][:, h * 512:(h + 1) * 512], in0=B[bk][:],
                    in1=A_BVbc[:, h * 512:(h + 1) * 512], op=ALU.add),
                    reads=[f"B{bk}", "BVbc"], writes=[f"v{it}"])

        icw = min(512, T)
        for dstl, wTl, bc, nm in ((A_qT, A_WqT, bqc, "q"), (A_kT, A_WkT, bkc, "k")):
            wkey = "WqT" if nm == "q" else "WkT"
            for fc in range(ECH):
                for ic in range(T // icw):
                    bk = (fc * (T // icw) + ic) % 4
                    for e8 in range(ECH):
                        p.op("PE", lambda e, fc=fc, ic=ic, bk=bk, e8=e8,
                             wTl=wTl: e.matmul(
                            out=B[bk][:, 0:icw],
                            lhsT=wTl[e8][:, fc * 128:(fc + 1) * 128],
                            rhs=A_xT[e8][:, ic * icw:(ic + 1) * icw],
                            start=(e8 == 0), stop=(e8 == ECH - 1)),
                            reads=[f"xT{e8}", f"{wkey}{e8}"], writes=[f"B{bk}"])
                    p.op("ACT", lambda e, fc=fc, ic=ic, bk=bk, dstl=dstl,
                         bc=bc: e.activation(
                        out=dstl[fc][:, ic * icw:(ic + 1) * icw],
                        in_=B[bk][:, 0:icw],
                        func=AF.Identity, bias=bc[:, fc:fc + 1], scale=1.0),
                        reads=[f"B{bk}", "b" + nm + "c"],
                        writes=[f"{nm}T{fc}"])

        p.fence([f"P{jt}" for jt in range(NT)],
                [f"xT{e}" for e in range(ECH)] + [f"WvT{e}" for e in range(ECH)]
                + ["BVbc", "BVbc_use", "brow", "bvrow"])
        gchunk = 0
        for jt in range(NT):
            c0 = jt * 128
            bounds = [c0]
            nxt = (c0 // 512 + 1) * 512
            while nxt < T:
                bounds.append(nxt)
                nxt += 512
            bounds.append(T)
            chunks = [(bounds[i], bounds[i + 1] - bounds[i])
                      for i in range(len(bounds) - 1)]
            nch = len(chunks)
            banks = []
            for ci in range(nch):
                banks.append(gchunk % 6)
                gchunk += 1
            for e8 in range(ECH):
                for ci, (cs, w) in enumerate(chunks):
                    bk = banks[ci]
                    p.op("PE", lambda e, e8=e8, jt=jt, cs=cs, w=w, bk=bk: e.matmul(
                        out=B[bk][:, 0:w],
                        lhsT=A_kT[e8][:, jt * 128:(jt + 1) * 128],
                        rhs=A_qT[e8][:, cs:cs + w],
                        start=(e8 == 0), stop=(e8 == ECH - 1)),
                        reads=[f"kT{e8}", f"qT{e8}"], writes=[f"B{bk}"])
            bk0 = banks[0]
            p.op("DVE", lambda e, bk0=bk0: e.tensor_tensor(
                out=B[bk0][:, 0:128], in0=B[bk0][:, 0:128], in1=maskneg[:],
                op=ALU.add),
                reads=[f"B{bk0}", "maskneg"], writes=[f"B{bk0}"])
            for ci, (cs, w) in enumerate(chunks):
                bk = banks[ci]
                p.op("DVE", lambda e, bk=bk, w=w, ci=ci: e.tensor_reduce(
                    out=misc[:, CMAX0 + ci:CMAX0 + ci + 1], in_=B[bk][:, 0:w],
                    axis=AX, op=ALU.max),
                    reads=[f"B{bk}"], writes=[f"cmax{ci}"])
            p.op("DVE", lambda e, nch=nch: e.tensor_reduce(
                out=mc(MRAW), in_=misc[:, CMAX0:CMAX0 + nch], axis=AX,
                op=ALU.max),
                reads=[f"cmax{ci}" for ci in range(nch)], writes=["Mraw"])
            p.op("DVE", lambda e: e.tensor_scalar(
                out=mc(BIASV), in0=mc(MRAW), scalar1=-1.0 / 32.0, scalar2=None,
                op0=ALU.mult),
                reads=["Mraw"], writes=["biasv"])
            for ci, (cs, w) in enumerate(chunks):
                bk = banks[ci]
                p.op("ACT", lambda e, bk=bk, w=w, jt=jt, cs=cs, c0=c0,
                     ci=ci: e.activation(
                    out=A_P[jt][:, cs - c0:cs - c0 + w], in_=B[bk][:, 0:w],
                    func=AF.Exp, bias=mc(BIASV), scale=1.0 / 32.0,
                    accum_out=misc[:, CSUM0 + ci:CSUM0 + ci + 1]),
                    reads=[f"B{bk}", "biasv"],
                    writes=[f"P{jt}", f"csum{ci}", f"B{bk}_free"])
            p.op("DVE", lambda e, nch=nch: e.tensor_reduce(
                out=mc(COLSUM), in_=misc[:, CSUM0:CSUM0 + nch], axis=AX,
                op=ALU.add),
                reads=[f"csum{ci}" for ci in range(nch)], writes=["colsum"])
            p.op("DVE", lambda e: e.reciprocal(out=mc(RCOL), in_=mc(COLSUM)),
                 reads=["colsum"], writes=["rcol"])
            p.op("DVE", lambda e, jt=jt: e.tensor_scalar(
                out=A_v[jt][:], in0=A_v[jt][:], scalar1=mc(RCOL), scalar2=None,
                op0=ALU.mult),
                reads=["rcol", f"v{jt}"], writes=[f"v{jt}"])

        p.fence(["xres", "xln0", "xln1", "xqstg", "xqbf", "pvx0", "pvx1"],
                [f"qT{e}" for e in range(ECH)])
        pv_pend = []

        def pv_xpose_flush():
            pit = pv_pend.pop(0)
            for e8 in range(ECH):
                xb = X6 if e8 % 2 == 0 else X7
                xk = "X6" if e8 % 2 == 0 else "X7"
                p.op("PE", lambda e, e8=e8, xb=xb: e.transpose(
                    out=xb[:, 0:128], in_=PV_xqbf[:, e8 * 128:(e8 + 1) * 128],
                    identity=iden_bf[:]),
                    reads=["xqbf", "idenb"], writes=[xk])
                p.op("ACT", lambda e, e8=e8, xb=xb: e.copy(
                    out=PV_xqstg[:, e8 * 128:(e8 + 1) * 128], in_=xb[:, 0:128]),
                    reads=[xk], writes=["xqstg"])
            p.op("POOL", lambda e, pit=pit: e.dma_start(
                out=s_xqT[b, :, :, 8 + pit * 128:8 + (pit + 1) * 128]
                .rearrange("e p t -> p e t"),
                in_=PV_xqstg[:].rearrange("p (e t) -> p e t", e=ECH)),
                reads=["xqstg"], writes=[f"sxq{b}_{pit}"], dma=True)

        for it in range(NT):
            sl = slice(it * 128, (it + 1) * 128)
            g = it % 2
            p.op("SP", lambda e, sl=sl, g=g: e.dma_start(
                out=PV_xsrc[g][:], in_=x_d[b, sl, :]),
                writes=[f"pvx{g}"], dma=True)
            for h in range(2):
                bk = 2 * g + h
                for jt in range(it + 1):
                    p.op("PE", lambda e, it=it, jt=jt, h=h, bk=bk: e.matmul(
                        out=B[bk][:],
                        lhsT=A_P[jt][:, it * 128 - jt * 128:
                                      it * 128 - jt * 128 + 128],
                        rhs=A_v[jt][:, h * 512:(h + 1) * 512],
                        start=(jt == 0), stop=(jt == it)),
                        reads=[f"P{jt}", f"v{jt}"], writes=[f"B{bk}"])
                p.op("DVE", lambda e, h=h, bk=bk, g=g: e.tensor_tensor(
                    out=PV_xres[:, h * 512:(h + 1) * 512], in0=B[bk][:],
                    in1=PV_xsrc[g][:, h * 512:(h + 1) * 512], op=ALU.add),
                    reads=[f"B{bk}", f"pvx{g}"], writes=["xres"])
            if pv_pend:
                pv_xpose_flush()
            p.op("DVE", lambda e: e.tensor_reduce(
                out=mc(MS), in_=PV_xres[:], axis=AX, op=ALU.add),
                reads=["xres"], writes=["ms"])
            p.op("DVE", lambda e: e.tensor_scalar(
                out=mc(NEGM), in0=mc(MS), scalar1=-1.0 / E, scalar2=None,
                op0=ALU.mult), reads=["ms"], writes=["negm"])
            for h, vs in ((0, VS1), (1, VS2)):
                bk = 4 + h
                p.op("ACT", lambda e, h=h, vs=vs, bk=bk: e.activation(
                    out=B[bk][:], in_=PV_xres[:, h * 512:(h + 1) * 512],
                    func=AF.Square, bias=mc(NEGM), scale=1.0,
                    accum_out=misc[:, vs:vs + 1]),
                    reads=["xres", "negm"], writes=[f"B{bk}", f"vs{h}"])
            p.op("DVE", lambda e: e.tensor_tensor(
                out=mc(VT), in0=mc(VS1), in1=mc(VS2), op=ALU.add),
                reads=["vs0", "vs1"], writes=["vt"])
            p.op("DVE", lambda e: e.tensor_scalar(
                out=mc(VT), in0=mc(VT), scalar1=1.0 / E, scalar2=LNEPS,
                op0=ALU.mult, op1=ALU.add), reads=["vt"], writes=["vt"])
            p.op("POOL", lambda e: e.tensor_tensor(
                out=mc(RSTD), in0=mc(VT), in1=neghalf[:], op=ALU.pow),
                reads=["vt", "neghalf"], writes=["rstd"])
            p.op("DVE", lambda e: e.tensor_scalar(
                out=mc(NEGMR), in0=mc(NEGM), scalar1=mc(RSTD), scalar2=None,
                op0=ALU.mult), reads=["negm", "rstd"], writes=["negmr"])
            p.op("ACT", lambda e, g=g: e.activation(
                out=PV_xln[g][:], in_=PV_xres[:], func=AF.Identity,
                scale=mc(RSTD), bias=mc(NEGMR)),
                reads=["xres", "rstd", "negmr"], writes=[f"xln{g}"])
            p.op("POOL", lambda e, sl=sl, g=g: e.dma_start(
                out=s_xln[b, sl, :], in_=PV_xln[g][:]),
                reads=[f"xln{g}"], writes=[f"sxln{b}_{it}"], dma=True)
            p.op("DVE", lambda e, g=g: e.tensor_reduce(
                out=mc(M1), in_=PV_xln[g][:], axis=AX, op=ALU.max,
                apply_absolute_value=True),
                reads=[f"xln{g}"], writes=["m1"])
            p.op("DVE", lambda e: e.tensor_scalar(
                out=mc(M1C), in0=mc(M1), scalar1=QCLIP, scalar2=None,
                op0=ALU.max), reads=["m1"], writes=["m1c"])
            p.op("DVE", lambda e, it=it: e.tensor_copy(
                out=m1ct[:, it:it + 1], in_=mc(M1C)),
                reads=["m1c"], writes=["m1ct"])
            p.op("DVE", lambda e: e.reciprocal(out=mc(R1), in_=mc(M1C)),
                 reads=["m1c"], writes=["r1"])
            p.op("DVE", lambda e: e.tensor_scalar(
                out=mc(S1V), in0=mc(R1), scalar1=127.0, scalar2=None,
                op0=ALU.mult), reads=["r1"], writes=["s1v"])
            p.op("DVE", lambda e, g=g: e.tensor_scalar(
                out=PV_xres[:], in0=PV_xln[g][:], scalar1=mc(S1V),
                scalar2=MAGIC, op0=ALU.mult, op1=ALU.add),
                reads=[f"xln{g}", "s1v"], writes=["xres"])
            p.op("DVE", lambda e: e.tensor_scalar(
                out=PV_xqbf[:], in0=PV_xres[:], scalar1=MAGIC, scalar2=None,
                op0=ALU.subtract), reads=["xres"], writes=["xqbf"])
            pv_pend.append(it)
        while pv_pend:
            pv_xpose_flush()
        p.op("PE", lambda e: e.transpose(
            out=B[5][0:NT, 0:128], in_=m1ct[:], identity=iden_f[:]),
            reads=["m1ct", "idenf"], writes=["B5"])
        p.op("DVE", lambda e: e.tensor_copy(out=mrow[:], in_=B[5][0:NT, 0:128]),
             reads=["B5"], writes=["mrow"])
        p.op("POOL", lambda e: e.dma_start(
            out=s_m1[b, 1:T + 1].rearrange("(i p) -> i p", p=128),
            in_=mrow[:]),
            reads=["mrow"], writes=[f"sm1_{b}"], dma=True)

    def prep():
        p.fence([f"w1t{i}" for i in range(32)] + [f"w2t{i}" for i in range(8)]
                + [f"w1ap{i}" for i in range(32)] + [f"w2ap{i}" for i in range(8)]
                + ["PRt1", "PRt2", "PRobf0", "PRobf1"],
                [f"P{jt}" for jt in range(NT)] + [f"v{t}" for t in range(NT)]
                + [f"qT{e}" for e in range(ECH)] + [f"kT{e}" for e in range(ECH)]
                + [f"WqT{e}" for e in range(ECH)] + [f"WkT{e}" for e in range(ECH)]
                + [f"WvT{e}" for e in range(ECH)] + [f"xT{e}" for e in range(ECH)]
                + ["xres", "xln0", "xln1", "xqstg", "xqbf", "pvx0", "pvx1",
                   "BVbc", "bvrow", "brow"])

        def ternarize(n_tiles, tiles, wd, sd, mwi, nmw, tile_rows, free):
            for i in range(n_tiles):
                sl = slice(i * 128, (i + 1) * 128)
                p.op("SP", lambda e, i=i, sl=sl, wd=wd: e.dma_start(
                    out=tiles[i][:], in_=wd[sl, :]),
                    writes=[f"{nmw}t{i}"], dma=True)
                p.op("ACT", lambda e, i=i: e.activation(
                    out=PR_t1[:, 0:free], in_=tiles[i][:], func=AF.Abs,
                    accum_out=PR_ap[:, i:i + 1]),
                    reads=[f"{nmw}t{i}"], writes=["PRt1", f"{nmw}ap{i}"])
            p.op("DVE", lambda e, n=n_tiles: e.tensor_reduce(
                out=mc(MS), in_=PR_ap[:, 0:n], axis=AX, op=ALU.add),
                reads=[f"{nmw}ap{i}" for i in range(n_tiles)], writes=["ms"])
            p.op("PE", lambda e: e.matmul(out=B[5][0:1, 0:1], lhsT=mc(MS),
                                          rhs=onescol[:], start=True, stop=True),
                 reads=["ms", "onescol"], writes=["B5"])
            p.op("DVE", lambda e, mwi=mwi: e.tensor_scalar(
                out=mw[:, 2 * mwi:2 * mwi + 1], in0=B[5][0:1, 0:1],
                scalar1=1.0 / (H * E), scalar2=QCLIP, op0=ALU.mult, op1=ALU.max),
                reads=["B5"], writes=[f"mw{mwi}"])
            p.op("DVE", lambda e, mwi=mwi: e.reciprocal(
                out=mw[:, 2 * mwi + 1:2 * mwi + 2], in_=mw[:, 2 * mwi:2 * mwi + 1]),
                reads=[f"mw{mwi}"], writes=[f"rw{mwi}"])
            p.op("PE", lambda e, mwi=mwi: e.matmul(
                out=B[5][:, 0:2], lhsT=ones1[:], rhs=mw[:, 2 * mwi:2 * mwi + 2],
                start=True, stop=True),
                reads=[f"mw{mwi}", f"rw{mwi}", "ones1"], writes=["B5"])
            p.op("DVE", lambda e, mwi=mwi: e.tensor_copy(
                out=mwb[:, mwi:mwi + 1], in_=B[5][:, 0:1]),
                reads=["B5"], writes=[f"mwb{mwi}"])
            p.op("DVE", lambda e: e.tensor_copy(
                out=misc[:, RWB:RWB + 1], in_=B[5][:, 1:2]),
                reads=["B5"], writes=["rwb"])
            for i in range(n_tiles):
                ob = PR_obf[i % 2]
                obk = f"PRobf{i % 2}"
                sl = slice(i * 128, (i + 1) * 128)
                p.op("DVE", lambda e, i=i, free=free: e.tensor_scalar(
                    out=PR_t1[:, 0:free], in0=tiles[i][:],
                    scalar1=misc[:, RWB:RWB + 1], scalar2=MAGIC,
                    op0=ALU.mult, op1=ALU.add),
                    reads=[f"{nmw}t{i}", "rwb"], writes=["PRt1"])
                p.op("DVE", lambda e, free=free: e.tensor_scalar(
                    out=PR_t2[:, 0:free], in0=PR_t1[:, 0:free],
                    scalar1=MAGIC, scalar2=1.0,
                    op0=ALU.subtract, op1=ALU.min),
                    reads=["PRt1"], writes=["PRt2"])
                p.op("DVE", lambda e, ob=ob, free=free: e.tensor_scalar(
                    out=ob[:, 0:free], in0=PR_t2[:, 0:free],
                    scalar1=-1.0, scalar2=None, op0=ALU.max),
                    reads=["PRt2"], writes=[obk])
                p.op("POOL", lambda e, i=i, sl=sl, ob=ob, sd=sd,
                     free=free: e.dma_start(
                    out=sd[sl, :], in_=ob[:, 0:free]),
                    reads=[obk], writes=[f"s{nmw}i{i}"], dma=True)

        ternarize(32, PR_w1, w1_d, s_w1i, 0, "w1", 128, E)
        p.fence([f"w2t{i}" for i in range(8)],
                [f"w1t{i}" for i in range(32)])
        ternarize(8, PR_w2, w2_d, s_w2i, 1, "w2", 128, H)

    NGC = NB * NCHUNK

    def ffn_entry():
        newk = ([f"W1qT{e}" for e in range(ECH)]
                + [f"W2qT{c}" for c in range(HCH)]
                + [f"Fg{pr}_{s}" for pr in range(2) for s in range(HCH)]
                + [f"F_xq{pr}_{e}" for pr in range(2) for e in range(ECH)]
                + [f"fxlnt{pr}_{t}" for pr in range(2) for t in range(2)]
                + [f"fw{pr}" for pr in range(2)] + [f"fbc{pr}" for pr in range(2)]
                + ["sbc", "fs2seg", "Mx", "Mn", "flnt"]
                + [f"hs{i}" for i in range(2)] + [f"u{i}" for i in range(2)]
                + [f"fev{i}" for i in range(2)]
                + [f"D{k}_{pr}" for k in range(3) for pr in range(2)]
                + ["fcm0", "fcm1", "fcn0", "fcn1", "fm2", "fr2", "fs2v", "frs2"])
        oldk = ([f"P{jt}" for jt in range(NT)] + [f"v{t}" for t in range(NT)]
                + [f"qT{e}" for e in range(ECH)] + [f"kT{e}" for e in range(ECH)]
                + [f"xT{e}" for e in range(ECH)] + [f"WqT{e}" for e in range(ECH)]
                + [f"WkT{e}" for e in range(ECH)] + [f"WvT{e}" for e in range(ECH)]
                + ["xres", "xln0", "xln1", "xqstg", "xqbf", "pvx0", "pvx1",
                   "BVbc", "bvrow", "brow", "x32_0", "x32_1", "xbf0", "xbf1"]
                + [f"w1t{i}" for i in range(32)] + [f"w2t{i}" for i in range(8)]
                + [f"w1ap{i}" for i in range(32)] + [f"w2ap{i}" for i in range(8)]
                + ["PRt1", "PRt2", "PRobf0", "PRobf1", "m1ct", "mrow"]
                + [f"cmax{i}" for i in range(4)] + [f"csum{i}" for i in range(4)]
                + ["colsum", "msczero", "Mraw"])
        p.fence(newk, oldk)
        for e8 in range(ECH):
            p.op("SP", lambda e, e8=e8: e.dma_start(
                out=F_W1qT[e8][:], in_=s_w1i[:, e8 * 128:(e8 + 1) * 128],
                transpose=True),
                reads=[f"sw1i{i}" for i in range(32)],
                writes=[f"W1qT{e8}"], dma=True, xbar=True)
        for c in range(HCH):
            p.op("SP", lambda e, c=c: e.dma_start(
                out=F_W2qT[c][:], in_=s_w2i[:, c * 128:(c + 1) * 128],
                transpose=True),
                reads=[f"sw2i{i}" for i in range(8)],
                writes=[f"W2qT{c}"], dma=True, xbar=True)

    def ffn_loads(gc):
        b, ct = divmod(gc, NCHUNK)
        t0 = ct * 256
        par = gc % 2
        for e8 in range(ECH):
            p.op("SP", lambda e, e8=e8, b=b, t0=t0, par=par: e.dma_start(
                out=F_xq[par][e8][:],
                in_=s_xqT[b, e8, :, 8 + t0 - 1:8 + t0 + 257]),
                reads=[f"sxq{b}_{it}" for it in range(max(0, t0 // 128 - 1),
                                                      min(NT, t0 // 128 + 3))]
                + [f"sxq_zh{b}_{e8}", f"sxq_zt{b}_{e8}"],
                writes=[f"F_xq{par}_{e8}"], dma=True)
        p.op("SP", lambda e, b=b, t0=t0, par=par: e.dma_start(
            out=F_fw[par][:],
            in_=s_m1[b, t0:t0 + 258].rearrange("(o a) -> o a", o=1)),
            reads=[f"sm1_{b}", f"sm1z{b}", f"sm1z2{b}"],
            writes=[f"fw{par}"], dma=True)

    def ffn_loads_xlnt(gc):
        b, ct = divmod(gc, NCHUNK)
        t0 = ct * 256
        par = gc % 2
        for tb in range(2):
            sl = slice(t0 + tb * 128, t0 + (tb + 1) * 128)
            p.op("SP", lambda e, sl=sl, tb=tb, b=b, par=par: e.dma_start(
                out=F_xlnt[par][tb][:], in_=s_xln[b, sl, :]),
                reads=[f"sxln{b}_{sl.start // 128}"],
                writes=[f"fxlnt{par}_{tb}"], dma=True)

    def ffn_fbc(gc):
        par = gc % 2
        p.op("DVE", lambda e, par=par: e.tensor_scalar(
            out=F_fw[par][:], in0=F_fw[par][:], scalar1=mw[0:1, 0:1],
            scalar2=1.0 / 127.0, op0=ALU.mult, op1=ALU.mult),
            reads=[f"fw{par}", "mw0"], writes=[f"fw{par}"])
        p.op("PE", lambda e, par=par: e.matmul(
            out=B[3][:, 0:258], lhsT=ones1[:], rhs=F_fw[par][:],
            start=True, stop=True),
            reads=[f"fw{par}", "ones1"], writes=["B3s"])
        p.op("DVE", lambda e, par=par: e.tensor_copy(
            out=F_fbc[par][:], in_=B[3][:, 0:258]),
            reads=["B3s"], writes=[f"fbc{par}"])

    def ffn_w1(gc, s):
        par = gc % 2
        bk = s % 2
        for e8 in range(ECH):
            p.op("PE", lambda e, s=s, e8=e8, bk=bk, par=par: e.matmul(
                out=B[bk][:, 0:258],
                lhsT=F_W1qT[e8][:, s * 128:(s + 1) * 128],
                rhs=F_xq[par][e8][:],
                start=(e8 == 0), stop=(e8 == ECH - 1)),
                reads=[f"W1qT{e8}", f"F_xq{par}_{e8}"], writes=[f"B{bk}"])
        p.op("DVE", lambda e, s=s, bk=bk, par=par: e.tensor_tensor(
            out=F_hs[s % 2][:], in0=B[bk][:, 0:258], in1=F_fbc[par][:],
            op=ALU.mult),
            reads=[f"B{bk}", f"fbc{par}"], writes=[f"hs{s % 2}"])

    def ffn_dbuild(s):
        for k, wc, wk in ((0, w0c, "w0c"), (1, w1c, "w1c"), (2, w2c, "w2c")):
            p.op("ACT", lambda e, k=k, s=s, wc=wc: e.activation(
                out=F_D[k][s % 2][:], in_=iden_f[:], func=AF.Identity,
                scale=wc[:, s:s + 1]),
                reads=["idenf", wk], writes=[f"D{k}_{s % 2}"])

    def ffn_conv(gc, s):
        par = gc % 2
        for k in range(3):
            p.op("PE", lambda e, k=k, s=s: e.matmul(
                out=B[2][:, 0:256], lhsT=F_D[k][s % 2][:],
                rhs=F_hs[s % 2][:, k:k + 256],
                start=(k == 0), stop=(k == 2)),
                reads=[f"D{k}_{s % 2}", f"hs{s % 2}"], writes=["B2c"])
        p.op("ACT", lambda e, s=s, par=par: e.activation(
            out=F_g[par][s][:], in_=B[2][:, 0:256], func=AF.Silu,
            bias=cbc[:, s:s + 1], scale=1.0),
            reads=["B2c", "cbc"], writes=[f"Fg{par}_{s}"])
        if s == 0:
            p.op("DVE", lambda e, par=par: e.tensor_copy(
                out=F_Mx[:], in_=F_g[par][0][:]),
                reads=[f"Fg{par}_0"], writes=["Mx"])
            p.op("DVE", lambda e, par=par: e.tensor_copy(
                out=F_Mn[:], in_=F_g[par][0][:]),
                reads=[f"Fg{par}_0"], writes=["Mn"])
        else:
            p.op("DVE", lambda e, s=s, par=par: e.tensor_tensor(
                out=F_Mx[:], in0=F_Mx[:], in1=F_g[par][s][:], op=ALU.max),
                reads=["Mx", f"Fg{par}_{s}"], writes=["Mx"])
            p.op("DVE", lambda e, s=s, par=par: e.tensor_tensor(
                out=F_Mn[:], in0=F_Mn[:], in1=F_g[par][s][:], op=ALU.min),
                reads=["Mn", f"Fg{par}_{s}"], writes=["Mn"])

    def ffn_scales_a(pgc):
        for i, key in ((0, "Mx"), (1, "Mn")):
            for tb in range(2):
                src = F_Mx if i == 0 else F_Mn
                p.op("PE", lambda e, i=i, tb=tb, src=src, key=key: e.transpose(
                    out=X6[:, (i * 2 + tb) * 128:(i * 2 + tb + 1) * 128],
                    in_=src[:, tb * 128:(tb + 1) * 128], identity=iden_bf[:]),
                    reads=[key, "idenb"], writes=["X6"])
        for tb in range(2):
            p.op("DVE", lambda e, tb=tb: e.tensor_reduce(
                out=misc[:, 16 + tb:17 + tb],
                in_=X6[:, tb * 128:(tb + 1) * 128], axis=AX, op=ALU.max),
                reads=["X6"], writes=[f"fcm{tb}"])
            p.op("DVE", lambda e, tb=tb: e.tensor_reduce(
                out=misc[:, 18 + tb:19 + tb],
                in_=X6[:, (2 + tb) * 128:(3 + tb) * 128], axis=AX, op=ALU.min,
                negate=True),
                reads=["X6"], writes=[f"fcn{tb}"])
        p.op("DVE", lambda e: e.tensor_tensor(
            out=misc[:, 20:22], in0=misc[:, 16:18], in1=misc[:, 18:20],
            op=ALU.max),
            reads=["fcm0", "fcm1", "fcn0", "fcn1"], writes=["fm2"])
        p.op("DVE", lambda e: e.tensor_scalar(
            out=misc[:, 20:22], in0=misc[:, 20:22], scalar1=QCLIP, scalar2=None,
            op0=ALU.max), reads=["fm2"], writes=["fm2"])
        p.op("DVE", lambda e: e.tensor_scalar(
            out=misc[:, 24:26], in0=misc[:, 20:22], scalar1=mwb[:, 1:2],
            scalar2=1.0 / 127.0, op0=ALU.mult, op1=ALU.mult),
            reads=["fm2", "mwb1"], writes=["frs2"])
        p.op("DVE", lambda e: e.reciprocal(out=misc[:, 22:24], in_=misc[:, 20:22]),
             reads=["fm2"], writes=["fr2"])
        p.op("DVE", lambda e: e.tensor_scalar(
            out=misc[:, 30:31], in0=misc[:, 22:23], scalar1=127.0, scalar2=None,
            op0=ALU.mult), reads=["fr2"], writes=["fs2v"])
        p.op("DVE", lambda e: e.tensor_scalar(
            out=misc[:, 62:63], in0=misc[:, 23:24], scalar1=127.0, scalar2=None,
            op0=ALU.mult), reads=["fr2"], writes=["fs2v"])

    def ffn_scales_b(pgc):
        import os as _os2
        _ksb = _os2.environ.get("KSB", "tcc")
        p.op("PE", lambda e: e.matmul(
            out=B[3][0:33, 384:512], lhsT=misc[:, 30:63], rhs=iden_f[:],
            start=True, stop=True, is_transpose=True),
            reads=["fs2v", "idenf"], writes=["B3s"])
        for half, row in ((0, 0), (1, 32))[:_ksb.count("c")]:
            p.op("DVE", lambda e, half=half, row=row: e.tensor_copy(
                out=F_s2seg[:, half * 128:(half + 1) * 128],
                in_=B[3][row:row + 1, 384:512]),
                reads=["B3s"], writes=["fs2seg"])

    def ffn_scales_c(pgc):
        p.op("PE", lambda e: e.matmul(
            out=B[3][:, 0:256], lhsT=ones1[:], rhs=F_s2seg[:],
            start=True, stop=True),
            reads=["fs2seg", "ones1"], writes=["B3s"])
        p.op("DVE", lambda e: e.tensor_copy(out=F_sbc[:], in_=B[3][:, 0:256]),
             reads=["B3s"], writes=["sbc"])

    def ffn_round(pgc, cs):
        par = pgc % 2
        up = cs % 2
        p.op("DVE", lambda e, cs=cs, par=par, up=up: e.tensor_tensor(
            out=F_u[up][:], in0=F_g[par][cs][:], in1=F_sbc[:], op=ALU.mult),
            reads=[f"Fg{par}_{cs}", "sbc"], writes=[f"u{up}"])
        p.op("DVE", lambda e, cs=cs, par=par, up=up: e.tensor_scalar(
            out=F_g[par][cs][:], in0=F_u[up][:], scalar1=MAGIC, scalar2=MAGIC,
            op0=ALU.add, op1=ALU.subtract),
            reads=[f"u{up}"], writes=[f"Fg{par}_{cs}"])

    def ffn_w2mm(pgc, grp, cs):
        par = pgc % 2
        tb, h = grp // 2, grp % 2
        bk = 4 + (grp % 2)
        p.op("PE", lambda e, par=par, tb=tb, h=h, cs=cs, bk=bk: e.matmul(
            out=B[bk][:],
            lhsT=F_g[par][cs][:, tb * 128:(tb + 1) * 128],
            rhs=F_W2qT[cs][:, h * 512:(h + 1) * 512],
            start=(cs == 0), stop=(cs == HCH - 1)),
            reads=[f"Fg{par}_{cs}", f"W2qT{cs}"], writes=[f"B{bk}"])

    def ffn_w2evac(pgc, grp):
        par = pgc % 2
        tb, h = grp // 2, grp % 2
        bk = 4 + (grp % 2)
        ev = grp % 2
        p.op("ACT", lambda e, bk=bk, ev=ev, tb=tb: e.activation(
            out=F_ev[ev][:], in_=B[bk][:], func=AF.Identity,
            scale=misc[:, 24 + tb:25 + tb]),
            reads=[f"B{bk}", "frs2"], writes=[f"fev{ev}"])
        p.op("DVE", lambda e, par=par, tb=tb, h=h, ev=ev: e.tensor_tensor(
            out=F_xlnt[par][tb][:, h * 512:(h + 1) * 512],
            in0=F_ev[ev][:], in1=F_xlnt[par][tb][:, h * 512:(h + 1) * 512],
            op=ALU.add),
            reads=[f"fev{ev}", f"fxlnt{par}_{tb}"], writes=[f"fxlnt{par}_{tb}"])

    def ffn_ln2(pgc, tb):
        b, ct = divmod(pgc, NCHUNK)
        par = pgc % 2
        t0 = ct * 256
        sl = slice(t0 + tb * 128, t0 + (tb + 1) * 128)
        xt = F_xlnt[par][tb]
        xk = f"fxlnt{par}_{tb}"
        p.op("DVE", lambda e, xt=xt: e.tensor_reduce(
            out=mc(MS), in_=xt[:], axis=AX, op=ALU.add),
            reads=[xk], writes=["ms"])
        p.op("DVE", lambda e: e.tensor_scalar(
            out=mc(NEGM), in0=mc(MS), scalar1=-1.0 / E, scalar2=None,
            op0=ALU.mult), reads=["ms"], writes=["negm"])
        for h, vs in ((0, VS1), (1, VS2)):
            p.op("ACT", lambda e, xt=xt, h=h, vs=vs: e.activation(
                out=F_lnt[:], in_=xt[:, h * 512:(h + 1) * 512],
                func=AF.Square, bias=mc(NEGM), scale=1.0,
                accum_out=misc[:, vs:vs + 1]),
                reads=[xk, "negm"], writes=["flnt", f"vs{h}"])
        p.op("DVE", lambda e: e.tensor_tensor(
            out=mc(VT), in0=mc(VS1), in1=mc(VS2), op=ALU.add),
            reads=["vs0", "vs1"], writes=["vt"])
        p.op("DVE", lambda e: e.tensor_scalar(
            out=mc(VT), in0=mc(VT), scalar1=1.0 / E, scalar2=LNEPS,
            op0=ALU.mult, op1=ALU.add), reads=["vt"], writes=["vt"])
        p.op("POOL", lambda e: e.tensor_tensor(
            out=mc(RSTD), in0=mc(VT), in1=neghalf[:], op=ALU.pow),
            reads=["vt", "neghalf"], writes=["rstd"])
        p.op("DVE", lambda e: e.tensor_scalar(
            out=mc(NEGMR), in0=mc(NEGM), scalar1=mc(RSTD), scalar2=None,
            op0=ALU.mult), reads=["negm", "rstd"], writes=["negmr"])
        p.op("ACT", lambda e, xt=xt: e.activation(
            out=xt[:], in_=xt[:], func=AF.Identity,
            scale=mc(RSTD), bias=mc(NEGMR)),
            reads=[xk, "rstd", "negmr"], writes=[xk])
        st = p.op("POOL", lambda e, sl=sl, xt=xt, b=b: e.dma_start(
            out=out_d[b, sl, :], in_=xt[:]),
            reads=[xk], writes=[f"outst{b}_{sl.start}"], dma=True)
        out_stores.append(st)

    def ffn_chunk(gc):
        if gc + 1 < NGC:
            ffn_loads(gc + 1)
        ffn_fbc(gc)
        ffn_dbuild(0)
        have_prev = gc > 0
        pgc = gc - 1
        import os as _os
        _kd = _os.environ.get("KDEF", "full")
        _lv = {"none": 0, "scales": 1, "round": 2, "w2": 3, "evac": 4,
               "full": 9}[_kd]
        W2S = {0: 12, 1: 14, 2: 20, 3: 22}
        for s in range(HCH):
            ffn_w1(gc, s)
            if s >= 1:
                ffn_conv(gc, s - 1)
            if s + 1 < HCH:
                ffn_dbuild(s + 1)
            if have_prev:
                if _lv >= 1:
                    _ksc = _os.environ.get("KSC", "abc")
                    if s == 0 and "a" in _ksc:
                        ffn_scales_a(pgc)
                    elif s == 2 and "b" in _ksc:
                        ffn_scales_b(pgc)
                    elif s == 3 and "c" in _ksc:
                        ffn_scales_c(pgc)
                if _lv >= 2 and 4 <= s <= 11:
                    for cs in range(4 * (s - 4), 4 * (s - 3)):
                        ffn_round(pgc, cs)
                if _lv >= 4:
                    if s == 20:
                        ffn_w2evac(pgc, 0)
                    elif s == 22:
                        ffn_w2evac(pgc, 1)
                    elif s == 28:
                        ffn_w2evac(pgc, 2)
                    elif s == 30:
                        ffn_w2evac(pgc, 3)
                if _lv >= 5:
                    if s == 23:
                        ffn_ln2(pgc, 0)
                    elif s == 31:
                        ffn_ln2(pgc, 1)
                if s == 31 and gc + 1 < NGC:
                    ffn_loads_xlnt(gc + 1)
                if _lv >= 3:
                    for grp, s0 in W2S.items():
                        if s0 <= s < s0 + 8:
                            for cs in range(4 * (s - s0), 4 * (s - s0 + 1)):
                                ffn_w2mm(pgc, grp, cs)
        if not have_prev and gc + 1 < NGC:
            ffn_loads_xlnt(gc + 1)
        ffn_conv(gc, HCH - 1)

    def ffn_tail():
        pgc = NGC - 1
        ffn_scales_a(pgc)
        ffn_scales_b(pgc)
        ffn_scales_c(pgc)
        for cs in range(HCH):
            ffn_round(pgc, cs)
        for grp in range(4):
            for cs in range(HCH):
                ffn_w2mm(pgc, grp, cs)
            ffn_w2evac(pgc, grp)
            if grp == 1:
                ffn_ln2(pgc, 0)
        ffn_ln2(pgc, 1)

    out_stores = []
    import os as _os
    _cut = _os.environ.get("KCUT", "full")
    for b in range(NB):
        attn(b)
    if _cut != "attn":
        if _cut != "noprep":
            prep()
        if _cut not in ("prep", "noprep"):
            ffn_entry()
            ffn_loads(0)
            ffn_loads_xlnt(0)
            if _cut != "entry":
                nch = NGC if _cut == "full" else min(NGC, int(_cut[1:]))
                for gc in range(nch):
                    ffn_chunk(gc)
                if _cut == "full":
                    ffn_tail()
    p.op("POOL", lambda e: e.nop(), deps=out_stores)

    import contextlib as _ctx
    with _ctx.ExitStack() as st:
        block = st.enter_context(nc.Block())
        sems = {e: st.enter_context(nc.semaphore("sem" + e)) for e in Prog.ENGINES}
        p.build(nc, block, sems)
    pstack.close()
    return nc


def kernel(L0_attn_X, L0_attn_Wq, L0_attn_bq, L0_attn_Wk, L0_attn_bk,
           L0_attn_Wv, L0_attn_bv, L0_ffn_w1, L0_ffn_conv_w, L0_ffn_conv_b,
           L0_ffn_w2):
    X = np.ascontiguousarray(np.asarray(L0_attn_X, dtype=np.float32))
    cw = np.asarray(L0_ffn_conv_w, dtype=np.float32).reshape(4096, 3)
    common = {
        "wq": np.ascontiguousarray(np.asarray(L0_attn_Wq, np.float32)),
        "wk": np.ascontiguousarray(np.asarray(L0_attn_Wk, np.float32)),
        "wv": np.ascontiguousarray(np.asarray(L0_attn_Wv, np.float32)),
        "bq": np.ascontiguousarray(np.asarray(L0_attn_bq, np.float32)),
        "bk": np.ascontiguousarray(np.asarray(L0_attn_bk, np.float32)),
        "bv": np.ascontiguousarray(np.asarray(L0_attn_bv, np.float32)),
        "w1": np.ascontiguousarray(np.asarray(L0_ffn_w1, np.float32)),
        "w2": np.ascontiguousarray(np.asarray(L0_ffn_w2, np.float32)),
        "cw0": np.ascontiguousarray(cw[:, 0]),
        "cw1": np.ascontiguousarray(cw[:, 1]),
        "cw2": np.ascontiguousarray(cw[:, 2]),
        "cb": np.ascontiguousarray(np.asarray(L0_ffn_conv_b, np.float32)),
    }
    n_cores = 8
    nb = X.shape[0] // n_cores
    nc = build_program(T=X.shape[1], NB=nb)
    in_maps = []
    for c in range(n_cores):
        m = dict(common)
        m["x"] = np.ascontiguousarray(X[c * nb:(c + 1) * nb])
        in_maps.append(m)
    res = run_bass_kernel_spmd(nc, in_maps, core_ids=list(range(n_cores)))
    out = np.concatenate([np.asarray(r["out"]) for r in res.results], axis=0)
    return out.astype(np.float32)


# version 11
# speedup vs baseline: 1.1508x; 1.0126x over previous
import numpy as np
import ml_dtypes
import concourse.bass as bass
import concourse.mybir as mybir
from concourse.bass_utils import run_bass_kernel_spmd

F32 = mybir.dt.float32
BF16 = mybir.dt.bfloat16
U8 = mybir.dt.uint8
AF = mybir.ActivationFunctionType
ALU = mybir.AluOpType
AX = mybir.AxisListType.X

MAGIC = 12582912.0
NEGBIG = -1e30
LNEPS = 1e-5
QCLIP = 1e-5
E = 1024
H = 4096
ECH = 8
HCH = 32
SBUF_CAP = 212_256


class Prog:

    ENGINES = ("SP", "PE", "ACT", "DVE", "POOL", "PD", "AD")

    def __init__(self):
        self.ops = []
        self.res = {}
        self.seed = {}
        self.last_dma = {}
        self.last_xbar = None
        import os
        self.serial = set(os.environ.get("KSYNC", "").split(",")) - {""}

    def _touch(self, k, depset):
        if k in self.seed:
            depset.update(self.seed.pop(k))
        r = self.res.get(k)
        if r is None:
            r = self.res[k] = {"w": None, "r": []}
        return r

    def op(self, engine, emit, reads=(), writes=(), deps=(), dma=False,
           xbar=False):
        idx = len(self.ops)
        depset = set(deps)
        for k in reads:
            r = self._touch(k, depset)
            if r["w"] is not None:
                depset.add(r["w"])
            r["r"].append(idx)
        for k in writes:
            r = self._touch(k, depset)
            if r["w"] is not None:
                depset.add(r["w"])
            depset.update(r["r"])
            r["w"] = idx
            r["r"] = []
        if dma:
            if engine in self.last_dma:
                depset.add(self.last_dma[engine])
            if self.last_xbar is not None:
                depset.add(self.last_xbar)
            if xbar:
                for q, d in self.last_dma.items():
                    depset.add(d)
                self.last_xbar = idx
            self.last_dma[engine] = idx
        depset.discard(idx)
        if engine == "PE":
            depset = {d for d in depset if self.ops[d][0] != "PE"}
        if idx > 0 and ("all" in self.serial or engine in self.serial):
            depset.add(idx - 1)
        self.ops.append((engine, emit, dma, depset))
        return idx

    def fence(self, new_keys, old_keys):
        dep = set()
        for k in old_keys:
            r = self.res.get(k)
            if r is not None:
                if r["w"] is not None:
                    dep.add(r["w"])
                dep.update(r["r"])
        for k in new_keys:
            self.seed[k] = set(self.seed.get(k, ())) | dep

    def build(self, nc, block, sems):
        needed = set()
        for _, _, _, deps in self.ops:
            needed.update(deps)
        incs = []
        cums = []
        cum = {e: 0 for e in self.ENGINES}
        streams = []
        for i, (eng, _, dma, _) in enumerate(self.ops):
            inc = 16 if dma else (1 if i in needed else 0)
            incs.append(inc)
            stream = ("PD" if eng == "POOL" else
                      "AD" if eng == "ACT" else eng) if dma else eng
            streams.append(stream)
            cum[stream] += inc
            cums.append(cum[stream])
        per = {e: [] for e in self.ENGINES}
        for i, (eng, emit, dma, deps) in enumerate(self.ops):
            per[eng].append((i, emit, incs[i], deps))
        engobj = {"SP": nc.sync, "PE": nc.tensor, "ACT": nc.scalar,
                  "DVE": nc.vector, "POOL": nc.gpsimd}
        openg = streams

        def make_section(ename):
            ops = per[ename]
            eo = engobj[ename]

            def section(_):
                waited = {f: 0 for f in self.ENGINES}
                for i, emit, inc, deps in ops:
                    want = {}
                    for d in deps:
                        f = openg[d]
                        v = cums[d]
                        if v > want.get(f, 0):
                            want[f] = v
                    for f, v in want.items():
                        if waited[f] < v:
                            eo.wait_ge(sems[f], v)
                            waited[f] = v
                    inst = emit(eo)
                    if inc:
                        inst.then_inc(sems[streams[i]], inc)

            return section

        for ename, sec in (("SP", block.sync), ("PE", block.tensor),
                           ("ACT", block.scalar), ("DVE", block.vector),
                           ("POOL", block.gpsimd)):
            if per[ename]:
                sec(make_section(ename))


class Mem:

    def __init__(self, nc, cap):
        self.nc = nc
        self.base = (nc.sbuf_base + 31) // 32 * 32
        self.cap = min(cap, nc.sbuf_top - self.base)

    def allocate(self):
        pass

    def at(self, off, name, shape, dtype):
        per = int(np.prod(shape[1:])) * mybir.dt.size(dtype)
        assert off % 32 == 0 and off + per <= self.cap, (name, off, per)
        return self.nc.alloc_sbuf_tensor_at(name, list(shape), dtype,
                                            offset=self.base + off)


def _ceil(a, m):
    return (a + m - 1) // m * m


def build_program(T=2048, NB=2, debug=False):
    nc = bass.Bass("TRN2", target_bir_lowering=False)
    NT = T // 128
    NCHUNK = T // 256
    p = Prog()

    dram = {}

    def din(name, shape, dtype=F32):
        dram[name] = nc.dram_tensor(name, shape, dtype, kind="ExternalInput").ap()
        return dram[name]

    def dout(name, shape, dtype=F32):
        dram[name] = nc.dram_tensor(name, shape, dtype, kind="ExternalOutput").ap()
        return dram[name]

    def dscr(name, shape, dtype):
        kind = "ExternalOutput" if debug else "Internal"
        dram[name] = nc.dram_tensor(name, shape, dtype, kind=kind).ap()
        return dram[name]

    x_d = din("x", [NB, T, E])
    wq_d = din("wq", [E, E]); wk_d = din("wk", [E, E]); wv_d = din("wv", [E, E])
    bq_d = din("bq", [E]); bk_d = din("bk", [E]); bv_d = din("bv", [E])
    w1_d = din("w1", [H, E]); w2_d = din("w2", [E, H])
    cw0_d = din("cw0", [H]); cw1_d = din("cw1", [H]); cw2_d = din("cw2", [H])
    cb_d = din("cb", [H])
    out_d = dout("out", [NB, T, E])

    s_xbf = dscr("s_xbf", [NB, T, E], BF16)
    s_xqT = dscr("s_xqT", [NB, ECH, 128, T + 16], BF16)
    s_xln = dscr("s_xln", [NB, T, E], F32)
    s_m1 = dscr("s_m1", [NB, T + 2], F32)
    s_wq = dscr("s_wq", [E, E], BF16)
    s_wk = dscr("s_wk", [E, E], BF16)
    s_wv = dscr("s_wv", [E, E], BF16)
    s_w1i = dscr("s_w1i", [H, E], BF16)
    s_w2i = dscr("s_w2i", [E, H], BF16)
    if debug:
        d_s2seg = dscr("d_s2seg", [1, 256], F32)
        d_sbc = dscr("d_sbc", [128, 256], F32)
        d_fbc = dscr("d_fbc", [128, 258], F32)
        d_g = dscr("d_g", [HCH, 128, 256], F32)
        d_hq = dscr("d_hq", [HCH, 128, 256], BF16)
        d_mx = dscr("d_mx", [128, 256], F32)
        d_mn = dscr("d_mn", [128, 256], F32)

    mem = Mem(nc, SBUF_CAP)
    mem.allocate()

    off = 0
    def P_(name, shape, dtype):
        nonlocal off
        h = mem.at(off, name, shape, dtype)
        per = int(np.prod(shape[1:])) * mybir.dt.size(dtype)
        off = _ceil(off + per, 32)
        return h

    iden_bf = P_("iden_bf", [128, 128], BF16)
    iden_f = P_("iden_f", [128, 128], F32)
    maskneg = P_("maskneg", [128, 128], F32)
    ones1 = P_("ones1", [1, 128], F32)
    onescol = P_("onescol", [128, 1], F32)
    neghalf = P_("neghalf", [128, 1], F32)
    w0c = P_("w0c", [128, 32], F32)
    w1c = P_("w1c", [128, 32], F32)
    w2c = P_("w2c", [128, 32], F32)
    cbc = P_("cbc", [128, 32], F32)
    bqc = P_("bqc", [128, 8], F32)
    bkc = P_("bkc", [128, 8], F32)
    misc = P_("misc", [128, 64], F32)
    m1ct = P_("m1ct", [128, NT], F32)
    mrow = P_("mrow", [NT, 128], F32)
    mw = P_("mw", [1, 8], F32)
    mwb = P_("mwb", [128, 2], F32)
    zpad = P_("zpad", [128, 8], BF16)
    PERS_END = off
    OB = _ceil(PERS_END, 32)
    AVAIL = SBUF_CAP - OB

    MS, NEGM, VS1, VS2, VT, RSTD, NEGMR, M1, M1C, R1, S1V, BIASV, RCOL = range(13)
    MRAW = 14
    CMAX0 = 16
    CSUM0 = 20
    COLSUM = 24
    MX0 = 28
    RWB = 44

    def mc(i):
        return misc[:, i:i + 1]

    A_P = [mem.at(OB + jt * 4096, f"A_P{jt}", [128, T], BF16) for jt in range(NT)]
    A_xT = [mem.at(OB + e * 4096, f"A_xT{e}", [128, T], BF16) for e in range(ECH)]
    A_WvT = [mem.at(OB + 32768 + e * 2048, f"A_WvT{e}", [128, E], BF16)
             for e in range(ECH)]
    A_BVbc = mem.at(OB + 49152, "A_BVbc", [128, E], F32)
    A_brow = mem.at(OB + 53248, "A_brow", [32, 128], F32)
    A_brow2 = mem.at(OB + 53760, "A_brow2", [1, E], F32)
    A2 = OB + 65536
    A_qT = [mem.at(A2 + e * 4096, f"A_qT{e}", [128, T], BF16) for e in range(ECH)]
    A_kT = [mem.at(A2 + 32768 + e * 4096, f"A_kT{e}", [128, T], BF16)
            for e in range(ECH)]
    A_v = [mem.at(A2 + 65536 + t * 2048, f"A_v{t}", [128, E], BF16)
           for t in range(NT)]
    A_WqT = [mem.at(A2 + 98304 + e * 2048, f"A_WqT{e}", [128, E], BF16)
             for e in range(ECH)]
    A_WkT = [mem.at(A2 + 114688 + e * 2048, f"A_WkT{e}", [128, E], BF16)
             for e in range(ECH)]
    AT = A2 + 131072
    A_x32 = [mem.at(AT + i * 4096, f"A_x32_{i}", [128, E], F32) for i in range(2)]
    A_xbf = [mem.at(AT + 8192 + i * 2048, f"A_xbf{i}", [128, E], BF16)
             for i in range(2)]
    assert AT + 12288 <= mem.cap
    PV_xres = mem.at(A2 + 0, "PV_xres", [128, E], F32)
    PV_xln = [mem.at(A2 + 4096 + i * 4096, f"PV_xln{i}", [128, E], F32)
              for i in range(2)]
    PV_xqstg = mem.at(A2 + 12288, "PV_xqstg", [128, E], BF16)
    PV_xqbf = mem.at(A2 + 14336, "PV_xqbf", [128, E], BF16)
    PV_xsrc = [mem.at(A2 + 16384 + i * 4096, f"PV_xsrc{i}", [128, E], F32)
               for i in range(2)]

    PR_in = [mem.at(OB + i * 8192, f"PR_in{i}", [128, 2048], F32)
             for i in range(4)]
    PR_ob = [mem.at(OB + 32768 + i * 4096, f"PR_ob{i}", [128, 2048], BF16)
             for i in range(2)]
    PR_ap = mem.at(OB + 40960, "PR_ap", [128, 32], F32)

    F_W1qT = [mem.at(OB + e * 8192, f"F_W1qT{e}", [128, H], BF16)
              for e in range(ECH)]
    F_W2qT = [mem.at(OB + 65536 + c * 2048, f"F_W2qT{c}", [128, E], BF16)
              for c in range(HCH)]
    F_g = [[mem.at(OB + 131072 + pr * 16384 + s * 512, f"Fg{pr}_{s}",
                   [128, 256], BF16) for s in range(HCH)] for pr in range(2)]
    FX = OB + 163840
    F_xq = [[mem.at(FX + pr * 4352 + e * 544, f"F_xq{pr}_{e}", [128, 258], BF16)
             for e in range(ECH)] for pr in range(2)]
    F_xlnt = [[mem.at(FX + 8704 + (pr * 2 + i) * 4096, f"F_xlnt{pr}_{i}",
                      [128, E], F32) for i in range(2)] for pr in range(2)]
    F_hs = [mem.at(FX + 25088 + i * 544, f"F_hs{i}", [128, 258], BF16)
            for i in range(2)]
    F_u = [mem.at(FX + 26176 + i * 1024, f"F_u{i}", [128, 256], F32)
           for i in range(2)]
    F_Mx = mem.at(FX + 28224, "F_Mx", [128, 256], BF16)
    F_Mn = mem.at(FX + 28736, "F_Mn", [128, 256], BF16)
    F_fbc = [mem.at(FX + 29248 + i * 1056, f"F_fbc{i}", [128, 258], F32)
             for i in range(2)]
    F_sbc = mem.at(FX + 31360, "F_sbc", [128, 256], F32)
    F_fw = [mem.at(FX + 32384 + i * 1056, f"F_fw{i}", [1, 258], F32)
            for i in range(2)]
    F_s2seg = mem.at(FX + 34496, "F_s2seg", [1, 256], F32)
    F_ev = [mem.at(FX + 35520 + i * 2048, f"F_ev{i}", [128, 512], F32)
            for i in range(2)]
    F_lnt = mem.at(FX + 39616, "F_lnt", [128, 512], F32)
    F_D = [[mem.at(FX + 41664 + (k * 2 + pr) * 256, f"F_D{k}_{pr}",
                   [128, 128], BF16) for pr in range(2)] for k in range(3)]
    assert FX + 43200 <= mem.cap, FX + 43200

    import contextlib
    pstack = contextlib.ExitStack()
    B = [pstack.enter_context(nc.psum_tensor(f"bank{i}", [128, 512], F32))
         for i in range(6)]
    X6 = pstack.enter_context(nc.psum_tensor("xb6", [128, 1024], BF16))
    X7 = pstack.enter_context(nc.psum_tensor("xb7", [128, 1024], BF16))

    p.op("POOL", lambda e: e.iota(out=iden_f[:], pattern=[[1, 128]], base=0,
                                  channel_multiplier=-1,
                                  allow_small_or_imprecise_dtypes=True),
         writes=["idenf"])
    p.op("DVE", lambda e: e.tensor_scalar(out=maskneg[:], in0=iden_f[:],
                                          scalar1=0.0, scalar2=NEGBIG,
                                          op0=ALU.is_lt, op1=ALU.mult),
         reads=["idenf"], writes=["maskneg"])
    p.op("DVE", lambda e: e.tensor_scalar(out=iden_f[:], in0=iden_f[:],
                                          scalar1=0.0, scalar2=None,
                                          op0=ALU.is_equal),
         reads=["idenf"], writes=["idenf"])
    p.op("DVE", lambda e: e.tensor_copy(out=iden_bf[:], in_=iden_f[:]),
         reads=["idenf"], writes=["idenb"])
    p.op("POOL", lambda e: e.memset(ones1[:], 1.0), writes=["ones1"])
    p.op("POOL", lambda e: e.memset(onescol[:], 1.0), writes=["onescol"])
    p.op("POOL", lambda e: e.memset(neghalf[:], -0.5), writes=["neghalf"])
    p.op("POOL", lambda e: e.memset(zpad[:], 0.0), writes=["zpad"])
    p.op("POOL", lambda e: e.memset(misc[:, 60:64], 0.0), writes=["msczero"])
    p.op("POOL", lambda e: e.memset(misc[:, 47:48], MAGIC), writes=["magc"])

    for b in range(NB):
        for e8 in range(ECH):
            p.op("POOL", lambda e, b=b, e8=e8: e.dma_start(
                out=s_xqT[b, e8, :, 0:8], in_=zpad[:]),
                reads=["zpad"], writes=[f"sxq_zh{b}_{e8}"], dma=True)
            p.op("POOL", lambda e, b=b, e8=e8: e.dma_start(
                out=s_xqT[b, e8, :, T + 8:T + 16], in_=zpad[:]),
                reads=["zpad"], writes=[f"sxq_zt{b}_{e8}"], dma=True)
    for b in range(NB):
        p.op("POOL", lambda e, b=b: e.dma_start(
            out=s_m1[b, 0:1].rearrange("(o a) -> o a", o=1), in_=misc[0:1, 60:61]),
            reads=["msczero"], writes=[f"sm1z{b}"], dma=True)
        p.op("POOL", lambda e, b=b: e.dma_start(
            out=s_m1[b, T + 1:T + 2].rearrange("(o a) -> o a", o=1), in_=misc[0:1, 61:62]),
            reads=["msczero"], writes=[f"sm1z2{b}"], dma=True)

    def load_col_params(d_ap, n, dst, key, bankslice, ident_n):
        p.op("SP", lambda e: e.dma_start(
            out=A_brow[0:n, :], in_=d_ap.rearrange("(a b) -> a b", b=128)),
            writes=["brow"], dma=True)
        p.op("PE", lambda e: e.transpose(out=bankslice, in_=A_brow[0:n, :],
                                         identity=iden_f[0:n, 0:n]),
             reads=["brow", "idenf"], writes=["B5"])
        p.op("DVE", lambda e: e.tensor_copy(out=dst, in_=bankslice),
             reads=["B5"], writes=[key])

    for d_ap, dst, key in ((cw0_d, w0c, "w0c"), (cw1_d, w1c, "w1c"),
                           (cw2_d, w2c, "w2c"), (cb_d, cbc, "cbc")):
        load_col_params(d_ap, 32, dst[:, 0:32], key, B[5][0:128, 0:32], 32)
    for d_ap, dst, key in ((bq_d, bqc, "bqc"), (bk_d, bkc, "bkc")):
        load_col_params(d_ap, 8, dst[:, 0:8], key, B[5][0:128, 0:8], 8)
    for wd, sd, nm in ((wq_d, s_wq, "wq"), (wk_d, s_wk, "wk"), (wv_d, s_wv, "wv")):
        for i in range(ECH):
            sl = slice(i * 128, (i + 1) * 128)
            p.op("SP", lambda e, wd=wd, sl=sl, i=i: e.dma_start(
                out=A_x32[i % 2][:], in_=wd[sl, :]),
                writes=[f"x32_{i % 2}"], dma=True)
            p.op("DVE", lambda e, i=i: e.tensor_copy(
                out=A_xbf[i % 2][:], in_=A_x32[i % 2][:]),
                reads=[f"x32_{i % 2}"], writes=[f"xbf{i % 2}"])
            p.op("POOL", lambda e, sd=sd, sl=sl, i=i: e.dma_start(
                out=sd[sl, :], in_=A_xbf[i % 2][:]),
                reads=[f"xbf{i % 2}"], writes=[f"scr_{nm}_{i}"], dma=True)

    def attn(b):
        for it in range(NT):
            sl = slice(it * 128, (it + 1) * 128)
            p.op("SP", lambda e, sl=sl, it=it: e.dma_start(
                out=A_x32[it % 2][:], in_=x_d[b, sl, :]),
                writes=[f"x32_{it % 2}"], dma=True)
            p.op("DVE", lambda e, it=it: e.tensor_copy(
                out=A_xbf[it % 2][:], in_=A_x32[it % 2][:]),
                reads=[f"x32_{it % 2}"], writes=[f"xbf{it % 2}"])
            p.op("POOL", lambda e, sl=sl, it=it: e.dma_start(
                out=s_xbf[b, sl, :], in_=A_xbf[it % 2][:]),
                reads=[f"xbf{it % 2}"], writes=[f"sxbf{it}"], dma=True)
        p.fence([f"xT{e}" for e in range(ECH)]
                + [f"WvT{e}" for e in range(ECH)] + ["BVbc", "bvrow"],
                [f"P{jt}" for jt in range(NT)])
        p.op("SP", lambda e: e.dma_start(out=A_brow2[:],
                                         in_=bv_d.rearrange("(o a) -> o a", o=1)),
             writes=["bvrow"], dma=True)
        for h in range(2):
            p.op("PE", lambda e, h=h: e.matmul(
                out=B[h][:], lhsT=ones1[:],
                rhs=A_brow2[:, h * 512:(h + 1) * 512], start=True, stop=True),
                reads=["bvrow", "ones1"], writes=[f"B{h}"])
            p.op("DVE", lambda e, h=h: e.tensor_copy(
                out=A_BVbc[:, h * 512:(h + 1) * 512], in_=B[h][:]),
                reads=[f"B{h}"], writes=["BVbc"])
        for e8 in range(ECH):
            p.op("SP", lambda e, e8=e8: e.dma_start(
                out=A_xT[e8][:], in_=s_xbf[b, :, e8 * 128:(e8 + 1) * 128],
                transpose=True),
                reads=[f"sxbf{it}" for it in range(NT)],
                writes=[f"xT{e8}"], dma=True, xbar=True)
        wlist = [(s_wv, A_WvT, "WvT", "wv")]
        if b == 0:
            wlist += [(s_wq, A_WqT, "WqT", "wq"), (s_wk, A_WkT, "WkT", "wk")]
        for sd, dstl, nm, srck in wlist:
            for e8 in range(ECH):
                p.op("SP", lambda e, sd=sd, dstl=dstl, e8=e8: e.dma_start(
                    out=dstl[e8][:], in_=sd[:, e8 * 128:(e8 + 1) * 128],
                    transpose=True),
                    reads=[f"scr_{srck}_{i}" for i in range(ECH)],
                    writes=[f"{nm}{e8}"], dma=True, xbar=True)

        for it in range(NT):
            g = it % 2
            bk0, bk1 = 2 * g, 2 * g + 1
            for h, bk in ((0, bk0), (1, bk1)):
                for e8 in range(ECH):
                    p.op("PE", lambda e, it=it, h=h, bk=bk, e8=e8: e.matmul(
                        out=B[bk][:],
                        lhsT=A_xT[e8][:, it * 128:(it + 1) * 128],
                        rhs=A_WvT[e8][:, h * 512:(h + 1) * 512],
                        start=(e8 == 0), stop=(e8 == ECH - 1)),
                        reads=[f"xT{e8}", f"WvT{e8}"], writes=[f"B{bk}"])
                p.op("DVE", lambda e, it=it, h=h, bk=bk: e.tensor_tensor(
                    out=A_v[it][:, h * 512:(h + 1) * 512], in0=B[bk][:],
                    in1=A_BVbc[:, h * 512:(h + 1) * 512], op=ALU.add),
                    reads=[f"B{bk}", "BVbc"], writes=[f"v{it}"])

        icw = min(512, T)
        for dstl, wTl, bc, nm in ((A_qT, A_WqT, bqc, "q"), (A_kT, A_WkT, bkc, "k")):
            wkey = "WqT" if nm == "q" else "WkT"
            for fc in range(ECH):
                for ic in range(T // icw):
                    bk = (fc * (T // icw) + ic) % 4
                    for e8 in range(ECH):
                        p.op("PE", lambda e, fc=fc, ic=ic, bk=bk, e8=e8,
                             wTl=wTl: e.matmul(
                            out=B[bk][:, 0:icw],
                            lhsT=wTl[e8][:, fc * 128:(fc + 1) * 128],
                            rhs=A_xT[e8][:, ic * icw:(ic + 1) * icw],
                            start=(e8 == 0), stop=(e8 == ECH - 1)),
                            reads=[f"xT{e8}", f"{wkey}{e8}"], writes=[f"B{bk}"])
                    p.op("ACT", lambda e, fc=fc, ic=ic, bk=bk, dstl=dstl,
                         bc=bc: e.activation(
                        out=dstl[fc][:, ic * icw:(ic + 1) * icw],
                        in_=B[bk][:, 0:icw],
                        func=AF.Identity, bias=bc[:, fc:fc + 1], scale=1.0),
                        reads=[f"B{bk}", "b" + nm + "c"],
                        writes=[f"{nm}T{fc}"])

        p.fence([f"P{jt}" for jt in range(NT)],
                [f"xT{e}" for e in range(ECH)] + [f"WvT{e}" for e in range(ECH)]
                + ["BVbc", "BVbc_use", "brow", "bvrow"])
        gchunk = 0
        for jt in range(NT):
            c0 = jt * 128
            bounds = [c0]
            nxt = (c0 // 512 + 1) * 512
            while nxt < T:
                bounds.append(nxt)
                nxt += 512
            bounds.append(T)
            chunks = [(bounds[i], bounds[i + 1] - bounds[i])
                      for i in range(len(bounds) - 1)]
            nch = len(chunks)
            banks = []
            for ci in range(nch):
                banks.append(gchunk % 6)
                gchunk += 1
            for e8 in range(ECH):
                for ci, (cs, w) in enumerate(chunks):
                    bk = banks[ci]
                    p.op("PE", lambda e, e8=e8, jt=jt, cs=cs, w=w, bk=bk: e.matmul(
                        out=B[bk][:, 0:w],
                        lhsT=A_kT[e8][:, jt * 128:(jt + 1) * 128],
                        rhs=A_qT[e8][:, cs:cs + w],
                        start=(e8 == 0), stop=(e8 == ECH - 1)),
                        reads=[f"kT{e8}", f"qT{e8}"], writes=[f"B{bk}"])
            bk0 = banks[0]
            p.op("DVE", lambda e, bk0=bk0: e.tensor_tensor(
                out=B[bk0][:, 0:128], in0=B[bk0][:, 0:128], in1=maskneg[:],
                op=ALU.add),
                reads=[f"B{bk0}", "maskneg"], writes=[f"B{bk0}"])
            for ci, (cs, w) in enumerate(chunks):
                bk = banks[ci]
                p.op("DVE", lambda e, bk=bk, w=w, ci=ci: e.tensor_reduce(
                    out=misc[:, CMAX0 + ci:CMAX0 + ci + 1], in_=B[bk][:, 0:w],
                    axis=AX, op=ALU.max),
                    reads=[f"B{bk}"], writes=[f"cmax{ci}"])
            p.op("DVE", lambda e, nch=nch: e.tensor_reduce(
                out=mc(MRAW), in_=misc[:, CMAX0:CMAX0 + nch], axis=AX,
                op=ALU.max),
                reads=[f"cmax{ci}" for ci in range(nch)], writes=["Mraw"])
            p.op("DVE", lambda e: e.tensor_scalar(
                out=mc(BIASV), in0=mc(MRAW), scalar1=-1.0 / 32.0, scalar2=None,
                op0=ALU.mult),
                reads=["Mraw"], writes=["biasv"])
            for ci, (cs, w) in enumerate(chunks):
                bk = banks[ci]
                p.op("ACT", lambda e, bk=bk, w=w, jt=jt, cs=cs, c0=c0,
                     ci=ci: e.activation(
                    out=A_P[jt][:, cs - c0:cs - c0 + w], in_=B[bk][:, 0:w],
                    func=AF.Exp, bias=mc(BIASV), scale=1.0 / 32.0,
                    accum_out=misc[:, CSUM0 + ci:CSUM0 + ci + 1]),
                    reads=[f"B{bk}", "biasv"],
                    writes=[f"P{jt}", f"csum{ci}", f"B{bk}_free"])
            p.op("DVE", lambda e, nch=nch: e.tensor_reduce(
                out=mc(COLSUM), in_=misc[:, CSUM0:CSUM0 + nch], axis=AX,
                op=ALU.add),
                reads=[f"csum{ci}" for ci in range(nch)], writes=["colsum"])
            p.op("DVE", lambda e: e.reciprocal(out=mc(RCOL), in_=mc(COLSUM)),
                 reads=["colsum"], writes=["rcol"])
            p.op("DVE", lambda e, jt=jt: e.tensor_scalar(
                out=A_v[jt][:], in0=A_v[jt][:], scalar1=mc(RCOL), scalar2=None,
                op0=ALU.mult),
                reads=["rcol", f"v{jt}"], writes=[f"v{jt}"])

        p.fence(["xres", "xln0", "xln1", "xqstg", "xqbf", "pvx0", "pvx1"],
                [f"qT{e}" for e in range(ECH)])
        pv_pend = []

        def pv_xpose_flush():
            pit = pv_pend.pop(0)
            for e8 in range(ECH):
                xb = X6 if e8 % 2 == 0 else X7
                xk = "X6" if e8 % 2 == 0 else "X7"
                p.op("PE", lambda e, e8=e8, xb=xb: e.transpose(
                    out=xb[:, 0:128], in_=PV_xqbf[:, e8 * 128:(e8 + 1) * 128],
                    identity=iden_bf[:]),
                    reads=["xqbf", "idenb"], writes=[xk])
                p.op("ACT", lambda e, e8=e8, xb=xb: e.copy(
                    out=PV_xqstg[:, e8 * 128:(e8 + 1) * 128], in_=xb[:, 0:128]),
                    reads=[xk], writes=["xqstg"])
            p.op("POOL", lambda e, pit=pit: e.dma_start(
                out=s_xqT[b, :, :, 8 + pit * 128:8 + (pit + 1) * 128]
                .rearrange("e p t -> p e t"),
                in_=PV_xqstg[:].rearrange("p (e t) -> p e t", e=ECH)),
                reads=["xqstg"], writes=[f"sxq{b}_{pit}"], dma=True)

        for it in range(NT):
            sl = slice(it * 128, (it + 1) * 128)
            g = it % 2
            p.op("SP", lambda e, sl=sl, g=g: e.dma_start(
                out=PV_xsrc[g][:], in_=x_d[b, sl, :]),
                writes=[f"pvx{g}"], dma=True)
            for h in range(2):
                bk = 2 * g + h
                for jt in range(it + 1):
                    p.op("PE", lambda e, it=it, jt=jt, h=h, bk=bk: e.matmul(
                        out=B[bk][:],
                        lhsT=A_P[jt][:, it * 128 - jt * 128:
                                      it * 128 - jt * 128 + 128],
                        rhs=A_v[jt][:, h * 512:(h + 1) * 512],
                        start=(jt == 0), stop=(jt == it)),
                        reads=[f"P{jt}", f"v{jt}"], writes=[f"B{bk}"])
                p.op("DVE", lambda e, h=h, bk=bk, g=g: e.tensor_tensor(
                    out=PV_xres[:, h * 512:(h + 1) * 512], in0=B[bk][:],
                    in1=PV_xsrc[g][:, h * 512:(h + 1) * 512], op=ALU.add),
                    reads=[f"B{bk}", f"pvx{g}"], writes=["xres"])
            if pv_pend:
                pv_xpose_flush()
            p.op("DVE", lambda e: e.tensor_reduce(
                out=mc(MS), in_=PV_xres[:], axis=AX, op=ALU.add),
                reads=["xres"], writes=["ms"])
            p.op("DVE", lambda e: e.tensor_scalar(
                out=mc(NEGM), in0=mc(MS), scalar1=-1.0 / E, scalar2=None,
                op0=ALU.mult), reads=["ms"], writes=["negm"])
            for h, vs in ((0, VS1), (1, VS2)):
                bk = 4 + h
                p.op("ACT", lambda e, h=h, vs=vs, bk=bk: e.activation(
                    out=B[bk][:], in_=PV_xres[:, h * 512:(h + 1) * 512],
                    func=AF.Square, bias=mc(NEGM), scale=1.0,
                    accum_out=misc[:, vs:vs + 1]),
                    reads=["xres", "negm"], writes=[f"B{bk}", f"vs{h}"])
            p.op("DVE", lambda e: e.tensor_tensor(
                out=mc(VT), in0=mc(VS1), in1=mc(VS2), op=ALU.add),
                reads=["vs0", "vs1"], writes=["vt"])
            p.op("DVE", lambda e: e.tensor_scalar(
                out=mc(VT), in0=mc(VT), scalar1=1.0 / E, scalar2=LNEPS,
                op0=ALU.mult, op1=ALU.add), reads=["vt"], writes=["vt"])
            p.op("POOL", lambda e: e.tensor_tensor(
                out=mc(RSTD), in0=mc(VT), in1=neghalf[:], op=ALU.pow),
                reads=["vt", "neghalf"], writes=["rstd"])
            p.op("DVE", lambda e: e.tensor_scalar(
                out=mc(NEGMR), in0=mc(NEGM), scalar1=mc(RSTD), scalar2=None,
                op0=ALU.mult), reads=["negm", "rstd"], writes=["negmr"])
            p.op("ACT", lambda e, g=g: e.activation(
                out=PV_xln[g][:], in_=PV_xres[:], func=AF.Identity,
                scale=mc(RSTD), bias=mc(NEGMR)),
                reads=["xres", "rstd", "negmr"], writes=[f"xln{g}"])
            p.op("POOL", lambda e, sl=sl, g=g: e.dma_start(
                out=s_xln[b, sl, :], in_=PV_xln[g][:]),
                reads=[f"xln{g}"], writes=[f"sxln{b}_{it}"], dma=True)
            p.op("DVE", lambda e, g=g: e.tensor_reduce(
                out=mc(M1), in_=PV_xln[g][:], axis=AX, op=ALU.max,
                apply_absolute_value=True),
                reads=[f"xln{g}"], writes=["m1"])
            p.op("DVE", lambda e: e.tensor_scalar(
                out=mc(M1C), in0=mc(M1), scalar1=QCLIP, scalar2=None,
                op0=ALU.max), reads=["m1"], writes=["m1c"])
            p.op("DVE", lambda e, it=it: e.tensor_copy(
                out=m1ct[:, it:it + 1], in_=mc(M1C)),
                reads=["m1c"], writes=["m1ct"])
            p.op("DVE", lambda e: e.reciprocal(out=mc(R1), in_=mc(M1C)),
                 reads=["m1c"], writes=["r1"])
            p.op("DVE", lambda e: e.tensor_scalar(
                out=mc(S1V), in0=mc(R1), scalar1=127.0, scalar2=None,
                op0=ALU.mult), reads=["r1"], writes=["s1v"])
            p.op("DVE", lambda e, g=g: e.tensor_scalar(
                out=PV_xres[:], in0=PV_xln[g][:], scalar1=mc(S1V),
                scalar2=MAGIC, op0=ALU.mult, op1=ALU.add),
                reads=[f"xln{g}", "s1v"], writes=["xres"])
            p.op("DVE", lambda e: e.tensor_scalar(
                out=PV_xqbf[:], in0=PV_xres[:], scalar1=MAGIC, scalar2=None,
                op0=ALU.subtract), reads=["xres"], writes=["xqbf"])
            pv_pend.append(it)
        while pv_pend:
            pv_xpose_flush()
        p.op("PE", lambda e: e.transpose(
            out=B[5][0:NT, 0:128], in_=m1ct[:], identity=iden_f[:]),
            reads=["m1ct", "idenf"], writes=["B5"])
        p.op("DVE", lambda e: e.tensor_copy(out=mrow[:], in_=B[5][0:NT, 0:128]),
             reads=["B5"], writes=["mrow"])
        p.op("POOL", lambda e: e.dma_start(
            out=s_m1[b, 1:T + 1].rearrange("(i p) -> i p", p=128),
            in_=mrow[:]),
            reads=["mrow"], writes=[f"sm1_{b}"], dma=True)

    RWB1, RWB2 = 44, 45

    def prep():
        p.fence([f"pin{i}" for i in range(4)] + ["pob0", "pob1"]
                + [f"pap{u}" for u in range(32)],
                [f"P{jt}" for jt in range(NT)] + [f"v{t}" for t in range(NT)]
                + [f"qT{e}" for e in range(ECH)] + [f"kT{e}" for e in range(ECH)]
                + [f"WqT{e}" for e in range(ECH)] + [f"WkT{e}" for e in range(ECH)]
                + [f"WvT{e}" for e in range(ECH)] + [f"xT{e}" for e in range(ECH)]
                + ["xres", "xln0", "xln1", "xqstg", "xqbf", "pvx0", "pvx1",
                   "BVbc", "bvrow", "brow", "x32_0", "x32_1", "xbf0", "xbf1"])
        tiles = []
        for i in range(16):
            tiles.append(("w1", i))
        for i in range(16):
            tiles.append(("w2", i))

        def t_ap(which, i):
            if which == "w1":
                src_ap = w1_d[i * 256:(i + 1) * 256, :].rearrange(
                    "(p a) e -> p (a e)", p=128)
                dst_ap = s_w1i[i * 256:(i + 1) * 256, :].rearrange(
                    "(p a) e -> p (a e)", p=128)
                return src_ap, dst_ap, f"sw1ib{i}"
            sl = slice((i // 2) * 128, (i // 2 + 1) * 128)
            hc = i % 2
            return (w2_d[sl, hc * 2048:(hc + 1) * 2048],
                    s_w2i[sl, hc * 2048:(hc + 1) * 2048], f"sw2ib{i}")

        def load(u, which, i):
            src_ap, _, _ = t_ap(which, i)
            eng = "SP" if u % 2 == 0 else "ACT"
            p.op(eng, lambda e, u=u, src_ap=src_ap: e.dma_start(
                out=PR_in[u % 4][:], in_=src_ap),
                writes=[f"pin{u % 4}"], dma=True)

        def mean_chain(m):
            base = 16 * m
            p.op("DVE", lambda e, base=base: e.tensor_reduce(
                out=mc(MS), in_=PR_ap[:, base:base + 16], axis=AX, op=ALU.add),
                reads=[f"pap{base + u}" for u in range(16)], writes=["ms"])
            p.op("PE", lambda e: e.matmul(out=B[5][0:1, 0:1], lhsT=mc(MS),
                                          rhs=onescol[:], start=True, stop=True),
                 reads=["ms", "onescol"], writes=["B5"])
            p.op("DVE", lambda e, m=m: e.tensor_scalar(
                out=mw[:, 2 * m:2 * m + 1], in0=B[5][0:1, 0:1],
                scalar1=1.0 / (H * E), scalar2=QCLIP, op0=ALU.mult, op1=ALU.max),
                reads=["B5"], writes=[f"mw{m}"])
            p.op("DVE", lambda e, m=m: e.reciprocal(
                out=mw[:, 2 * m + 1:2 * m + 2], in_=mw[:, 2 * m:2 * m + 1]),
                reads=[f"mw{m}"], writes=[f"rw{m}"])
            p.op("PE", lambda e, m=m: e.matmul(
                out=B[5][:, 0:2], lhsT=ones1[:], rhs=mw[:, 2 * m:2 * m + 2],
                start=True, stop=True),
                reads=[f"mw{m}", f"rw{m}", "ones1"], writes=["B5"])
            p.op("DVE", lambda e, m=m: e.tensor_copy(
                out=mwb[:, m:m + 1], in_=B[5][:, 0:1]),
                reads=["B5"], writes=[f"mwb{m}"])
            p.op("DVE", lambda e, m=m: e.tensor_copy(
                out=misc[:, RWB1 + m:RWB1 + m + 1], in_=B[5][:, 1:2]),
                reads=["B5"], writes=[f"rwb{m + 1}"])

        for u, (which, i) in enumerate(tiles):
            load(u, which, i)
            if u % 2 == 0:
                p.op("DVE", lambda e, u=u: e.tensor_reduce(
                    out=PR_ap[:, u:u + 1], in_=PR_in[u % 4][:], axis=AX,
                    op=ALU.add, apply_absolute_value=True),
                    reads=[f"pin{u % 4}"], writes=[f"pap{u}"])
            else:
                p.op("ACT", lambda e, u=u: e.activation(
                    out=PR_in[u % 4][:], in_=PR_in[u % 4][:], func=AF.Abs,
                    accum_out=PR_ap[:, u:u + 1]),
                    reads=[f"pin{u % 4}"], writes=[f"pin{u % 4}", f"pap{u}"])
            if u == 15:
                mean_chain(0)
            elif u == 31:
                mean_chain(1)
        for u, (which, i) in enumerate(tiles):
            m = 0 if which == "w1" else 1
            _, dst_ap, skey = t_ap(which, i)
            load(32 + u, which, i)
            up = u % 4
            ob = u % 2
            p.op("ACT", lambda e, up=up, m=m: e.activation(
                out=PR_in[up][:], in_=PR_in[up][:], func=AF.Identity,
                scale=misc[:, RWB1 + m:RWB1 + m + 1], bias=misc[:, 47:48]),
                reads=[f"pin{up}", f"rwb{m + 1}", "magc"], writes=[f"pin{up}"])
            p.op("DVE", lambda e, up=up: e.tensor_scalar(
                out=PR_in[up][:], in0=PR_in[up][:], scalar1=MAGIC, scalar2=1.0,
                op0=ALU.subtract, op1=ALU.min),
                reads=[f"pin{up}"], writes=[f"pin{up}"])
            p.op("DVE", lambda e, up=up, ob=ob: e.tensor_scalar(
                out=PR_ob[ob][:], in0=PR_in[up][:], scalar1=-1.0, scalar2=None,
                op0=ALU.max),
                reads=[f"pin{up}"], writes=[f"pob{ob}"])
            p.op("POOL", lambda e, ob=ob, dst_ap=dst_ap: e.dma_start(
                out=dst_ap, in_=PR_ob[ob][:]),
                reads=[f"pob{ob}"], writes=[skey], dma=True)

    NGC = NB * NCHUNK

    def ffn_entry():
        newk = ([f"W1qT{e}" for e in range(ECH)]
                + [f"W2qT{c}" for c in range(HCH)]
                + [f"Fg{pr}_{s}" for pr in range(2) for s in range(HCH)]
                + [f"F_xq{pr}_{e}" for pr in range(2) for e in range(ECH)]
                + [f"fxlnt{pr}_{t}" for pr in range(2) for t in range(2)]
                + [f"fw{pr}" for pr in range(2)] + [f"fbc{pr}" for pr in range(2)]
                + ["sbc", "fs2seg", "Mx", "Mn", "flnt"]
                + [f"hs{i}" for i in range(2)] + [f"u{i}" for i in range(2)]
                + [f"fev{i}" for i in range(2)]
                + [f"D{k}_{pr}" for k in range(3) for pr in range(2)]
                + ["fcm0", "fcm1", "fcn0", "fcn1", "fm2", "fr2", "fs2v", "frs2"])
        oldk = ([f"P{jt}" for jt in range(NT)] + [f"v{t}" for t in range(NT)]
                + [f"qT{e}" for e in range(ECH)] + [f"kT{e}" for e in range(ECH)]
                + [f"xT{e}" for e in range(ECH)] + [f"WqT{e}" for e in range(ECH)]
                + [f"WkT{e}" for e in range(ECH)] + [f"WvT{e}" for e in range(ECH)]
                + ["xres", "xln0", "xln1", "xqstg", "xqbf", "pvx0", "pvx1",
                   "BVbc", "bvrow", "brow", "x32_0", "x32_1", "xbf0", "xbf1"]
                + [f"pin{i}" for i in range(4)] + [f"pap{u}" for u in range(32)]
                + ["pob0", "pob1", "m1ct", "mrow"]
                + [f"cmax{i}" for i in range(4)] + [f"csum{i}" for i in range(4)]
                + ["colsum", "msczero", "Mraw"])
        p.fence(newk, oldk)
        for e8 in range(ECH):
            p.op("SP", lambda e, e8=e8: e.dma_start(
                out=F_W1qT[e8][:], in_=s_w1i[:, e8 * 128:(e8 + 1) * 128],
                transpose=True),
                reads=[f"sw1ib{i}" for i in range(16)],
                writes=[f"W1qT{e8}"], dma=True, xbar=True)
        for c in range(HCH):
            p.op("SP", lambda e, c=c: e.dma_start(
                out=F_W2qT[c][:], in_=s_w2i[:, c * 128:(c + 1) * 128],
                transpose=True),
                reads=[f"sw2ib{i}" for i in range(16)],
                writes=[f"W2qT{c}"], dma=True, xbar=True)

    def ffn_loads(gc):
        b, ct = divmod(gc, NCHUNK)
        t0 = ct * 256
        par = gc % 2
        for e8 in range(ECH):
            p.op("SP", lambda e, e8=e8, b=b, t0=t0, par=par: e.dma_start(
                out=F_xq[par][e8][:],
                in_=s_xqT[b, e8, :, 8 + t0 - 1:8 + t0 + 257]),
                reads=[f"sxq{b}_{it}" for it in range(max(0, t0 // 128 - 1),
                                                      min(NT, t0 // 128 + 3))]
                + [f"sxq_zh{b}_{e8}", f"sxq_zt{b}_{e8}"],
                writes=[f"F_xq{par}_{e8}"], dma=True)
        p.op("SP", lambda e, b=b, t0=t0, par=par: e.dma_start(
            out=F_fw[par][:],
            in_=s_m1[b, t0:t0 + 258].rearrange("(o a) -> o a", o=1)),
            reads=[f"sm1_{b}", f"sm1z{b}", f"sm1z2{b}"],
            writes=[f"fw{par}"], dma=True)

    def ffn_loads_xlnt(gc):
        b, ct = divmod(gc, NCHUNK)
        t0 = ct * 256
        par = gc % 2
        for tb in range(2):
            sl = slice(t0 + tb * 128, t0 + (tb + 1) * 128)
            p.op("SP", lambda e, sl=sl, tb=tb, b=b, par=par: e.dma_start(
                out=F_xlnt[par][tb][:], in_=s_xln[b, sl, :]),
                reads=[f"sxln{b}_{sl.start // 128}"],
                writes=[f"fxlnt{par}_{tb}"], dma=True)

    def ffn_fbc(gc):
        par = gc % 2
        p.op("DVE", lambda e, par=par: e.tensor_scalar(
            out=F_fw[par][:], in0=F_fw[par][:], scalar1=mw[0:1, 0:1],
            scalar2=1.0 / 127.0, op0=ALU.mult, op1=ALU.mult),
            reads=[f"fw{par}", "mw0"], writes=[f"fw{par}"])
        p.op("PE", lambda e, par=par: e.matmul(
            out=B[3][:, 0:258], lhsT=ones1[:], rhs=F_fw[par][:],
            start=True, stop=True),
            reads=[f"fw{par}", "ones1"], writes=["B3s"])
        p.op("DVE", lambda e, par=par: e.tensor_copy(
            out=F_fbc[par][:], in_=B[3][:, 0:258]),
            reads=["B3s"], writes=[f"fbc{par}"])

    def ffn_w1(gc, s):
        par = gc % 2
        bk = s % 2
        for e8 in range(ECH):
            p.op("PE", lambda e, s=s, e8=e8, bk=bk, par=par: e.matmul(
                out=B[bk][:, 0:258],
                lhsT=F_W1qT[e8][:, s * 128:(s + 1) * 128],
                rhs=F_xq[par][e8][:],
                start=(e8 == 0), stop=(e8 == ECH - 1)),
                reads=[f"W1qT{e8}", f"F_xq{par}_{e8}"], writes=[f"B{bk}"])
        p.op("DVE", lambda e, s=s, bk=bk, par=par: e.tensor_tensor(
            out=F_hs[s % 2][:], in0=B[bk][:, 0:258], in1=F_fbc[par][:],
            op=ALU.mult),
            reads=[f"B{bk}", f"fbc{par}"], writes=[f"hs{s % 2}"])

    def ffn_dbuild(s):
        for k, wc, wk in ((0, w0c, "w0c"), (1, w1c, "w1c"), (2, w2c, "w2c")):
            p.op("ACT", lambda e, k=k, s=s, wc=wc: e.activation(
                out=F_D[k][s % 2][:], in_=iden_f[:], func=AF.Identity,
                scale=wc[:, s:s + 1]),
                reads=["idenf", wk], writes=[f"D{k}_{s % 2}"])

    def ffn_conv(gc, s):
        par = gc % 2
        for k in range(3):
            p.op("PE", lambda e, k=k, s=s: e.matmul(
                out=B[2][:, 0:256], lhsT=F_D[k][s % 2][:],
                rhs=F_hs[s % 2][:, k:k + 256],
                start=(k == 0), stop=(k == 2)),
                reads=[f"D{k}_{s % 2}", f"hs{s % 2}"], writes=["B2c"])
        p.op("ACT", lambda e, s=s, par=par: e.activation(
            out=F_g[par][s][:], in_=B[2][:, 0:256], func=AF.Silu,
            bias=cbc[:, s:s + 1], scale=1.0),
            reads=["B2c", "cbc"], writes=[f"Fg{par}_{s}"])
        if s == 0:
            p.op("DVE", lambda e, par=par: e.tensor_copy(
                out=F_Mx[:], in_=F_g[par][0][:]),
                reads=[f"Fg{par}_0"], writes=["Mx"])
            p.op("DVE", lambda e, par=par: e.tensor_copy(
                out=F_Mn[:], in_=F_g[par][0][:]),
                reads=[f"Fg{par}_0"], writes=["Mn"])
        else:
            p.op("DVE", lambda e, s=s, par=par: e.tensor_tensor(
                out=F_Mx[:], in0=F_Mx[:], in1=F_g[par][s][:], op=ALU.max),
                reads=["Mx", f"Fg{par}_{s}"], writes=["Mx"])
            p.op("DVE", lambda e, s=s, par=par: e.tensor_tensor(
                out=F_Mn[:], in0=F_Mn[:], in1=F_g[par][s][:], op=ALU.min),
                reads=["Mn", f"Fg{par}_{s}"], writes=["Mn"])

    def ffn_scales_a(pgc):
        for i, key in ((0, "Mx"), (1, "Mn")):
            for tb in range(2):
                src = F_Mx if i == 0 else F_Mn
                p.op("PE", lambda e, i=i, tb=tb, src=src, key=key: e.transpose(
                    out=X6[:, (i * 2 + tb) * 128:(i * 2 + tb + 1) * 128],
                    in_=src[:, tb * 128:(tb + 1) * 128], identity=iden_bf[:]),
                    reads=[key, "idenb"], writes=["X6"])
        for tb in range(2):
            p.op("DVE", lambda e, tb=tb: e.tensor_reduce(
                out=misc[:, 16 + tb:17 + tb],
                in_=X6[:, tb * 128:(tb + 1) * 128], axis=AX, op=ALU.max),
                reads=["X6"], writes=[f"fcm{tb}"])
            p.op("DVE", lambda e, tb=tb: e.tensor_reduce(
                out=misc[:, 18 + tb:19 + tb],
                in_=X6[:, (2 + tb) * 128:(3 + tb) * 128], axis=AX, op=ALU.min,
                negate=True),
                reads=["X6"], writes=[f"fcn{tb}"])
        p.op("DVE", lambda e: e.tensor_tensor(
            out=misc[:, 20:22], in0=misc[:, 16:18], in1=misc[:, 18:20],
            op=ALU.max),
            reads=["fcm0", "fcm1", "fcn0", "fcn1"], writes=["fm2"])
        p.op("DVE", lambda e: e.tensor_scalar(
            out=misc[:, 20:22], in0=misc[:, 20:22], scalar1=QCLIP, scalar2=None,
            op0=ALU.max), reads=["fm2"], writes=["fm2"])
        p.op("DVE", lambda e: e.tensor_scalar(
            out=misc[:, 24:26], in0=misc[:, 20:22], scalar1=mwb[:, 1:2],
            scalar2=1.0 / 127.0, op0=ALU.mult, op1=ALU.mult),
            reads=["fm2", "mwb1"], writes=["frs2"])
        p.op("DVE", lambda e: e.reciprocal(out=misc[:, 22:24], in_=misc[:, 20:22]),
             reads=["fm2"], writes=["fr2"])
        p.op("DVE", lambda e: e.tensor_scalar(
            out=misc[:, 30:31], in0=misc[:, 22:23], scalar1=127.0, scalar2=None,
            op0=ALU.mult), reads=["fr2"], writes=["fs2v"])
        p.op("DVE", lambda e: e.tensor_scalar(
            out=misc[:, 62:63], in0=misc[:, 23:24], scalar1=127.0, scalar2=None,
            op0=ALU.mult), reads=["fr2"], writes=["fs2v"])

    def ffn_scales_b(pgc):
        import os as _os2
        _ksb = _os2.environ.get("KSB", "tcc")
        p.op("PE", lambda e: e.matmul(
            out=B[3][0:33, 384:512], lhsT=misc[:, 30:63], rhs=iden_f[:],
            start=True, stop=True, is_transpose=True),
            reads=["fs2v", "idenf"], writes=["B3s"])
        for half, row in ((0, 0), (1, 32))[:_ksb.count("c")]:
            p.op("DVE", lambda e, half=half, row=row: e.tensor_copy(
                out=F_s2seg[:, half * 128:(half + 1) * 128],
                in_=B[3][row:row + 1, 384:512]),
                reads=["B3s"], writes=["fs2seg"])

    def ffn_scales_c(pgc):
        p.op("PE", lambda e: e.matmul(
            out=B[3][:, 0:256], lhsT=ones1[:], rhs=F_s2seg[:],
            start=True, stop=True),
            reads=["fs2seg", "ones1"], writes=["B3s"])
        p.op("DVE", lambda e: e.tensor_copy(out=F_sbc[:], in_=B[3][:, 0:256]),
             reads=["B3s"], writes=["sbc"])

    def ffn_round(pgc, cs):
        par = pgc % 2
        up = cs % 2
        p.op("DVE", lambda e, cs=cs, par=par, up=up: e.tensor_tensor(
            out=F_u[up][:], in0=F_g[par][cs][:], in1=F_sbc[:], op=ALU.mult),
            reads=[f"Fg{par}_{cs}", "sbc"], writes=[f"u{up}"])
        p.op("DVE", lambda e, cs=cs, par=par, up=up: e.tensor_scalar(
            out=F_g[par][cs][:], in0=F_u[up][:], scalar1=MAGIC, scalar2=MAGIC,
            op0=ALU.add, op1=ALU.subtract),
            reads=[f"u{up}"], writes=[f"Fg{par}_{cs}"])

    def ffn_w2mm(pgc, grp, cs):
        par = pgc % 2
        tb, h = grp // 2, grp % 2
        bk = 4 + (grp % 2)
        p.op("PE", lambda e, par=par, tb=tb, h=h, cs=cs, bk=bk: e.matmul(
            out=B[bk][:],
            lhsT=F_g[par][cs][:, tb * 128:(tb + 1) * 128],
            rhs=F_W2qT[cs][:, h * 512:(h + 1) * 512],
            start=(cs == 0), stop=(cs == HCH - 1)),
            reads=[f"Fg{par}_{cs}", f"W2qT{cs}"], writes=[f"B{bk}"])

    def ffn_w2evac(pgc, grp):
        par = pgc % 2
        tb, h = grp // 2, grp % 2
        bk = 4 + (grp % 2)
        ev = grp % 2
        p.op("ACT", lambda e, bk=bk, ev=ev, tb=tb: e.activation(
            out=F_ev[ev][:], in_=B[bk][:], func=AF.Identity,
            scale=misc[:, 24 + tb:25 + tb]),
            reads=[f"B{bk}", "frs2"], writes=[f"fev{ev}"])
        p.op("DVE", lambda e, par=par, tb=tb, h=h, ev=ev: e.tensor_tensor(
            out=F_xlnt[par][tb][:, h * 512:(h + 1) * 512],
            in0=F_ev[ev][:], in1=F_xlnt[par][tb][:, h * 512:(h + 1) * 512],
            op=ALU.add),
            reads=[f"fev{ev}", f"fxlnt{par}_{tb}"], writes=[f"fxlnt{par}_{tb}"])

    def ffn_ln2(pgc, tb):
        b, ct = divmod(pgc, NCHUNK)
        par = pgc % 2
        t0 = ct * 256
        sl = slice(t0 + tb * 128, t0 + (tb + 1) * 128)
        xt = F_xlnt[par][tb]
        xk = f"fxlnt{par}_{tb}"
        p.op("DVE", lambda e, xt=xt: e.tensor_reduce(
            out=mc(MS), in_=xt[:], axis=AX, op=ALU.add),
            reads=[xk], writes=["ms"])
        p.op("DVE", lambda e: e.tensor_scalar(
            out=mc(NEGM), in0=mc(MS), scalar1=-1.0 / E, scalar2=None,
            op0=ALU.mult), reads=["ms"], writes=["negm"])
        for h, vs in ((0, VS1), (1, VS2)):
            p.op("ACT", lambda e, xt=xt, h=h, vs=vs: e.activation(
                out=F_lnt[:], in_=xt[:, h * 512:(h + 1) * 512],
                func=AF.Square, bias=mc(NEGM), scale=1.0,
                accum_out=misc[:, vs:vs + 1]),
                reads=[xk, "negm"], writes=["flnt", f"vs{h}"])
        p.op("DVE", lambda e: e.tensor_tensor(
            out=mc(VT), in0=mc(VS1), in1=mc(VS2), op=ALU.add),
            reads=["vs0", "vs1"], writes=["vt"])
        p.op("DVE", lambda e: e.tensor_scalar(
            out=mc(VT), in0=mc(VT), scalar1=1.0 / E, scalar2=LNEPS,
            op0=ALU.mult, op1=ALU.add), reads=["vt"], writes=["vt"])
        p.op("POOL", lambda e: e.tensor_tensor(
            out=mc(RSTD), in0=mc(VT), in1=neghalf[:], op=ALU.pow),
            reads=["vt", "neghalf"], writes=["rstd"])
        p.op("DVE", lambda e: e.tensor_scalar(
            out=mc(NEGMR), in0=mc(NEGM), scalar1=mc(RSTD), scalar2=None,
            op0=ALU.mult), reads=["negm", "rstd"], writes=["negmr"])
        p.op("ACT", lambda e, xt=xt: e.activation(
            out=xt[:], in_=xt[:], func=AF.Identity,
            scale=mc(RSTD), bias=mc(NEGMR)),
            reads=[xk, "rstd", "negmr"], writes=[xk])
        st = p.op("POOL", lambda e, sl=sl, xt=xt, b=b: e.dma_start(
            out=out_d[b, sl, :], in_=xt[:]),
            reads=[xk], writes=[f"outst{b}_{sl.start}"], dma=True)
        out_stores.append(st)

    def ffn_chunk(gc):
        if gc + 1 < NGC:
            ffn_loads(gc + 1)
        ffn_fbc(gc)
        ffn_dbuild(0)
        have_prev = gc > 0
        pgc = gc - 1
        import os as _os
        _kd = _os.environ.get("KDEF", "full")
        _lv = {"none": 0, "scales": 1, "round": 2, "w2": 3, "evac": 4,
               "full": 9}[_kd]
        W2S = {0: 12, 1: 14, 2: 20, 3: 22}
        for s in range(HCH):
            ffn_w1(gc, s)
            if s >= 1:
                ffn_conv(gc, s - 1)
            if s + 1 < HCH:
                ffn_dbuild(s + 1)
            if have_prev:
                if _lv >= 1:
                    _ksc = _os.environ.get("KSC", "abc")
                    if s == 0 and "a" in _ksc:
                        ffn_scales_a(pgc)
                    elif s == 2 and "b" in _ksc:
                        ffn_scales_b(pgc)
                    elif s == 3 and "c" in _ksc:
                        ffn_scales_c(pgc)
                if _lv >= 2 and 4 <= s <= 11:
                    for cs in range(4 * (s - 4), 4 * (s - 3)):
                        ffn_round(pgc, cs)
                if _lv >= 4:
                    if s == 20:
                        ffn_w2evac(pgc, 0)
                    elif s == 22:
                        ffn_w2evac(pgc, 1)
                    elif s == 28:
                        ffn_w2evac(pgc, 2)
                    elif s == 30:
                        ffn_w2evac(pgc, 3)
                if _lv >= 5:
                    if s == 23:
                        ffn_ln2(pgc, 0)
                    elif s == 31:
                        ffn_ln2(pgc, 1)
                if s == 31 and gc + 1 < NGC:
                    ffn_loads_xlnt(gc + 1)
                if _lv >= 3:
                    for grp, s0 in W2S.items():
                        if s0 <= s < s0 + 8:
                            for cs in range(4 * (s - s0), 4 * (s - s0 + 1)):
                                ffn_w2mm(pgc, grp, cs)
        if not have_prev and gc + 1 < NGC:
            ffn_loads_xlnt(gc + 1)
        ffn_conv(gc, HCH - 1)

    def ffn_tail():
        pgc = NGC - 1
        ffn_scales_a(pgc)
        ffn_scales_b(pgc)
        ffn_scales_c(pgc)
        for cs in range(HCH):
            ffn_round(pgc, cs)
        for grp in range(4):
            for cs in range(HCH):
                ffn_w2mm(pgc, grp, cs)
            ffn_w2evac(pgc, grp)
            if grp == 1:
                ffn_ln2(pgc, 0)
        ffn_ln2(pgc, 1)

    out_stores = []
    import os as _os
    _cut = _os.environ.get("KCUT", "full")
    for b in range(NB):
        attn(b)
    if _cut != "attn":
        if _cut != "noprep":
            prep()
        if _cut not in ("prep", "noprep"):
            ffn_entry()
            ffn_loads(0)
            ffn_loads_xlnt(0)
            if _cut != "entry":
                nch = NGC if _cut == "full" else min(NGC, int(_cut[1:]))
                for gc in range(nch):
                    ffn_chunk(gc)
                if _cut == "full":
                    ffn_tail()
    p.op("POOL", lambda e: e.nop(), deps=out_stores)

    import contextlib as _ctx
    with _ctx.ExitStack() as st:
        block = st.enter_context(nc.Block())
        sems = {e: st.enter_context(nc.semaphore("sem" + e)) for e in Prog.ENGINES}
        p.build(nc, block, sems)
    pstack.close()
    return nc


def kernel(L0_attn_X, L0_attn_Wq, L0_attn_bq, L0_attn_Wk, L0_attn_bk,
           L0_attn_Wv, L0_attn_bv, L0_ffn_w1, L0_ffn_conv_w, L0_ffn_conv_b,
           L0_ffn_w2):
    X = np.ascontiguousarray(np.asarray(L0_attn_X, dtype=np.float32))
    cw = np.asarray(L0_ffn_conv_w, dtype=np.float32).reshape(4096, 3)
    common = {
        "wq": np.ascontiguousarray(np.asarray(L0_attn_Wq, np.float32)),
        "wk": np.ascontiguousarray(np.asarray(L0_attn_Wk, np.float32)),
        "wv": np.ascontiguousarray(np.asarray(L0_attn_Wv, np.float32)),
        "bq": np.ascontiguousarray(np.asarray(L0_attn_bq, np.float32)),
        "bk": np.ascontiguousarray(np.asarray(L0_attn_bk, np.float32)),
        "bv": np.ascontiguousarray(np.asarray(L0_attn_bv, np.float32)),
        "w1": np.ascontiguousarray(np.asarray(L0_ffn_w1, np.float32)),
        "w2": np.ascontiguousarray(np.asarray(L0_ffn_w2, np.float32)),
        "cw0": np.ascontiguousarray(cw[:, 0]),
        "cw1": np.ascontiguousarray(cw[:, 1]),
        "cw2": np.ascontiguousarray(cw[:, 2]),
        "cb": np.ascontiguousarray(np.asarray(L0_ffn_conv_b, np.float32)),
    }
    n_cores = 8
    nb = X.shape[0] // n_cores
    nc = build_program(T=X.shape[1], NB=nb)
    in_maps = []
    for c in range(n_cores):
        m = dict(common)
        m["x"] = np.ascontiguousarray(X[c * nb:(c + 1) * nb])
        in_maps.append(m)
    res = run_bass_kernel_spmd(nc, in_maps, core_ids=list(range(n_cores)))
    out = np.concatenate([np.asarray(r["out"]) for r in res.results], axis=0)
    return out.astype(np.float32)


# version 12
# speedup vs baseline: 1.2026x; 1.0450x over previous
import numpy as np
import ml_dtypes
import concourse.bass as bass
import concourse.mybir as mybir
from concourse.bass_utils import run_bass_kernel_spmd

F32 = mybir.dt.float32
BF16 = mybir.dt.bfloat16
U8 = mybir.dt.uint8
AF = mybir.ActivationFunctionType
ALU = mybir.AluOpType
AX = mybir.AxisListType.X

MAGIC = 12582912.0
NEGBIG = -1e30
LNEPS = 1e-5
QCLIP = 1e-5
E = 1024
H = 4096
ECH = 8
HCH = 32
SBUF_CAP = 212_256


class Prog:

    ENGINES = ("SP", "PE", "ACT", "DVE", "POOL", "PD", "AD")

    def __init__(self):
        self.ops = []
        self.res = {}
        self.seed = {}
        self.last_dma = {}
        self.last_xbar = None
        import os
        self.serial = set(os.environ.get("KSYNC", "").split(",")) - {""}

    def _touch(self, k, depset):
        if k in self.seed:
            depset.update(self.seed.pop(k))
        r = self.res.get(k)
        if r is None:
            r = self.res[k] = {"w": None, "r": []}
        return r

    def op(self, engine, emit, reads=(), writes=(), deps=(), dma=False,
           xbar=False):
        idx = len(self.ops)
        depset = set(deps)
        for k in reads:
            r = self._touch(k, depset)
            if r["w"] is not None:
                depset.add(r["w"])
            r["r"].append(idx)
        for k in writes:
            r = self._touch(k, depset)
            if r["w"] is not None:
                depset.add(r["w"])
            depset.update(r["r"])
            r["w"] = idx
            r["r"] = []
        if dma:
            if engine in self.last_dma:
                depset.add(self.last_dma[engine])
            if self.last_xbar is not None:
                depset.add(self.last_xbar)
            if xbar:
                for q, d in self.last_dma.items():
                    depset.add(d)
                self.last_xbar = idx
            self.last_dma[engine] = idx
        depset.discard(idx)
        if engine == "PE":
            depset = {d for d in depset if self.ops[d][0] != "PE"}
        if idx > 0 and ("all" in self.serial or engine in self.serial):
            depset.add(idx - 1)
        self.ops.append((engine, emit, dma, depset))
        return idx

    def fence(self, new_keys, old_keys):
        dep = set()
        for k in old_keys:
            r = self.res.get(k)
            if r is not None:
                if r["w"] is not None:
                    dep.add(r["w"])
                dep.update(r["r"])
        for k in new_keys:
            self.seed[k] = set(self.seed.get(k, ())) | dep

    def build(self, nc, block, sems):
        needed = set()
        for _, _, _, deps in self.ops:
            needed.update(deps)
        incs = []
        cums = []
        cum = {e: 0 for e in self.ENGINES}
        streams = []
        for i, (eng, _, dma, _) in enumerate(self.ops):
            inc = 16 if dma else (1 if i in needed else 0)
            incs.append(inc)
            stream = ("PD" if eng == "POOL" else
                      "AD" if eng == "ACT" else eng) if dma else eng
            streams.append(stream)
            cum[stream] += inc
            cums.append(cum[stream])
        per = {e: [] for e in self.ENGINES}
        for i, (eng, emit, dma, deps) in enumerate(self.ops):
            per[eng].append((i, emit, incs[i], deps))
        engobj = {"SP": nc.sync, "PE": nc.tensor, "ACT": nc.scalar,
                  "DVE": nc.vector, "POOL": nc.gpsimd}
        openg = streams

        def make_section(ename):
            ops = per[ename]
            eo = engobj[ename]

            def section(_):
                waited = {f: 0 for f in self.ENGINES}
                for i, emit, inc, deps in ops:
                    want = {}
                    for d in deps:
                        f = openg[d]
                        v = cums[d]
                        if v > want.get(f, 0):
                            want[f] = v
                    for f, v in want.items():
                        if waited[f] < v:
                            eo.wait_ge(sems[f], v)
                            waited[f] = v
                    inst = emit(eo)
                    if inc:
                        inst.then_inc(sems[streams[i]], inc)

            return section

        for ename, sec in (("SP", block.sync), ("PE", block.tensor),
                           ("ACT", block.scalar), ("DVE", block.vector),
                           ("POOL", block.gpsimd)):
            if per[ename]:
                sec(make_section(ename))


class Mem:

    def __init__(self, nc, cap):
        self.nc = nc
        self.base = (nc.sbuf_base + 31) // 32 * 32
        self.cap = min(cap, nc.sbuf_top - self.base)

    def allocate(self):
        pass

    def at(self, off, name, shape, dtype):
        per = int(np.prod(shape[1:])) * mybir.dt.size(dtype)
        assert off % 32 == 0 and off + per <= self.cap, (name, off, per)
        return self.nc.alloc_sbuf_tensor_at(name, list(shape), dtype,
                                            offset=self.base + off)


def _ceil(a, m):
    return (a + m - 1) // m * m


def build_program(T=2048, NB=2, debug=False):
    nc = bass.Bass("TRN2", target_bir_lowering=False)
    NT = T // 128
    NCHUNK = T // 256
    p = Prog()

    dram = {}

    def din(name, shape, dtype=F32):
        dram[name] = nc.dram_tensor(name, shape, dtype, kind="ExternalInput").ap()
        return dram[name]

    def dout(name, shape, dtype=F32):
        dram[name] = nc.dram_tensor(name, shape, dtype, kind="ExternalOutput").ap()
        return dram[name]

    def dscr(name, shape, dtype):
        kind = "ExternalOutput" if debug else "Internal"
        dram[name] = nc.dram_tensor(name, shape, dtype, kind=kind).ap()
        return dram[name]

    x_d = din("x", [NB, T, E])
    wq_d = din("wq", [E, E]); wk_d = din("wk", [E, E]); wv_d = din("wv", [E, E])
    bq_d = din("bq", [E]); bk_d = din("bk", [E]); bv_d = din("bv", [E])
    w1_d = din("w1", [H, E]); w2_d = din("w2", [E, H])
    cw0_d = din("cw0", [H]); cw1_d = din("cw1", [H]); cw2_d = din("cw2", [H])
    cb_d = din("cb", [H])
    out_d = dout("out", [NB, T, E])

    s_xbf = dscr("s_xbf", [NB, T, E], BF16)
    s_xqT = dscr("s_xqT", [NB, ECH, 128, T + 16], BF16)
    s_xln = dscr("s_xln", [NB, T, E], F32)
    s_m1 = dscr("s_m1", [NB, T + 2], F32)
    s_wq = dscr("s_wq", [E, E], BF16)
    s_wk = dscr("s_wk", [E, E], BF16)
    s_wv = dscr("s_wv", [E, E], BF16)
    s_w1i = dscr("s_w1i", [H, E], BF16)
    s_w2i = dscr("s_w2i", [E, H], BF16)
    if debug:
        d_s2seg = dscr("d_s2seg", [1, 256], F32)
        d_sbc = dscr("d_sbc", [128, 256], F32)
        d_fbc = dscr("d_fbc", [128, 258], F32)
        d_g = dscr("d_g", [HCH, 128, 256], F32)
        d_hq = dscr("d_hq", [HCH, 128, 256], BF16)
        d_mx = dscr("d_mx", [128, 256], F32)
        d_mn = dscr("d_mn", [128, 256], F32)

    mem = Mem(nc, SBUF_CAP)
    mem.allocate()

    off = 0
    def P_(name, shape, dtype):
        nonlocal off
        h = mem.at(off, name, shape, dtype)
        per = int(np.prod(shape[1:])) * mybir.dt.size(dtype)
        off = _ceil(off + per, 32)
        return h

    iden_bf = P_("iden_bf", [128, 128], BF16)
    iden_f = P_("iden_f", [128, 128], F32)
    maskneg = P_("maskneg", [128, 128], F32)
    ones1 = P_("ones1", [1, 128], F32)
    onescol = P_("onescol", [128, 1], F32)
    neghalf = P_("neghalf", [128, 1], F32)
    w0c = P_("w0c", [128, 32], F32)
    w1c = P_("w1c", [128, 32], F32)
    w2c = P_("w2c", [128, 32], F32)
    cbc = P_("cbc", [128, 32], F32)
    bqc = P_("bqc", [128, 8], F32)
    bkc = P_("bkc", [128, 8], F32)
    misc = P_("misc", [128, 64], F32)
    m1ct = P_("m1ct", [128, NT], F32)
    mrow = P_("mrow", [NT, 128], F32)
    mw = P_("mw", [1, 8], F32)
    mwb = P_("mwb", [128, 2], F32)
    zpad = P_("zpad", [128, 8], BF16)
    PERS_END = off
    OB = _ceil(PERS_END, 32)
    AVAIL = SBUF_CAP - OB

    MS, NEGM, VS1, VS2, VT, RSTD, NEGMR, M1, M1C, R1, S1V, BIASV, RCOL = range(13)
    MRAW = 14
    CMAX0 = 16
    CSUM0 = 20
    COLSUM = 24
    MX0 = 28
    RWB = 44

    def mc(i):
        return misc[:, i:i + 1]

    A_P = [mem.at(OB + jt * 4096, f"A_P{jt}", [128, T], BF16) for jt in range(NT)]
    A_xT = [mem.at(OB + e * 4096, f"A_xT{e}", [128, T], BF16) for e in range(ECH)]
    A_WvT = [mem.at(OB + 32768 + e * 2048, f"A_WvT{e}", [128, E], BF16)
             for e in range(ECH)]
    A_BVbc = mem.at(OB + 49152, "A_BVbc", [128, E], F32)
    A_brow = mem.at(OB + 53248, "A_brow", [32, 128], F32)
    A_brow2 = mem.at(OB + 53760, "A_brow2", [1, E], F32)
    A2 = OB + 65536
    A_qT = [mem.at(A2 + e * 4096, f"A_qT{e}", [128, T], BF16) for e in range(ECH)]
    A_kT = [mem.at(A2 + 32768 + e * 4096, f"A_kT{e}", [128, T], BF16)
            for e in range(ECH)]
    A_v = [mem.at(A2 + 65536 + t * 2048, f"A_v{t}", [128, E], BF16)
           for t in range(NT)]
    A_WqT = [mem.at(A2 + 98304 + e * 2048, f"A_WqT{e}", [128, E], BF16)
             for e in range(ECH)]
    A_WkT = [mem.at(A2 + 114688 + e * 2048, f"A_WkT{e}", [128, E], BF16)
             for e in range(ECH)]
    AT = A2 + 131072
    A_x32 = [mem.at(AT + i * 4096, f"A_x32_{i}", [128, E], F32) for i in range(2)]
    A_xbf = [mem.at(AT + 8192 + i * 2048, f"A_xbf{i}", [128, E], BF16)
             for i in range(2)]
    assert AT + 12288 <= mem.cap
    PV_xres = mem.at(A2 + 0, "PV_xres", [128, E], F32)
    PV_xln = [mem.at(A2 + 4096 + i * 4096, f"PV_xln{i}", [128, E], F32)
              for i in range(2)]
    PV_xqstg = mem.at(A2 + 12288, "PV_xqstg", [128, E], BF16)
    PV_xqbf = mem.at(A2 + 14336, "PV_xqbf", [128, E], BF16)
    PV_xsrc = [mem.at(A2 + 16384 + i * 4096, f"PV_xsrc{i}", [128, E], F32)
               for i in range(2)]

    PR_in = [mem.at(OB + i * 8192, f"PR_in{i}", [128, 2048], F32)
             for i in range(4)]
    PR_ob = [mem.at(OB + 32768 + i * 4096, f"PR_ob{i}", [128, 2048], BF16)
             for i in range(2)]
    PR_ap = mem.at(OB + 40960, "PR_ap", [128, 32], F32)

    F_W1qT = [mem.at(OB + e * 8192, f"F_W1qT{e}", [128, H], BF16)
              for e in range(ECH)]
    F_W2qT = [mem.at(OB + 65536 + c * 2048, f"F_W2qT{c}", [128, E], BF16)
              for c in range(HCH)]
    F_g = [[mem.at(OB + 131072 + pr * 16384 + s * 512, f"Fg{pr}_{s}",
                   [128, 256], BF16) for s in range(HCH)] for pr in range(2)]
    FX = OB + 163840
    F_xq = [[mem.at(FX + pr * 4352 + e * 544, f"F_xq{pr}_{e}", [128, 258], BF16)
             for e in range(ECH)] for pr in range(2)]
    F_xlnt = [[mem.at(FX + 8704 + (pr * 2 + i) * 4096, f"F_xlnt{pr}_{i}",
                      [128, E], F32) for i in range(2)] for pr in range(2)]
    F_hs = [mem.at(FX + 25088 + i * 544, f"F_hs{i}", [128, 258], BF16)
            for i in range(2)]
    F_u = [mem.at(FX + 26176 + i * 1024, f"F_u{i}", [128, 256], F32)
           for i in range(2)]
    F_Mx = mem.at(FX + 28224, "F_Mx", [128, 256], BF16)
    F_Mn = mem.at(FX + 28736, "F_Mn", [128, 256], BF16)
    F_fbc = [mem.at(FX + 29248 + i * 1056, f"F_fbc{i}", [128, 258], F32)
             for i in range(2)]
    F_sbc = mem.at(FX + 31360, "F_sbc", [128, 256], F32)
    F_fw = [mem.at(FX + 32384 + i * 1056, f"F_fw{i}", [1, 258], F32)
            for i in range(2)]
    F_s2seg = mem.at(FX + 34496, "F_s2seg", [1, 256], F32)
    F_ev = [mem.at(FX + 35520 + i * 2048, f"F_ev{i}", [128, 512], F32)
            for i in range(2)]
    F_lnt = mem.at(FX + 39616, "F_lnt", [128, 512], F32)
    F_D = [[mem.at(FX + 41664 + (k * 2 + pr) * 256, f"F_D{k}_{pr}",
                   [128, 128], BF16) for pr in range(2)] for k in range(3)]
    assert FX + 43200 <= mem.cap, FX + 43200

    import contextlib
    pstack = contextlib.ExitStack()
    B = [pstack.enter_context(nc.psum_tensor(f"bank{i}", [128, 512], F32))
         for i in range(6)]
    X6 = pstack.enter_context(nc.psum_tensor("xb6", [128, 1024], BF16))
    X7 = pstack.enter_context(nc.psum_tensor("xb7", [128, 1024], BF16))

    p.op("POOL", lambda e: e.iota(out=iden_f[:], pattern=[[1, 128]], base=0,
                                  channel_multiplier=-1,
                                  allow_small_or_imprecise_dtypes=True),
         writes=["idenf"])
    p.op("DVE", lambda e: e.tensor_scalar(out=maskneg[:], in0=iden_f[:],
                                          scalar1=0.0, scalar2=NEGBIG,
                                          op0=ALU.is_lt, op1=ALU.mult),
         reads=["idenf"], writes=["maskneg"])
    p.op("DVE", lambda e: e.tensor_scalar(out=iden_f[:], in0=iden_f[:],
                                          scalar1=0.0, scalar2=None,
                                          op0=ALU.is_equal),
         reads=["idenf"], writes=["idenf"])
    p.op("DVE", lambda e: e.tensor_copy(out=iden_bf[:], in_=iden_f[:]),
         reads=["idenf"], writes=["idenb"])
    p.op("POOL", lambda e: e.memset(ones1[:], 1.0), writes=["ones1"])
    p.op("POOL", lambda e: e.memset(onescol[:], 1.0), writes=["onescol"])
    p.op("POOL", lambda e: e.memset(neghalf[:], -0.5), writes=["neghalf"])
    p.op("POOL", lambda e: e.memset(zpad[:], 0.0), writes=["zpad"])
    p.op("POOL", lambda e: e.memset(misc[:, 60:64], 0.0), writes=["msczero"])
    p.op("POOL", lambda e: e.memset(misc[:, 47:48], MAGIC), writes=["magc"])

    for b in range(NB):
        for e8 in range(ECH):
            p.op("POOL", lambda e, b=b, e8=e8: e.dma_start(
                out=s_xqT[b, e8, :, 0:8], in_=zpad[:]),
                reads=["zpad"], writes=[f"sxq_zh{b}_{e8}"], dma=True)
            p.op("POOL", lambda e, b=b, e8=e8: e.dma_start(
                out=s_xqT[b, e8, :, T + 8:T + 16], in_=zpad[:]),
                reads=["zpad"], writes=[f"sxq_zt{b}_{e8}"], dma=True)
    for b in range(NB):
        p.op("POOL", lambda e, b=b: e.dma_start(
            out=s_m1[b, 0:1].rearrange("(o a) -> o a", o=1), in_=misc[0:1, 60:61]),
            reads=["msczero"], writes=[f"sm1z{b}"], dma=True)
        p.op("POOL", lambda e, b=b: e.dma_start(
            out=s_m1[b, T + 1:T + 2].rearrange("(o a) -> o a", o=1), in_=misc[0:1, 61:62]),
            reads=["msczero"], writes=[f"sm1z2{b}"], dma=True)

    def load_col_params(d_ap, n, dst, key, bankslice, ident_n):
        p.op("SP", lambda e: e.dma_start(
            out=A_brow[0:n, :], in_=d_ap.rearrange("(a b) -> a b", b=128)),
            writes=["brow"], dma=True)
        p.op("PE", lambda e: e.transpose(out=bankslice, in_=A_brow[0:n, :],
                                         identity=iden_f[0:n, 0:n]),
             reads=["brow", "idenf"], writes=["B5"])
        p.op("DVE", lambda e: e.tensor_copy(out=dst, in_=bankslice),
             reads=["B5"], writes=[key])

    for d_ap, dst, key in ((cw0_d, w0c, "w0c"), (cw1_d, w1c, "w1c"),
                           (cw2_d, w2c, "w2c"), (cb_d, cbc, "cbc")):
        load_col_params(d_ap, 32, dst[:, 0:32], key, B[5][0:128, 0:32], 32)
    for d_ap, dst, key in ((bq_d, bqc, "bqc"), (bk_d, bkc, "bkc")):
        load_col_params(d_ap, 8, dst[:, 0:8], key, B[5][0:128, 0:8], 8)
    for wd, sd, nm in ((wq_d, s_wq, "wq"), (wk_d, s_wk, "wk"), (wv_d, s_wv, "wv")):
        for i in range(ECH):
            sl = slice(i * 128, (i + 1) * 128)
            p.op("SP", lambda e, wd=wd, sl=sl, i=i: e.dma_start(
                out=A_x32[i % 2][:], in_=wd[sl, :]),
                writes=[f"x32_{i % 2}"], dma=True)
            p.op("DVE", lambda e, i=i: e.tensor_copy(
                out=A_xbf[i % 2][:], in_=A_x32[i % 2][:]),
                reads=[f"x32_{i % 2}"], writes=[f"xbf{i % 2}"])
            p.op("POOL", lambda e, sd=sd, sl=sl, i=i: e.dma_start(
                out=sd[sl, :], in_=A_xbf[i % 2][:]),
                reads=[f"xbf{i % 2}"], writes=[f"scr_{nm}_{i}"], dma=True)

    def attn(b):
        p.fence([f"xT{e}_{t}" for e in range(ECH) for t in range(NT)]
                + [f"WvT{e}" for e in range(ECH)] + ["BVbc", "bvrow",
                   "x32_0", "x32_1", "xbf0", "xbf1"],
                [f"P{jt}" for jt in range(NT)])
        for it in range(NT):
            sl = slice(it * 128, (it + 1) * 128)
            p.op("SP", lambda e, sl=sl, it=it: e.dma_start(
                out=A_x32[it % 2][:], in_=x_d[b, sl, :]),
                writes=[f"x32_{it % 2}"], dma=True)
            p.op("DVE", lambda e, it=it: e.tensor_copy(
                out=A_xbf[it % 2][:], in_=A_x32[it % 2][:]),
                reads=[f"x32_{it % 2}"], writes=[f"xbf{it % 2}"])
            for e8 in range(ECH):
                xb = X6 if e8 % 2 == 0 else X7
                xk = "X6" if e8 % 2 == 0 else "X7"
                p.op("PE", lambda e, it=it, e8=e8, xb=xb: e.transpose(
                    out=xb[:, 0:128],
                    in_=A_xbf[it % 2][:, e8 * 128:(e8 + 1) * 128],
                    identity=iden_bf[:]),
                    reads=[f"xbf{it % 2}", "idenb"], writes=[xk])
                p.op("ACT", lambda e, it=it, e8=e8, xb=xb: e.copy(
                    out=A_xT[e8][:, it * 128:(it + 1) * 128], in_=xb[:, 0:128]),
                    reads=[xk], writes=[f"xT{e8}_{it}"])
        p.op("SP", lambda e: e.dma_start(out=A_brow2[:],
                                         in_=bv_d.rearrange("(o a) -> o a", o=1)),
             writes=["bvrow"], dma=True)
        for h in range(2):
            p.op("PE", lambda e, h=h: e.matmul(
                out=B[h][:], lhsT=ones1[:],
                rhs=A_brow2[:, h * 512:(h + 1) * 512], start=True, stop=True),
                reads=["bvrow", "ones1"], writes=[f"B{h}"])
            p.op("DVE", lambda e, h=h: e.tensor_copy(
                out=A_BVbc[:, h * 512:(h + 1) * 512], in_=B[h][:]),
                reads=[f"B{h}"], writes=["BVbc"])
        wlist = [(s_wv, A_WvT, "WvT", "wv")]
        if b == 0:
            wlist += [(s_wq, A_WqT, "WqT", "wq"), (s_wk, A_WkT, "WkT", "wk")]
        for sd, dstl, nm, srck in wlist:
            for e8 in range(ECH):
                p.op("SP", lambda e, sd=sd, dstl=dstl, e8=e8: e.dma_start(
                    out=dstl[e8][:], in_=sd[:, e8 * 128:(e8 + 1) * 128],
                    transpose=True),
                    reads=[f"scr_{srck}_{i}" for i in range(ECH)],
                    writes=[f"{nm}{e8}"], dma=True, xbar=True)

        for it in range(NT):
            g = it % 2
            bk0, bk1 = 2 * g, 2 * g + 1
            for h, bk in ((0, bk0), (1, bk1)):
                for e8 in range(ECH):
                    p.op("PE", lambda e, it=it, h=h, bk=bk, e8=e8: e.matmul(
                        out=B[bk][:],
                        lhsT=A_xT[e8][:, it * 128:(it + 1) * 128],
                        rhs=A_WvT[e8][:, h * 512:(h + 1) * 512],
                        start=(e8 == 0), stop=(e8 == ECH - 1)),
                        reads=[f"xT{e8}_{it}", f"WvT{e8}"], writes=[f"B{bk}"])
                p.op("DVE", lambda e, it=it, h=h, bk=bk: e.tensor_tensor(
                    out=A_v[it][:, h * 512:(h + 1) * 512], in0=B[bk][:],
                    in1=A_BVbc[:, h * 512:(h + 1) * 512], op=ALU.add),
                    reads=[f"B{bk}", "BVbc"], writes=[f"v{it}"])

        icw = min(512, T)
        for dstl, wTl, bc, nm in ((A_qT, A_WqT, bqc, "q"), (A_kT, A_WkT, bkc, "k")):
            wkey = "WqT" if nm == "q" else "WkT"
            for fc in range(ECH):
                for ic in range(T // icw):
                    bk = (fc * (T // icw) + ic) % 4
                    for e8 in range(ECH):
                        p.op("PE", lambda e, fc=fc, ic=ic, bk=bk, e8=e8,
                             wTl=wTl: e.matmul(
                            out=B[bk][:, 0:icw],
                            lhsT=wTl[e8][:, fc * 128:(fc + 1) * 128],
                            rhs=A_xT[e8][:, ic * icw:(ic + 1) * icw],
                            start=(e8 == 0), stop=(e8 == ECH - 1)),
                            reads=[f"xT{e8}_{ic * icw // 128 + t}"
                                   for t in range(icw // 128)]
                            + [f"{wkey}{e8}"], writes=[f"B{bk}"])
                    p.op("ACT", lambda e, fc=fc, ic=ic, bk=bk, dstl=dstl,
                         bc=bc: e.activation(
                        out=dstl[fc][:, ic * icw:(ic + 1) * icw],
                        in_=B[bk][:, 0:icw],
                        func=AF.Identity, bias=bc[:, fc:fc + 1], scale=1.0),
                        reads=[f"B{bk}", "b" + nm + "c"],
                        writes=[f"{nm}T{fc}"])

        p.fence([f"P{jt}" for jt in range(NT)],
                [f"xT{e}_{t}" for e in range(ECH) for t in range(NT)]
                + [f"WvT{e}" for e in range(ECH)]
                + ["BVbc", "BVbc_use", "brow", "bvrow"])
        gchunk = 0
        for jt in range(NT):
            c0 = jt * 128
            bounds = [c0]
            nxt = (c0 // 512 + 1) * 512
            while nxt < T:
                bounds.append(nxt)
                nxt += 512
            bounds.append(T)
            chunks = [(bounds[i], bounds[i + 1] - bounds[i])
                      for i in range(len(bounds) - 1)]
            nch = len(chunks)
            banks = []
            for ci in range(nch):
                banks.append(gchunk % 6)
                gchunk += 1
            for e8 in range(ECH):
                for ci, (cs, w) in enumerate(chunks):
                    bk = banks[ci]
                    p.op("PE", lambda e, e8=e8, jt=jt, cs=cs, w=w, bk=bk: e.matmul(
                        out=B[bk][:, 0:w],
                        lhsT=A_kT[e8][:, jt * 128:(jt + 1) * 128],
                        rhs=A_qT[e8][:, cs:cs + w],
                        start=(e8 == 0), stop=(e8 == ECH - 1)),
                        reads=[f"kT{e8}", f"qT{e8}"], writes=[f"B{bk}"])
            bk0 = banks[0]
            p.op("DVE", lambda e, bk0=bk0: e.tensor_tensor(
                out=B[bk0][:, 0:128], in0=B[bk0][:, 0:128], in1=maskneg[:],
                op=ALU.add),
                reads=[f"B{bk0}", "maskneg"], writes=[f"B{bk0}"])
            for ci, (cs, w) in enumerate(chunks):
                bk = banks[ci]
                p.op("DVE", lambda e, bk=bk, w=w, ci=ci: e.tensor_reduce(
                    out=misc[:, CMAX0 + ci:CMAX0 + ci + 1], in_=B[bk][:, 0:w],
                    axis=AX, op=ALU.max),
                    reads=[f"B{bk}"], writes=[f"cmax{ci}"])
            p.op("DVE", lambda e, nch=nch: e.tensor_reduce(
                out=mc(MRAW), in_=misc[:, CMAX0:CMAX0 + nch], axis=AX,
                op=ALU.max),
                reads=[f"cmax{ci}" for ci in range(nch)], writes=["Mraw"])
            p.op("DVE", lambda e: e.tensor_scalar(
                out=mc(BIASV), in0=mc(MRAW), scalar1=-1.0 / 32.0, scalar2=None,
                op0=ALU.mult),
                reads=["Mraw"], writes=["biasv"])
            for ci, (cs, w) in enumerate(chunks):
                bk = banks[ci]
                p.op("ACT", lambda e, bk=bk, w=w, jt=jt, cs=cs, c0=c0,
                     ci=ci: e.activation(
                    out=A_P[jt][:, cs - c0:cs - c0 + w], in_=B[bk][:, 0:w],
                    func=AF.Exp, bias=mc(BIASV), scale=1.0 / 32.0,
                    accum_out=misc[:, CSUM0 + ci:CSUM0 + ci + 1]),
                    reads=[f"B{bk}", "biasv"],
                    writes=[f"P{jt}", f"csum{ci}", f"B{bk}_free"])
            p.op("DVE", lambda e, nch=nch: e.tensor_reduce(
                out=mc(COLSUM), in_=misc[:, CSUM0:CSUM0 + nch], axis=AX,
                op=ALU.add),
                reads=[f"csum{ci}" for ci in range(nch)], writes=["colsum"])
            p.op("DVE", lambda e: e.reciprocal(out=mc(RCOL), in_=mc(COLSUM)),
                 reads=["colsum"], writes=["rcol"])
            p.op("DVE", lambda e, jt=jt: e.tensor_scalar(
                out=A_v[jt][:], in0=A_v[jt][:], scalar1=mc(RCOL), scalar2=None,
                op0=ALU.mult),
                reads=["rcol", f"v{jt}"], writes=[f"v{jt}"])

        p.fence(["xres", "xln0", "xln1", "xqstg", "xqbf", "pvx0", "pvx1"],
                [f"qT{e}" for e in range(ECH)])
        pv_pend = []

        def pv_xpose_flush():
            pit = pv_pend.pop(0)
            for e8 in range(ECH):
                xb = X6 if e8 % 2 == 0 else X7
                xk = "X6" if e8 % 2 == 0 else "X7"
                p.op("PE", lambda e, e8=e8, xb=xb: e.transpose(
                    out=xb[:, 0:128], in_=PV_xqbf[:, e8 * 128:(e8 + 1) * 128],
                    identity=iden_bf[:]),
                    reads=["xqbf", "idenb"], writes=[xk])
                p.op("ACT", lambda e, e8=e8, xb=xb: e.copy(
                    out=PV_xqstg[:, e8 * 128:(e8 + 1) * 128], in_=xb[:, 0:128]),
                    reads=[xk], writes=["xqstg"])
            p.op("POOL", lambda e, pit=pit: e.dma_start(
                out=s_xqT[b, :, :, 8 + pit * 128:8 + (pit + 1) * 128]
                .rearrange("e p t -> p e t"),
                in_=PV_xqstg[:].rearrange("p (e t) -> p e t", e=ECH)),
                reads=["xqstg"], writes=[f"sxq{b}_{pit}"], dma=True)

        for it in range(NT):
            sl = slice(it * 128, (it + 1) * 128)
            g = it % 2
            p.op("SP", lambda e, sl=sl, g=g: e.dma_start(
                out=PV_xsrc[g][:], in_=x_d[b, sl, :]),
                writes=[f"pvx{g}"], dma=True)
            for h in range(2):
                bk = 2 * g + h
                for jt in range(it + 1):
                    p.op("PE", lambda e, it=it, jt=jt, h=h, bk=bk: e.matmul(
                        out=B[bk][:],
                        lhsT=A_P[jt][:, it * 128 - jt * 128:
                                      it * 128 - jt * 128 + 128],
                        rhs=A_v[jt][:, h * 512:(h + 1) * 512],
                        start=(jt == 0), stop=(jt == it)),
                        reads=[f"P{jt}", f"v{jt}"], writes=[f"B{bk}"])
                p.op("DVE", lambda e, h=h, bk=bk, g=g: e.tensor_tensor(
                    out=PV_xres[:, h * 512:(h + 1) * 512], in0=B[bk][:],
                    in1=PV_xsrc[g][:, h * 512:(h + 1) * 512], op=ALU.add),
                    reads=[f"B{bk}", f"pvx{g}"], writes=["xres"])
            if pv_pend:
                pv_xpose_flush()
            p.op("DVE", lambda e: e.tensor_reduce(
                out=mc(MS), in_=PV_xres[:], axis=AX, op=ALU.add),
                reads=["xres"], writes=["ms"])
            p.op("DVE", lambda e: e.tensor_scalar(
                out=mc(NEGM), in0=mc(MS), scalar1=-1.0 / E, scalar2=None,
                op0=ALU.mult), reads=["ms"], writes=["negm"])
            for h, vs in ((0, VS1), (1, VS2)):
                bk = 4 + h
                p.op("ACT", lambda e, h=h, vs=vs, bk=bk: e.activation(
                    out=B[bk][:], in_=PV_xres[:, h * 512:(h + 1) * 512],
                    func=AF.Square, bias=mc(NEGM), scale=1.0,
                    accum_out=misc[:, vs:vs + 1]),
                    reads=["xres", "negm"], writes=[f"B{bk}", f"vs{h}"])
            p.op("DVE", lambda e: e.tensor_tensor(
                out=mc(VT), in0=mc(VS1), in1=mc(VS2), op=ALU.add),
                reads=["vs0", "vs1"], writes=["vt"])
            p.op("DVE", lambda e: e.tensor_scalar(
                out=mc(VT), in0=mc(VT), scalar1=1.0 / E, scalar2=LNEPS,
                op0=ALU.mult, op1=ALU.add), reads=["vt"], writes=["vt"])
            p.op("POOL", lambda e: e.tensor_tensor(
                out=mc(RSTD), in0=mc(VT), in1=neghalf[:], op=ALU.pow),
                reads=["vt", "neghalf"], writes=["rstd"])
            p.op("DVE", lambda e: e.tensor_scalar(
                out=mc(NEGMR), in0=mc(NEGM), scalar1=mc(RSTD), scalar2=None,
                op0=ALU.mult), reads=["negm", "rstd"], writes=["negmr"])
            p.op("ACT", lambda e, g=g: e.activation(
                out=PV_xln[g][:], in_=PV_xres[:], func=AF.Identity,
                scale=mc(RSTD), bias=mc(NEGMR)),
                reads=["xres", "rstd", "negmr"], writes=[f"xln{g}"])
            p.op("POOL", lambda e, sl=sl, g=g: e.dma_start(
                out=s_xln[b, sl, :], in_=PV_xln[g][:]),
                reads=[f"xln{g}"], writes=[f"sxln{b}_{it}"], dma=True)
            p.op("DVE", lambda e, g=g: e.tensor_reduce(
                out=mc(M1), in_=PV_xln[g][:], axis=AX, op=ALU.max,
                apply_absolute_value=True),
                reads=[f"xln{g}"], writes=["m1"])
            p.op("DVE", lambda e: e.tensor_scalar(
                out=mc(M1C), in0=mc(M1), scalar1=QCLIP, scalar2=None,
                op0=ALU.max), reads=["m1"], writes=["m1c"])
            p.op("DVE", lambda e, it=it: e.tensor_copy(
                out=m1ct[:, it:it + 1], in_=mc(M1C)),
                reads=["m1c"], writes=["m1ct"])
            p.op("DVE", lambda e: e.reciprocal(out=mc(R1), in_=mc(M1C)),
                 reads=["m1c"], writes=["r1"])
            p.op("DVE", lambda e: e.tensor_scalar(
                out=mc(S1V), in0=mc(R1), scalar1=127.0, scalar2=None,
                op0=ALU.mult), reads=["r1"], writes=["s1v"])
            p.op("DVE", lambda e, g=g: e.tensor_scalar(
                out=PV_xres[:], in0=PV_xln[g][:], scalar1=mc(S1V),
                scalar2=MAGIC, op0=ALU.mult, op1=ALU.add),
                reads=[f"xln{g}", "s1v"], writes=["xres"])
            p.op("DVE", lambda e: e.tensor_scalar(
                out=PV_xqbf[:], in0=PV_xres[:], scalar1=MAGIC, scalar2=None,
                op0=ALU.subtract), reads=["xres"], writes=["xqbf"])
            pv_pend.append(it)
        while pv_pend:
            pv_xpose_flush()
        p.op("PE", lambda e: e.transpose(
            out=B[5][0:NT, 0:128], in_=m1ct[:], identity=iden_f[:]),
            reads=["m1ct", "idenf"], writes=["B5"])
        p.op("DVE", lambda e: e.tensor_copy(out=mrow[:], in_=B[5][0:NT, 0:128]),
             reads=["B5"], writes=["mrow"])
        p.op("POOL", lambda e: e.dma_start(
            out=s_m1[b, 1:T + 1].rearrange("(i p) -> i p", p=128),
            in_=mrow[:]),
            reads=["mrow"], writes=[f"sm1_{b}"], dma=True)

    RWB1, RWB2 = 44, 45

    def prep():
        p.fence([f"pin{i}" for i in range(4)] + ["pob0", "pob1"]
                + [f"pap{u}" for u in range(32)],
                [f"P{jt}" for jt in range(NT)] + [f"v{t}" for t in range(NT)]
                + [f"qT{e}" for e in range(ECH)] + [f"kT{e}" for e in range(ECH)]
                + [f"WqT{e}" for e in range(ECH)] + [f"WkT{e}" for e in range(ECH)]
                + [f"WvT{e}" for e in range(ECH)]
                + [f"xT{e}_{t}" for e in range(ECH) for t in range(NT)]
                + ["xres", "xln0", "xln1", "xqstg", "xqbf", "pvx0", "pvx1",
                   "BVbc", "bvrow", "brow", "x32_0", "x32_1", "xbf0", "xbf1"])
        tiles = []
        for i in range(16):
            tiles.append(("w1", i))
        for i in range(16):
            tiles.append(("w2", i))

        def t_ap(which, i):
            if which == "w1":
                src_ap = w1_d[i * 256:(i + 1) * 256, :].rearrange(
                    "(p a) e -> p (a e)", p=128)
                dst_ap = s_w1i[i * 256:(i + 1) * 256, :].rearrange(
                    "(p a) e -> p (a e)", p=128)
                return src_ap, dst_ap, f"sw1ib{i}"
            sl = slice((i // 2) * 128, (i // 2 + 1) * 128)
            hc = i % 2
            return (w2_d[sl, hc * 2048:(hc + 1) * 2048],
                    s_w2i[sl, hc * 2048:(hc + 1) * 2048], f"sw2ib{i}")

        def load(u, which, i):
            src_ap, _, _ = t_ap(which, i)
            eng = "SP" if u % 2 == 0 else "ACT"
            p.op(eng, lambda e, u=u, src_ap=src_ap: e.dma_start(
                out=PR_in[u % 4][:], in_=src_ap),
                writes=[f"pin{u % 4}"], dma=True)

        def mean_chain(m):
            base = 16 * m
            p.op("DVE", lambda e, base=base: e.tensor_reduce(
                out=mc(MS), in_=PR_ap[:, base:base + 16], axis=AX, op=ALU.add),
                reads=[f"pap{base + u}" for u in range(16)], writes=["ms"])
            p.op("PE", lambda e: e.matmul(out=B[5][0:1, 0:1], lhsT=mc(MS),
                                          rhs=onescol[:], start=True, stop=True),
                 reads=["ms", "onescol"], writes=["B5"])
            p.op("DVE", lambda e, m=m: e.tensor_scalar(
                out=mw[:, 2 * m:2 * m + 1], in0=B[5][0:1, 0:1],
                scalar1=1.0 / (H * E), scalar2=QCLIP, op0=ALU.mult, op1=ALU.max),
                reads=["B5"], writes=[f"mw{m}"])
            p.op("DVE", lambda e, m=m: e.reciprocal(
                out=mw[:, 2 * m + 1:2 * m + 2], in_=mw[:, 2 * m:2 * m + 1]),
                reads=[f"mw{m}"], writes=[f"rw{m}"])
            p.op("PE", lambda e, m=m: e.matmul(
                out=B[5][:, 0:2], lhsT=ones1[:], rhs=mw[:, 2 * m:2 * m + 2],
                start=True, stop=True),
                reads=[f"mw{m}", f"rw{m}", "ones1"], writes=["B5"])
            p.op("DVE", lambda e, m=m: e.tensor_copy(
                out=mwb[:, m:m + 1], in_=B[5][:, 0:1]),
                reads=["B5"], writes=[f"mwb{m}"])
            p.op("DVE", lambda e, m=m: e.tensor_copy(
                out=misc[:, RWB1 + m:RWB1 + m + 1], in_=B[5][:, 1:2]),
                reads=["B5"], writes=[f"rwb{m + 1}"])

        for u, (which, i) in enumerate(tiles):
            load(u, which, i)
            if u % 2 == 0:
                p.op("DVE", lambda e, u=u: e.tensor_reduce(
                    out=PR_ap[:, u:u + 1], in_=PR_in[u % 4][:], axis=AX,
                    op=ALU.add, apply_absolute_value=True),
                    reads=[f"pin{u % 4}"], writes=[f"pap{u}"])
            else:
                p.op("ACT", lambda e, u=u: e.activation(
                    out=PR_in[u % 4][:], in_=PR_in[u % 4][:], func=AF.Abs,
                    accum_out=PR_ap[:, u:u + 1]),
                    reads=[f"pin{u % 4}"], writes=[f"pin{u % 4}", f"pap{u}"])
            if u == 15:
                mean_chain(0)
            elif u == 31:
                mean_chain(1)
        for u, (which, i) in enumerate(tiles):
            m = 0 if which == "w1" else 1
            _, dst_ap, skey = t_ap(which, i)
            load(32 + u, which, i)
            up = u % 4
            ob = u % 2
            p.op("ACT", lambda e, up=up, m=m: e.activation(
                out=PR_in[up][:], in_=PR_in[up][:], func=AF.Identity,
                scale=misc[:, RWB1 + m:RWB1 + m + 1], bias=misc[:, 47:48]),
                reads=[f"pin{up}", f"rwb{m + 1}", "magc"], writes=[f"pin{up}"])
            p.op("DVE", lambda e, up=up: e.tensor_scalar(
                out=PR_in[up][:], in0=PR_in[up][:], scalar1=MAGIC, scalar2=1.0,
                op0=ALU.subtract, op1=ALU.min),
                reads=[f"pin{up}"], writes=[f"pin{up}"])
            p.op("DVE", lambda e, up=up, ob=ob: e.tensor_scalar(
                out=PR_ob[ob][:], in0=PR_in[up][:], scalar1=-1.0, scalar2=None,
                op0=ALU.max),
                reads=[f"pin{up}"], writes=[f"pob{ob}"])
            p.op("POOL", lambda e, ob=ob, dst_ap=dst_ap: e.dma_start(
                out=dst_ap, in_=PR_ob[ob][:]),
                reads=[f"pob{ob}"], writes=[skey], dma=True)

    NGC = NB * NCHUNK

    def ffn_entry():
        newk = ([f"W1qT{e}" for e in range(ECH)]
                + [f"W2qT{c}" for c in range(HCH)]
                + [f"Fg{pr}_{s}" for pr in range(2) for s in range(HCH)]
                + [f"F_xq{pr}_{e}" for pr in range(2) for e in range(ECH)]
                + [f"fxlnt{pr}_{t}" for pr in range(2) for t in range(2)]
                + [f"fw{pr}" for pr in range(2)] + [f"fbc{pr}" for pr in range(2)]
                + ["sbc", "fs2seg", "Mx", "Mn", "flnt"]
                + [f"hs{i}" for i in range(2)] + [f"u{i}" for i in range(2)]
                + [f"fev{i}" for i in range(2)]
                + [f"D{k}_{pr}" for k in range(3) for pr in range(2)]
                + ["fcm0", "fcm1", "fcn0", "fcn1", "fm2", "fr2", "fs2v", "frs2"])
        oldk = ([f"P{jt}" for jt in range(NT)] + [f"v{t}" for t in range(NT)]
                + [f"qT{e}" for e in range(ECH)] + [f"kT{e}" for e in range(ECH)]
                + [f"xT{e}_{t}" for e in range(ECH) for t in range(NT)]
                + [f"WqT{e}" for e in range(ECH)]
                + [f"WkT{e}" for e in range(ECH)] + [f"WvT{e}" for e in range(ECH)]
                + ["xres", "xln0", "xln1", "xqstg", "xqbf", "pvx0", "pvx1",
                   "BVbc", "bvrow", "brow", "x32_0", "x32_1", "xbf0", "xbf1"]
                + [f"pin{i}" for i in range(4)] + [f"pap{u}" for u in range(32)]
                + ["pob0", "pob1", "m1ct", "mrow"]
                + [f"cmax{i}" for i in range(4)] + [f"csum{i}" for i in range(4)]
                + ["colsum", "msczero", "Mraw"])
        p.fence(newk, oldk)
        for e8 in range(ECH):
            p.op("SP", lambda e, e8=e8: e.dma_start(
                out=F_W1qT[e8][:], in_=s_w1i[:, e8 * 128:(e8 + 1) * 128],
                transpose=True),
                reads=[f"sw1ib{i}" for i in range(16)],
                writes=[f"W1qT{e8}"], dma=True, xbar=True)
        for c in range(HCH):
            p.op("SP", lambda e, c=c: e.dma_start(
                out=F_W2qT[c][:], in_=s_w2i[:, c * 128:(c + 1) * 128],
                transpose=True),
                reads=[f"sw2ib{i}" for i in range(16)],
                writes=[f"W2qT{c}"], dma=True, xbar=True)

    def ffn_loads(gc):
        b, ct = divmod(gc, NCHUNK)
        t0 = ct * 256
        par = gc % 2
        for e8 in range(ECH):
            p.op("SP", lambda e, e8=e8, b=b, t0=t0, par=par: e.dma_start(
                out=F_xq[par][e8][:],
                in_=s_xqT[b, e8, :, 8 + t0 - 1:8 + t0 + 257]),
                reads=[f"sxq{b}_{it}" for it in range(max(0, t0 // 128 - 1),
                                                      min(NT, t0 // 128 + 3))]
                + [f"sxq_zh{b}_{e8}", f"sxq_zt{b}_{e8}"],
                writes=[f"F_xq{par}_{e8}"], dma=True)
        p.op("SP", lambda e, b=b, t0=t0, par=par: e.dma_start(
            out=F_fw[par][:],
            in_=s_m1[b, t0:t0 + 258].rearrange("(o a) -> o a", o=1)),
            reads=[f"sm1_{b}", f"sm1z{b}", f"sm1z2{b}"],
            writes=[f"fw{par}"], dma=True)

    def ffn_loads_xlnt(gc):
        b, ct = divmod(gc, NCHUNK)
        t0 = ct * 256
        par = gc % 2
        for tb in range(2):
            sl = slice(t0 + tb * 128, t0 + (tb + 1) * 128)
            p.op("SP", lambda e, sl=sl, tb=tb, b=b, par=par: e.dma_start(
                out=F_xlnt[par][tb][:], in_=s_xln[b, sl, :]),
                reads=[f"sxln{b}_{sl.start // 128}"],
                writes=[f"fxlnt{par}_{tb}"], dma=True)

    def ffn_fbc(gc):
        par = gc % 2
        p.op("DVE", lambda e, par=par: e.tensor_scalar(
            out=F_fw[par][:], in0=F_fw[par][:], scalar1=mw[0:1, 0:1],
            scalar2=1.0 / 127.0, op0=ALU.mult, op1=ALU.mult),
            reads=[f"fw{par}", "mw0"], writes=[f"fw{par}"])
        p.op("PE", lambda e, par=par: e.matmul(
            out=B[3][:, 0:258], lhsT=ones1[:], rhs=F_fw[par][:],
            start=True, stop=True),
            reads=[f"fw{par}", "ones1"], writes=["B3s"])
        p.op("DVE", lambda e, par=par: e.tensor_copy(
            out=F_fbc[par][:], in_=B[3][:, 0:258]),
            reads=["B3s"], writes=[f"fbc{par}"])

    def ffn_w1(gc, s):
        par = gc % 2
        bk = s % 2
        for e8 in range(ECH):
            p.op("PE", lambda e, s=s, e8=e8, bk=bk, par=par: e.matmul(
                out=B[bk][:, 0:258],
                lhsT=F_W1qT[e8][:, s * 128:(s + 1) * 128],
                rhs=F_xq[par][e8][:],
                start=(e8 == 0), stop=(e8 == ECH - 1)),
                reads=[f"W1qT{e8}", f"F_xq{par}_{e8}"], writes=[f"B{bk}"])
        p.op("DVE", lambda e, s=s, bk=bk, par=par: e.tensor_tensor(
            out=F_hs[s % 2][:], in0=B[bk][:, 0:258], in1=F_fbc[par][:],
            op=ALU.mult),
            reads=[f"B{bk}", f"fbc{par}"], writes=[f"hs{s % 2}"])

    def ffn_dbuild(s):
        for k, wc, wk in ((0, w0c, "w0c"), (1, w1c, "w1c"), (2, w2c, "w2c")):
            p.op("ACT", lambda e, k=k, s=s, wc=wc: e.activation(
                out=F_D[k][s % 2][:], in_=iden_f[:], func=AF.Identity,
                scale=wc[:, s:s + 1]),
                reads=["idenf", wk], writes=[f"D{k}_{s % 2}"])

    def ffn_conv(gc, s):
        par = gc % 2
        for k in range(3):
            p.op("PE", lambda e, k=k, s=s: e.matmul(
                out=B[2][:, 0:256], lhsT=F_D[k][s % 2][:],
                rhs=F_hs[s % 2][:, k:k + 256],
                start=(k == 0), stop=(k == 2)),
                reads=[f"D{k}_{s % 2}", f"hs{s % 2}"], writes=["B2c"])
        p.op("ACT", lambda e, s=s, par=par: e.activation(
            out=F_g[par][s][:], in_=B[2][:, 0:256], func=AF.Silu,
            bias=cbc[:, s:s + 1], scale=1.0),
            reads=["B2c", "cbc"], writes=[f"Fg{par}_{s}"])
        if s == 0:
            p.op("DVE", lambda e, par=par: e.tensor_copy(
                out=F_Mx[:], in_=F_g[par][0][:]),
                reads=[f"Fg{par}_0"], writes=["Mx"])
            p.op("DVE", lambda e, par=par: e.tensor_copy(
                out=F_Mn[:], in_=F_g[par][0][:]),
                reads=[f"Fg{par}_0"], writes=["Mn"])
        else:
            p.op("DVE", lambda e, s=s, par=par: e.tensor_tensor(
                out=F_Mx[:], in0=F_Mx[:], in1=F_g[par][s][:], op=ALU.max),
                reads=["Mx", f"Fg{par}_{s}"], writes=["Mx"])
            p.op("DVE", lambda e, s=s, par=par: e.tensor_tensor(
                out=F_Mn[:], in0=F_Mn[:], in1=F_g[par][s][:], op=ALU.min),
                reads=["Mn", f"Fg{par}_{s}"], writes=["Mn"])

    def ffn_scales_a(pgc):
        for i, key in ((0, "Mx"), (1, "Mn")):
            for tb in range(2):
                src = F_Mx if i == 0 else F_Mn
                p.op("PE", lambda e, i=i, tb=tb, src=src, key=key: e.transpose(
                    out=X6[:, (i * 2 + tb) * 128:(i * 2 + tb + 1) * 128],
                    in_=src[:, tb * 128:(tb + 1) * 128], identity=iden_bf[:]),
                    reads=[key, "idenb"], writes=["X6"])
        for tb in range(2):
            p.op("DVE", lambda e, tb=tb: e.tensor_reduce(
                out=misc[:, 16 + tb:17 + tb],
                in_=X6[:, tb * 128:(tb + 1) * 128], axis=AX, op=ALU.max),
                reads=["X6"], writes=[f"fcm{tb}"])
            p.op("DVE", lambda e, tb=tb: e.tensor_reduce(
                out=misc[:, 18 + tb:19 + tb],
                in_=X6[:, (2 + tb) * 128:(3 + tb) * 128], axis=AX, op=ALU.min,
                negate=True),
                reads=["X6"], writes=[f"fcn{tb}"])
        p.op("DVE", lambda e: e.tensor_tensor(
            out=misc[:, 20:22], in0=misc[:, 16:18], in1=misc[:, 18:20],
            op=ALU.max),
            reads=["fcm0", "fcm1", "fcn0", "fcn1"], writes=["fm2"])
        p.op("DVE", lambda e: e.tensor_scalar(
            out=misc[:, 20:22], in0=misc[:, 20:22], scalar1=QCLIP, scalar2=None,
            op0=ALU.max), reads=["fm2"], writes=["fm2"])
        p.op("DVE", lambda e: e.tensor_scalar(
            out=misc[:, 24:26], in0=misc[:, 20:22], scalar1=mwb[:, 1:2],
            scalar2=1.0 / 127.0, op0=ALU.mult, op1=ALU.mult),
            reads=["fm2", "mwb1"], writes=["frs2"])
        p.op("DVE", lambda e: e.reciprocal(out=misc[:, 22:24], in_=misc[:, 20:22]),
             reads=["fm2"], writes=["fr2"])
        p.op("DVE", lambda e: e.tensor_scalar(
            out=misc[:, 30:31], in0=misc[:, 22:23], scalar1=127.0, scalar2=None,
            op0=ALU.mult), reads=["fr2"], writes=["fs2v"])
        p.op("DVE", lambda e: e.tensor_scalar(
            out=misc[:, 62:63], in0=misc[:, 23:24], scalar1=127.0, scalar2=None,
            op0=ALU.mult), reads=["fr2"], writes=["fs2v"])

    def ffn_scales_b(pgc):
        import os as _os2
        _ksb = _os2.environ.get("KSB", "tcc")
        p.op("PE", lambda e: e.matmul(
            out=B[3][0:33, 384:512], lhsT=misc[:, 30:63], rhs=iden_f[:],
            start=True, stop=True, is_transpose=True),
            reads=["fs2v", "idenf"], writes=["B3s"])
        for half, row in ((0, 0), (1, 32))[:_ksb.count("c")]:
            p.op("DVE", lambda e, half=half, row=row: e.tensor_copy(
                out=F_s2seg[:, half * 128:(half + 1) * 128],
                in_=B[3][row:row + 1, 384:512]),
                reads=["B3s"], writes=["fs2seg"])

    def ffn_scales_c(pgc):
        p.op("PE", lambda e: e.matmul(
            out=B[3][:, 0:256], lhsT=ones1[:], rhs=F_s2seg[:],
            start=True, stop=True),
            reads=["fs2seg", "ones1"], writes=["B3s"])
        p.op("DVE", lambda e: e.tensor_copy(out=F_sbc[:], in_=B[3][:, 0:256]),
             reads=["B3s"], writes=["sbc"])

    def ffn_round(pgc, cs):
        par = pgc % 2
        up = cs % 2
        p.op("DVE", lambda e, cs=cs, par=par, up=up: e.tensor_tensor(
            out=F_u[up][:], in0=F_g[par][cs][:], in1=F_sbc[:], op=ALU.mult),
            reads=[f"Fg{par}_{cs}", "sbc"], writes=[f"u{up}"])
        p.op("DVE", lambda e, cs=cs, par=par, up=up: e.tensor_scalar(
            out=F_g[par][cs][:], in0=F_u[up][:], scalar1=MAGIC, scalar2=MAGIC,
            op0=ALU.add, op1=ALU.subtract),
            reads=[f"u{up}"], writes=[f"Fg{par}_{cs}"])

    def ffn_w2mm(pgc, grp, cs):
        par = pgc % 2
        tb, h = grp // 2, grp % 2
        bk = 4 + (grp % 2)
        p.op("PE", lambda e, par=par, tb=tb, h=h, cs=cs, bk=bk: e.matmul(
            out=B[bk][:],
            lhsT=F_g[par][cs][:, tb * 128:(tb + 1) * 128],
            rhs=F_W2qT[cs][:, h * 512:(h + 1) * 512],
            start=(cs == 0), stop=(cs == HCH - 1)),
            reads=[f"Fg{par}_{cs}", f"W2qT{cs}"], writes=[f"B{bk}"])

    def ffn_w2evac(pgc, grp):
        par = pgc % 2
        tb, h = grp // 2, grp % 2
        bk = 4 + (grp % 2)
        ev = grp % 2
        p.op("ACT", lambda e, bk=bk, ev=ev, tb=tb: e.activation(
            out=F_ev[ev][:], in_=B[bk][:], func=AF.Identity,
            scale=misc[:, 24 + tb:25 + tb]),
            reads=[f"B{bk}", "frs2"], writes=[f"fev{ev}"])
        p.op("DVE", lambda e, par=par, tb=tb, h=h, ev=ev: e.tensor_tensor(
            out=F_xlnt[par][tb][:, h * 512:(h + 1) * 512],
            in0=F_ev[ev][:], in1=F_xlnt[par][tb][:, h * 512:(h + 1) * 512],
            op=ALU.add),
            reads=[f"fev{ev}", f"fxlnt{par}_{tb}"], writes=[f"fxlnt{par}_{tb}"])

    def ffn_ln2(pgc, tb):
        b, ct = divmod(pgc, NCHUNK)
        par = pgc % 2
        t0 = ct * 256
        sl = slice(t0 + tb * 128, t0 + (tb + 1) * 128)
        xt = F_xlnt[par][tb]
        xk = f"fxlnt{par}_{tb}"
        p.op("DVE", lambda e, xt=xt: e.tensor_reduce(
            out=mc(MS), in_=xt[:], axis=AX, op=ALU.add),
            reads=[xk], writes=["ms"])
        p.op("DVE", lambda e: e.tensor_scalar(
            out=mc(NEGM), in0=mc(MS), scalar1=-1.0 / E, scalar2=None,
            op0=ALU.mult), reads=["ms"], writes=["negm"])
        for h, vs in ((0, VS1), (1, VS2)):
            p.op("ACT", lambda e, xt=xt, h=h, vs=vs: e.activation(
                out=F_lnt[:], in_=xt[:, h * 512:(h + 1) * 512],
                func=AF.Square, bias=mc(NEGM), scale=1.0,
                accum_out=misc[:, vs:vs + 1]),
                reads=[xk, "negm"], writes=["flnt", f"vs{h}"])
        p.op("DVE", lambda e: e.tensor_tensor(
            out=mc(VT), in0=mc(VS1), in1=mc(VS2), op=ALU.add),
            reads=["vs0", "vs1"], writes=["vt"])
        p.op("DVE", lambda e: e.tensor_scalar(
            out=mc(VT), in0=mc(VT), scalar1=1.0 / E, scalar2=LNEPS,
            op0=ALU.mult, op1=ALU.add), reads=["vt"], writes=["vt"])
        p.op("POOL", lambda e: e.tensor_tensor(
            out=mc(RSTD), in0=mc(VT), in1=neghalf[:], op=ALU.pow),
            reads=["vt", "neghalf"], writes=["rstd"])
        p.op("DVE", lambda e: e.tensor_scalar(
            out=mc(NEGMR), in0=mc(NEGM), scalar1=mc(RSTD), scalar2=None,
            op0=ALU.mult), reads=["negm", "rstd"], writes=["negmr"])
        p.op("ACT", lambda e, xt=xt: e.activation(
            out=xt[:], in_=xt[:], func=AF.Identity,
            scale=mc(RSTD), bias=mc(NEGMR)),
            reads=[xk, "rstd", "negmr"], writes=[xk])
        st = p.op("POOL", lambda e, sl=sl, xt=xt, b=b: e.dma_start(
            out=out_d[b, sl, :], in_=xt[:]),
            reads=[xk], writes=[f"outst{b}_{sl.start}"], dma=True)
        out_stores.append(st)

    def ffn_chunk(gc):
        if gc + 1 < NGC:
            ffn_loads(gc + 1)
        ffn_fbc(gc)
        ffn_dbuild(0)
        have_prev = gc > 0
        pgc = gc - 1
        import os as _os
        _kd = _os.environ.get("KDEF", "full")
        _lv = {"none": 0, "scales": 1, "round": 2, "w2": 3, "evac": 4,
               "full": 9}[_kd]
        W2S = {0: 12, 1: 14, 2: 20, 3: 22}
        for s in range(HCH):
            ffn_w1(gc, s)
            if s >= 1:
                ffn_conv(gc, s - 1)
            if s + 1 < HCH:
                ffn_dbuild(s + 1)
            if have_prev:
                if _lv >= 1:
                    _ksc = _os.environ.get("KSC", "abc")
                    if s == 0 and "a" in _ksc:
                        ffn_scales_a(pgc)
                    elif s == 2 and "b" in _ksc:
                        ffn_scales_b(pgc)
                    elif s == 3 and "c" in _ksc:
                        ffn_scales_c(pgc)
                if _lv >= 2 and 4 <= s <= 11:
                    for cs in range(4 * (s - 4), 4 * (s - 3)):
                        ffn_round(pgc, cs)
                if _lv >= 4:
                    if s == 20:
                        ffn_w2evac(pgc, 0)
                    elif s == 22:
                        ffn_w2evac(pgc, 1)
                    elif s == 28:
                        ffn_w2evac(pgc, 2)
                    elif s == 30:
                        ffn_w2evac(pgc, 3)
                if _lv >= 5:
                    if s == 23:
                        ffn_ln2(pgc, 0)
                    elif s == 31:
                        ffn_ln2(pgc, 1)
                if s == 31 and gc + 1 < NGC:
                    ffn_loads_xlnt(gc + 1)
                if _lv >= 3:
                    for grp, s0 in W2S.items():
                        if s0 <= s < s0 + 8:
                            for cs in range(4 * (s - s0), 4 * (s - s0 + 1)):
                                ffn_w2mm(pgc, grp, cs)
        if not have_prev and gc + 1 < NGC:
            ffn_loads_xlnt(gc + 1)
        ffn_conv(gc, HCH - 1)

    def ffn_tail():
        pgc = NGC - 1
        ffn_scales_a(pgc)
        ffn_scales_b(pgc)
        ffn_scales_c(pgc)
        for cs in range(HCH):
            ffn_round(pgc, cs)
        for grp in range(4):
            for cs in range(HCH):
                ffn_w2mm(pgc, grp, cs)
            ffn_w2evac(pgc, grp)
            if grp == 1:
                ffn_ln2(pgc, 0)
        ffn_ln2(pgc, 1)

    out_stores = []
    import os as _os
    _cut = _os.environ.get("KCUT", "full")
    for b in range(NB):
        attn(b)
    if _cut != "attn":
        if _cut != "noprep":
            prep()
        if _cut not in ("prep", "noprep"):
            ffn_entry()
            ffn_loads(0)
            ffn_loads_xlnt(0)
            if _cut != "entry":
                nch = NGC if _cut == "full" else min(NGC, int(_cut[1:]))
                for gc in range(nch):
                    ffn_chunk(gc)
                if _cut == "full":
                    ffn_tail()
    p.op("POOL", lambda e: e.nop(), deps=out_stores)

    import contextlib as _ctx
    with _ctx.ExitStack() as st:
        block = st.enter_context(nc.Block())
        sems = {e: st.enter_context(nc.semaphore("sem" + e)) for e in Prog.ENGINES}
        p.build(nc, block, sems)
    pstack.close()
    return nc


def kernel(L0_attn_X, L0_attn_Wq, L0_attn_bq, L0_attn_Wk, L0_attn_bk,
           L0_attn_Wv, L0_attn_bv, L0_ffn_w1, L0_ffn_conv_w, L0_ffn_conv_b,
           L0_ffn_w2):
    X = np.ascontiguousarray(np.asarray(L0_attn_X, dtype=np.float32))
    cw = np.asarray(L0_ffn_conv_w, dtype=np.float32).reshape(4096, 3)
    common = {
        "wq": np.ascontiguousarray(np.asarray(L0_attn_Wq, np.float32)),
        "wk": np.ascontiguousarray(np.asarray(L0_attn_Wk, np.float32)),
        "wv": np.ascontiguousarray(np.asarray(L0_attn_Wv, np.float32)),
        "bq": np.ascontiguousarray(np.asarray(L0_attn_bq, np.float32)),
        "bk": np.ascontiguousarray(np.asarray(L0_attn_bk, np.float32)),
        "bv": np.ascontiguousarray(np.asarray(L0_attn_bv, np.float32)),
        "w1": np.ascontiguousarray(np.asarray(L0_ffn_w1, np.float32)),
        "w2": np.ascontiguousarray(np.asarray(L0_ffn_w2, np.float32)),
        "cw0": np.ascontiguousarray(cw[:, 0]),
        "cw1": np.ascontiguousarray(cw[:, 1]),
        "cw2": np.ascontiguousarray(cw[:, 2]),
        "cb": np.ascontiguousarray(np.asarray(L0_ffn_conv_b, np.float32)),
    }
    n_cores = 8
    nb = X.shape[0] // n_cores
    nc = build_program(T=X.shape[1], NB=nb)
    in_maps = []
    for c in range(n_cores):
        m = dict(common)
        m["x"] = np.ascontiguousarray(X[c * nb:(c + 1) * nb])
        in_maps.append(m)
    res = run_bass_kernel_spmd(nc, in_maps, core_ids=list(range(n_cores)))
    out = np.concatenate([np.asarray(r["out"]) for r in res.results], axis=0)
    return out.astype(np.float32)


# version 13
# speedup vs baseline: 1.2062x; 1.0030x over previous
import numpy as np
import ml_dtypes
import concourse.bass as bass
import concourse.mybir as mybir
from concourse.bass_utils import run_bass_kernel_spmd

F32 = mybir.dt.float32
BF16 = mybir.dt.bfloat16
U8 = mybir.dt.uint8
AF = mybir.ActivationFunctionType
ALU = mybir.AluOpType
AX = mybir.AxisListType.X

MAGIC = 12582912.0
NEGBIG = -1e30
LNEPS = 1e-5
QCLIP = 1e-5
E = 1024
H = 4096
ECH = 8
HCH = 32
SBUF_CAP = 212_256


class Prog:

    ENGINES = ("SP", "PE", "ACT", "DVE", "POOL", "PD", "AD")

    def __init__(self):
        self.ops = []
        self.res = {}
        self.seed = {}
        self.last_dma = {}
        self.last_xbar = None
        import os
        self.serial = set(os.environ.get("KSYNC", "").split(",")) - {""}

    def _touch(self, k, depset):
        if k in self.seed:
            depset.update(self.seed.pop(k))
        r = self.res.get(k)
        if r is None:
            r = self.res[k] = {"w": None, "r": []}
        return r

    def op(self, engine, emit, reads=(), writes=(), deps=(), dma=False,
           xbar=False):
        idx = len(self.ops)
        depset = set(deps)
        for k in reads:
            r = self._touch(k, depset)
            if r["w"] is not None:
                depset.add(r["w"])
            r["r"].append(idx)
        for k in writes:
            r = self._touch(k, depset)
            if r["w"] is not None:
                depset.add(r["w"])
            depset.update(r["r"])
            r["w"] = idx
            r["r"] = []
        if dma:
            if engine in self.last_dma:
                depset.add(self.last_dma[engine])
            if self.last_xbar is not None:
                depset.add(self.last_xbar)
            if xbar:
                for q, d in self.last_dma.items():
                    depset.add(d)
                self.last_xbar = idx
            self.last_dma[engine] = idx
        depset.discard(idx)
        if engine == "PE":
            depset = {d for d in depset if self.ops[d][0] != "PE"}
        if idx > 0 and ("all" in self.serial or engine in self.serial):
            depset.add(idx - 1)
        self.ops.append((engine, emit, dma, depset))
        return idx

    def fence(self, new_keys, old_keys):
        dep = set()
        for k in old_keys:
            r = self.res.get(k)
            if r is not None:
                if r["w"] is not None:
                    dep.add(r["w"])
                dep.update(r["r"])
        for k in new_keys:
            self.seed[k] = set(self.seed.get(k, ())) | dep

    def build(self, nc, block, sems):
        needed = set()
        for _, _, _, deps in self.ops:
            needed.update(deps)
        incs = []
        cums = []
        cum = {e: 0 for e in self.ENGINES}
        streams = []
        for i, (eng, _, dma, _) in enumerate(self.ops):
            inc = 16 if dma else (1 if i in needed else 0)
            incs.append(inc)
            stream = ("PD" if eng == "POOL" else
                      "AD" if eng == "ACT" else eng) if dma else eng
            streams.append(stream)
            cum[stream] += inc
            cums.append(cum[stream])
        per = {e: [] for e in self.ENGINES}
        for i, (eng, emit, dma, deps) in enumerate(self.ops):
            per[eng].append((i, emit, incs[i], deps))
        engobj = {"SP": nc.sync, "PE": nc.tensor, "ACT": nc.scalar,
                  "DVE": nc.vector, "POOL": nc.gpsimd}
        openg = streams

        def make_section(ename):
            ops = per[ename]
            eo = engobj[ename]

            def section(_):
                waited = {f: 0 for f in self.ENGINES}
                for i, emit, inc, deps in ops:
                    want = {}
                    for d in deps:
                        f = openg[d]
                        v = cums[d]
                        if v > want.get(f, 0):
                            want[f] = v
                    for f, v in want.items():
                        if waited[f] < v:
                            eo.wait_ge(sems[f], v)
                            waited[f] = v
                    inst = emit(eo)
                    if inc:
                        inst.then_inc(sems[streams[i]], inc)

            return section

        for ename, sec in (("SP", block.sync), ("PE", block.tensor),
                           ("ACT", block.scalar), ("DVE", block.vector),
                           ("POOL", block.gpsimd)):
            if per[ename]:
                sec(make_section(ename))


class Mem:

    def __init__(self, nc, cap):
        self.nc = nc
        self.base = (nc.sbuf_base + 31) // 32 * 32
        self.cap = min(cap, nc.sbuf_top - self.base)

    def allocate(self):
        pass

    def at(self, off, name, shape, dtype):
        per = int(np.prod(shape[1:])) * mybir.dt.size(dtype)
        assert off % 32 == 0 and off + per <= self.cap, (name, off, per)
        return self.nc.alloc_sbuf_tensor_at(name, list(shape), dtype,
                                            offset=self.base + off)


def _ceil(a, m):
    return (a + m - 1) // m * m


def build_program(T=2048, NB=2, debug=False):
    nc = bass.Bass("TRN2", target_bir_lowering=False)
    NT = T // 128
    NCHUNK = T // 256
    p = Prog()

    dram = {}

    def din(name, shape, dtype=F32):
        dram[name] = nc.dram_tensor(name, shape, dtype, kind="ExternalInput").ap()
        return dram[name]

    def dout(name, shape, dtype=F32):
        dram[name] = nc.dram_tensor(name, shape, dtype, kind="ExternalOutput").ap()
        return dram[name]

    def dscr(name, shape, dtype):
        kind = "ExternalOutput" if debug else "Internal"
        dram[name] = nc.dram_tensor(name, shape, dtype, kind=kind).ap()
        return dram[name]

    x_d = din("x", [NB, T, E])
    wq_d = din("wq", [E, E]); wk_d = din("wk", [E, E]); wv_d = din("wv", [E, E])
    bq_d = din("bq", [E]); bk_d = din("bk", [E]); bv_d = din("bv", [E])
    w1_d = din("w1", [H, E]); w2_d = din("w2", [E, H])
    cw0_d = din("cw0", [H]); cw1_d = din("cw1", [H]); cw2_d = din("cw2", [H])
    cb_d = din("cb", [H])
    out_d = dout("out", [NB, T, E])

    s_xbf = dscr("s_xbf", [NB, T, E], BF16)
    s_xqT = dscr("s_xqT", [NB, ECH, 128, T + 16], BF16)
    s_xln = dscr("s_xln", [NB, T, E], F32)
    s_m1 = dscr("s_m1", [NB, T + 2], F32)
    s_wq = dscr("s_wq", [E, E], BF16)
    s_wk = dscr("s_wk", [E, E], BF16)
    s_wv = dscr("s_wv", [E, E], BF16)
    s_w1i = dscr("s_w1i", [H, E], BF16)
    s_w2i = dscr("s_w2i", [E, H], BF16)
    if debug:
        d_s2seg = dscr("d_s2seg", [1, 256], F32)
        d_sbc = dscr("d_sbc", [128, 256], F32)
        d_fbc = dscr("d_fbc", [128, 258], F32)
        d_g = dscr("d_g", [HCH, 128, 256], F32)
        d_hq = dscr("d_hq", [HCH, 128, 256], BF16)
        d_mx = dscr("d_mx", [128, 256], F32)
        d_mn = dscr("d_mn", [128, 256], F32)

    mem = Mem(nc, SBUF_CAP)
    mem.allocate()

    off = 0
    def P_(name, shape, dtype):
        nonlocal off
        h = mem.at(off, name, shape, dtype)
        per = int(np.prod(shape[1:])) * mybir.dt.size(dtype)
        off = _ceil(off + per, 32)
        return h

    iden_bf = P_("iden_bf", [128, 128], BF16)
    iden_f = P_("iden_f", [128, 128], F32)
    maskneg = P_("maskneg", [128, 128], F32)
    ones1 = P_("ones1", [1, 128], F32)
    onescol = P_("onescol", [128, 1], F32)
    neghalf = P_("neghalf", [128, 1], F32)
    w0c = P_("w0c", [128, 32], F32)
    w1c = P_("w1c", [128, 32], F32)
    w2c = P_("w2c", [128, 32], F32)
    cbc = P_("cbc", [128, 32], F32)
    bqc = P_("bqc", [128, 8], F32)
    bkc = P_("bkc", [128, 8], F32)
    misc = P_("misc", [128, 64], F32)
    m1ct = P_("m1ct", [128, NT], F32)
    mrow = P_("mrow", [NT, 128], F32)
    mw = P_("mw", [1, 8], F32)
    mwb = P_("mwb", [128, 2], F32)
    zpad = P_("zpad", [128, 8], BF16)
    PERS_END = off
    OB = _ceil(PERS_END, 32)
    AVAIL = SBUF_CAP - OB

    MS, NEGM, VS1, VS2, VT, RSTD, NEGMR, M1, M1C, R1, S1V, BIASV, RCOL = range(13)
    MRAW = 14
    CMAX0 = 16
    CSUM0 = 20
    COLSUM = 24
    MX0 = 28
    RWB = 44

    def mc(i):
        return misc[:, i:i + 1]

    A_P = [mem.at(OB + jt * 4096, f"A_P{jt}", [128, T], BF16) for jt in range(NT)]
    A_xT = [mem.at(OB + e * 4096, f"A_xT{e}", [128, T], BF16) for e in range(ECH)]
    A_WvT = [mem.at(OB + 32768 + e * 2048, f"A_WvT{e}", [128, E], BF16)
             for e in range(ECH)]
    A_BVbc = mem.at(OB + 49152, "A_BVbc", [128, E], F32)
    A_brow = mem.at(OB + 53248, "A_brow", [32, 128], F32)
    A_brow2 = mem.at(OB + 53760, "A_brow2", [1, E], F32)
    A2 = OB + 65536
    A_qT = [mem.at(A2 + e * 4096, f"A_qT{e}", [128, T], BF16) for e in range(ECH)]
    A_kT = [mem.at(A2 + 32768 + e * 4096, f"A_kT{e}", [128, T], BF16)
            for e in range(ECH)]
    A_v = [mem.at(A2 + 65536 + t * 2048, f"A_v{t}", [128, E], BF16)
           for t in range(NT)]
    A_WqT = [mem.at(A2 + 98304 + e * 2048, f"A_WqT{e}", [128, E], BF16)
             for e in range(ECH)]
    A_WkT = [mem.at(A2 + 114688 + e * 2048, f"A_WkT{e}", [128, E], BF16)
             for e in range(ECH)]
    AT = A2 + 131072
    A_x32 = [mem.at(AT + i * 4096, f"A_x32_{i}", [128, E], F32) for i in range(2)]
    A_xbf = [mem.at(AT + 8192 + i * 2048, f"A_xbf{i}", [128, E], BF16)
             for i in range(2)]
    assert AT + 12288 <= mem.cap
    PV_xres = mem.at(A2 + 0, "PV_xres", [128, E], F32)
    PV_xln = [mem.at(A2 + 4096 + i * 4096, f"PV_xln{i}", [128, E], F32)
              for i in range(2)]
    PV_xqstg = mem.at(A2 + 12288, "PV_xqstg", [128, E], BF16)
    PV_xqbf = mem.at(A2 + 14336, "PV_xqbf", [128, E], BF16)
    PV_xsrc = [mem.at(A2 + 16384 + i * 4096, f"PV_xsrc{i}", [128, E], F32)
               for i in range(2)]

    PR_in = [mem.at(OB + i * 8192, f"PR_in{i}", [128, 2048], F32)
             for i in range(4)]
    PR_ob = [mem.at(OB + 32768 + i * 4096, f"PR_ob{i}", [128, 2048], BF16)
             for i in range(2)]
    PR_ap = mem.at(OB + 40960, "PR_ap", [128, 32], F32)

    F_W1qT = [mem.at(OB + e * 8192, f"F_W1qT{e}", [128, H], BF16)
              for e in range(ECH)]
    F_W2qT = [mem.at(OB + 65536 + c * 2048, f"F_W2qT{c}", [128, E], BF16)
              for c in range(HCH)]
    F_g = [[mem.at(OB + 131072 + pr * 16384 + s * 512, f"Fg{pr}_{s}",
                   [128, 256], BF16) for s in range(HCH)] for pr in range(2)]
    FX = OB + 163840
    F_xq = [[mem.at(FX + pr * 4352 + e * 544, f"F_xq{pr}_{e}", [128, 258], BF16)
             for e in range(ECH)] for pr in range(2)]
    F_xlnt = [[mem.at(FX + 8704 + (pr * 2 + i) * 4096, f"F_xlnt{pr}_{i}",
                      [128, E], F32) for i in range(2)] for pr in range(2)]
    F_hs = [mem.at(FX + 25088 + i * 544, f"F_hs{i}", [128, 258], BF16)
            for i in range(2)]
    F_u = [mem.at(FX + 26176 + i * 1024, f"F_u{i}", [128, 256], F32)
           for i in range(2)]
    F_Mx = mem.at(FX + 28224, "F_Mx", [128, 256], BF16)
    F_Mn = mem.at(FX + 28736, "F_Mn", [128, 256], BF16)
    F_fbc = [mem.at(FX + 29248 + i * 1056, f"F_fbc{i}", [128, 258], F32)
             for i in range(2)]
    F_sbc = mem.at(FX + 31360, "F_sbc", [128, 256], F32)
    F_fw = [mem.at(FX + 32384 + i * 1056, f"F_fw{i}", [1, 258], F32)
            for i in range(2)]
    F_s2seg = mem.at(FX + 34496, "F_s2seg", [1, 256], F32)
    F_ev = [mem.at(FX + 35520 + i * 2048, f"F_ev{i}", [128, 512], F32)
            for i in range(2)]
    F_lnt = mem.at(FX + 39616, "F_lnt", [128, 512], F32)
    F_D = [[mem.at(FX + 41664 + (k * 2 + pr) * 256, f"F_D{k}_{pr}",
                   [128, 128], BF16) for pr in range(2)] for k in range(3)]
    assert FX + 43200 <= mem.cap, FX + 43200

    import contextlib
    pstack = contextlib.ExitStack()
    B = [pstack.enter_context(nc.psum_tensor(f"bank{i}", [128, 512], F32))
         for i in range(6)]
    X6 = pstack.enter_context(nc.psum_tensor("xb6", [128, 1024], BF16))
    X7 = pstack.enter_context(nc.psum_tensor("xb7", [128, 1024], BF16))

    p.op("POOL", lambda e: e.iota(out=iden_f[:], pattern=[[1, 128]], base=0,
                                  channel_multiplier=-1,
                                  allow_small_or_imprecise_dtypes=True),
         writes=["idenf"])
    p.op("DVE", lambda e: e.tensor_scalar(out=maskneg[:], in0=iden_f[:],
                                          scalar1=0.0, scalar2=NEGBIG,
                                          op0=ALU.is_lt, op1=ALU.mult),
         reads=["idenf"], writes=["maskneg"])
    p.op("DVE", lambda e: e.tensor_scalar(out=iden_f[:], in0=iden_f[:],
                                          scalar1=0.0, scalar2=None,
                                          op0=ALU.is_equal),
         reads=["idenf"], writes=["idenf"])
    p.op("DVE", lambda e: e.tensor_copy(out=iden_bf[:], in_=iden_f[:]),
         reads=["idenf"], writes=["idenb"])
    p.op("POOL", lambda e: e.memset(ones1[:], 1.0), writes=["ones1"])
    p.op("POOL", lambda e: e.memset(onescol[:], 1.0), writes=["onescol"])
    p.op("POOL", lambda e: e.memset(neghalf[:], -0.5), writes=["neghalf"])
    p.op("POOL", lambda e: e.memset(zpad[:], 0.0), writes=["zpad"])
    p.op("POOL", lambda e: e.memset(misc[:, 60:64], 0.0), writes=["msczero"])
    p.op("POOL", lambda e: e.memset(misc[:, 47:48], MAGIC), writes=["magc"])

    for b in range(NB):
        for e8 in range(ECH):
            p.op("POOL", lambda e, b=b, e8=e8: e.dma_start(
                out=s_xqT[b, e8, :, 0:8], in_=zpad[:]),
                reads=["zpad"], writes=[f"sxq_zh{b}_{e8}"], dma=True)
            p.op("POOL", lambda e, b=b, e8=e8: e.dma_start(
                out=s_xqT[b, e8, :, T + 8:T + 16], in_=zpad[:]),
                reads=["zpad"], writes=[f"sxq_zt{b}_{e8}"], dma=True)
    for b in range(NB):
        p.op("POOL", lambda e, b=b: e.dma_start(
            out=s_m1[b, 0:1].rearrange("(o a) -> o a", o=1), in_=misc[0:1, 60:61]),
            reads=["msczero"], writes=[f"sm1z{b}"], dma=True)
        p.op("POOL", lambda e, b=b: e.dma_start(
            out=s_m1[b, T + 1:T + 2].rearrange("(o a) -> o a", o=1), in_=misc[0:1, 61:62]),
            reads=["msczero"], writes=[f"sm1z2{b}"], dma=True)

    def load_col_params(d_ap, n, dst, key, bankslice, ident_n):
        p.op("SP", lambda e: e.dma_start(
            out=A_brow[0:n, :], in_=d_ap.rearrange("(a b) -> a b", b=128)),
            writes=["brow"], dma=True)
        p.op("PE", lambda e: e.transpose(out=bankslice, in_=A_brow[0:n, :],
                                         identity=iden_f[0:n, 0:n]),
             reads=["brow", "idenf"], writes=["B5"])
        p.op("DVE", lambda e: e.tensor_copy(out=dst, in_=bankslice),
             reads=["B5"], writes=[key])

    for d_ap, dst, key in ((cw0_d, w0c, "w0c"), (cw1_d, w1c, "w1c"),
                           (cw2_d, w2c, "w2c"), (cb_d, cbc, "cbc")):
        load_col_params(d_ap, 32, dst[:, 0:32], key, B[5][0:128, 0:32], 32)
    for d_ap, dst, key in ((bq_d, bqc, "bqc"), (bk_d, bkc, "bkc")):
        load_col_params(d_ap, 8, dst[:, 0:8], key, B[5][0:128, 0:8], 8)
    for wd, sd, nm in ((wq_d, s_wq, "wq"), (wk_d, s_wk, "wk"), (wv_d, s_wv, "wv")):
        for i in range(ECH):
            sl = slice(i * 128, (i + 1) * 128)
            p.op("SP", lambda e, wd=wd, sl=sl, i=i: e.dma_start(
                out=A_x32[i % 2][:], in_=wd[sl, :]),
                writes=[f"x32_{i % 2}"], dma=True)
            p.op("DVE", lambda e, i=i: e.tensor_copy(
                out=A_xbf[i % 2][:], in_=A_x32[i % 2][:]),
                reads=[f"x32_{i % 2}"], writes=[f"xbf{i % 2}"])
            p.op("POOL", lambda e, sd=sd, sl=sl, i=i: e.dma_start(
                out=sd[sl, :], in_=A_xbf[i % 2][:]),
                reads=[f"xbf{i % 2}"], writes=[f"scr_{nm}_{i}"], dma=True)

    def attn(b):
        p.fence([f"xT{e}_{t}" for e in range(ECH) for t in range(NT)]
                + [f"WvT{e}" for e in range(ECH)] + ["BVbc", "bvrow",
                   "x32_0", "x32_1", "xbf0", "xbf1"],
                [f"P{jt}" for jt in range(NT)])
        for it in range(NT):
            sl = slice(it * 128, (it + 1) * 128)
            p.op("SP", lambda e, sl=sl, it=it: e.dma_start(
                out=A_x32[it % 2][:], in_=x_d[b, sl, :]),
                writes=[f"x32_{it % 2}"], dma=True)
            p.op("DVE", lambda e, it=it: e.tensor_copy(
                out=A_xbf[it % 2][:], in_=A_x32[it % 2][:]),
                reads=[f"x32_{it % 2}"], writes=[f"xbf{it % 2}"])
            for e8 in range(ECH):
                xb = X6 if e8 % 2 == 0 else X7
                xk = "X6" if e8 % 2 == 0 else "X7"
                p.op("PE", lambda e, it=it, e8=e8, xb=xb: e.transpose(
                    out=xb[:, 0:128],
                    in_=A_xbf[it % 2][:, e8 * 128:(e8 + 1) * 128],
                    identity=iden_bf[:]),
                    reads=[f"xbf{it % 2}", "idenb"], writes=[xk])
                p.op("ACT", lambda e, it=it, e8=e8, xb=xb: e.copy(
                    out=A_xT[e8][:, it * 128:(it + 1) * 128], in_=xb[:, 0:128]),
                    reads=[xk], writes=[f"xT{e8}_{it}"])
        p.op("SP", lambda e: e.dma_start(out=A_brow2[:],
                                         in_=bv_d.rearrange("(o a) -> o a", o=1)),
             writes=["bvrow"], dma=True)
        for h in range(2):
            p.op("PE", lambda e, h=h: e.matmul(
                out=B[h][:], lhsT=ones1[:],
                rhs=A_brow2[:, h * 512:(h + 1) * 512], start=True, stop=True),
                reads=["bvrow", "ones1"], writes=[f"B{h}"])
            p.op("DVE", lambda e, h=h: e.tensor_copy(
                out=A_BVbc[:, h * 512:(h + 1) * 512], in_=B[h][:]),
                reads=[f"B{h}"], writes=["BVbc"])
        wlist = [(s_wv, A_WvT, "WvT", "wv")]
        if b == 0:
            wlist += [(s_wq, A_WqT, "WqT", "wq"), (s_wk, A_WkT, "WkT", "wk")]
        for sd, dstl, nm, srck in wlist:
            for e8 in range(ECH):
                p.op("SP", lambda e, sd=sd, dstl=dstl, e8=e8: e.dma_start(
                    out=dstl[e8][:], in_=sd[:, e8 * 128:(e8 + 1) * 128],
                    transpose=True),
                    reads=[f"scr_{srck}_{i}" for i in range(ECH)],
                    writes=[f"{nm}{e8}"], dma=True, xbar=True)

        for it in range(NT):
            g = it % 2
            bk0, bk1 = 2 * g, 2 * g + 1
            for h, bk in ((0, bk0), (1, bk1)):
                for e8 in range(ECH):
                    p.op("PE", lambda e, it=it, h=h, bk=bk, e8=e8: e.matmul(
                        out=B[bk][:],
                        lhsT=A_xT[e8][:, it * 128:(it + 1) * 128],
                        rhs=A_WvT[e8][:, h * 512:(h + 1) * 512],
                        start=(e8 == 0), stop=(e8 == ECH - 1)),
                        reads=[f"xT{e8}_{it}", f"WvT{e8}"], writes=[f"B{bk}"])
                p.op("DVE", lambda e, it=it, h=h, bk=bk: e.tensor_tensor(
                    out=A_v[it][:, h * 512:(h + 1) * 512], in0=B[bk][:],
                    in1=A_BVbc[:, h * 512:(h + 1) * 512], op=ALU.add),
                    reads=[f"B{bk}", "BVbc"], writes=[f"v{it}"])

        icw = min(512, T)
        for dstl, wTl, bc, nm in ((A_qT, A_WqT, bqc, "q"), (A_kT, A_WkT, bkc, "k")):
            wkey = "WqT" if nm == "q" else "WkT"
            for fc in range(ECH):
                for ic in range(T // icw):
                    bk = (fc * (T // icw) + ic) % 4
                    for e8 in range(ECH):
                        p.op("PE", lambda e, fc=fc, ic=ic, bk=bk, e8=e8,
                             wTl=wTl: e.matmul(
                            out=B[bk][:, 0:icw],
                            lhsT=wTl[e8][:, fc * 128:(fc + 1) * 128],
                            rhs=A_xT[e8][:, ic * icw:(ic + 1) * icw],
                            start=(e8 == 0), stop=(e8 == ECH - 1)),
                            reads=[f"xT{e8}_{ic * icw // 128 + t}"
                                   for t in range(icw // 128)]
                            + [f"{wkey}{e8}"], writes=[f"B{bk}"])
                    p.op("ACT", lambda e, fc=fc, ic=ic, bk=bk, dstl=dstl,
                         bc=bc: e.activation(
                        out=dstl[fc][:, ic * icw:(ic + 1) * icw],
                        in_=B[bk][:, 0:icw],
                        func=AF.Identity, bias=bc[:, fc:fc + 1], scale=1.0),
                        reads=[f"B{bk}", "b" + nm + "c"],
                        writes=[f"{nm}T{fc}"])

        p.fence([f"P{jt}" for jt in range(NT)],
                [f"xT{e}_{t}" for e in range(ECH) for t in range(NT)]
                + [f"WvT{e}" for e in range(ECH)]
                + ["BVbc", "BVbc_use", "brow", "bvrow"])
        gchunk = 0
        for jt in range(NT):
            c0 = jt * 128
            bounds = [c0]
            nxt = (c0 // 512 + 1) * 512
            while nxt < T:
                bounds.append(nxt)
                nxt += 512
            bounds.append(T)
            chunks = [(bounds[i], bounds[i + 1] - bounds[i])
                      for i in range(len(bounds) - 1)]
            nch = len(chunks)
            banks = []
            for ci in range(nch):
                banks.append(gchunk % 6)
                gchunk += 1
            for e8 in range(ECH):
                for ci, (cs, w) in enumerate(chunks):
                    bk = banks[ci]
                    p.op("PE", lambda e, e8=e8, jt=jt, cs=cs, w=w, bk=bk: e.matmul(
                        out=B[bk][:, 0:w],
                        lhsT=A_kT[e8][:, jt * 128:(jt + 1) * 128],
                        rhs=A_qT[e8][:, cs:cs + w],
                        start=(e8 == 0), stop=(e8 == ECH - 1)),
                        reads=[f"kT{e8}", f"qT{e8}"], writes=[f"B{bk}"])
            bk0 = banks[0]
            p.op("DVE", lambda e, bk0=bk0: e.tensor_tensor(
                out=B[bk0][:, 0:128], in0=B[bk0][:, 0:128], in1=maskneg[:],
                op=ALU.add),
                reads=[f"B{bk0}", "maskneg"], writes=[f"B{bk0}"])
            for ci, (cs, w) in enumerate(chunks):
                bk = banks[ci]
                p.op("DVE", lambda e, bk=bk, w=w, ci=ci: e.tensor_reduce(
                    out=misc[:, CMAX0 + ci:CMAX0 + ci + 1], in_=B[bk][:, 0:w],
                    axis=AX, op=ALU.max),
                    reads=[f"B{bk}"], writes=[f"cmax{ci}"])
            p.op("DVE", lambda e, nch=nch: e.tensor_reduce(
                out=mc(MRAW), in_=misc[:, CMAX0:CMAX0 + nch], axis=AX,
                op=ALU.max),
                reads=[f"cmax{ci}" for ci in range(nch)], writes=["Mraw"])
            p.op("DVE", lambda e: e.tensor_scalar(
                out=mc(BIASV), in0=mc(MRAW), scalar1=-1.0 / 32.0, scalar2=None,
                op0=ALU.mult),
                reads=["Mraw"], writes=["biasv"])
            for ci, (cs, w) in enumerate(chunks):
                bk = banks[ci]
                p.op("ACT", lambda e, bk=bk, w=w, jt=jt, cs=cs, c0=c0,
                     ci=ci: e.activation(
                    out=A_P[jt][:, cs - c0:cs - c0 + w], in_=B[bk][:, 0:w],
                    func=AF.Exp, bias=mc(BIASV), scale=1.0 / 32.0,
                    accum_out=misc[:, CSUM0 + ci:CSUM0 + ci + 1]),
                    reads=[f"B{bk}", "biasv"],
                    writes=[f"P{jt}", f"csum{ci}", f"B{bk}_free"])
            p.op("DVE", lambda e, nch=nch: e.tensor_reduce(
                out=mc(COLSUM), in_=misc[:, CSUM0:CSUM0 + nch], axis=AX,
                op=ALU.add),
                reads=[f"csum{ci}" for ci in range(nch)], writes=["colsum"])
            p.op("DVE", lambda e: e.reciprocal(out=mc(RCOL), in_=mc(COLSUM)),
                 reads=["colsum"], writes=["rcol"])
            p.op("DVE", lambda e, jt=jt: e.tensor_scalar(
                out=A_v[jt][:], in0=A_v[jt][:], scalar1=mc(RCOL), scalar2=None,
                op0=ALU.mult),
                reads=["rcol", f"v{jt}"], writes=[f"v{jt}"])

        p.fence(["xres", "xln0", "xln1", "xqstg", "xqbf", "pvx0", "pvx1"],
                [f"qT{e}" for e in range(ECH)])
        pv_pend = []

        def pv_xpose_flush():
            pit = pv_pend.pop(0)
            for e8 in range(ECH):
                xb = X6 if e8 % 2 == 0 else X7
                xk = "X6" if e8 % 2 == 0 else "X7"
                p.op("PE", lambda e, e8=e8, xb=xb: e.transpose(
                    out=xb[:, 0:128], in_=PV_xqbf[:, e8 * 128:(e8 + 1) * 128],
                    identity=iden_bf[:]),
                    reads=["xqbf", "idenb"], writes=[xk])
                p.op("ACT", lambda e, e8=e8, xb=xb: e.copy(
                    out=PV_xqstg[:, e8 * 128:(e8 + 1) * 128], in_=xb[:, 0:128]),
                    reads=[xk], writes=["xqstg"])
            p.op("POOL", lambda e, pit=pit: e.dma_start(
                out=s_xqT[b, :, :, 8 + pit * 128:8 + (pit + 1) * 128]
                .rearrange("e p t -> p e t"),
                in_=PV_xqstg[:].rearrange("p (e t) -> p e t", e=ECH)),
                reads=["xqstg"], writes=[f"sxq{b}_{pit}"], dma=True)

        for it in range(NT):
            sl = slice(it * 128, (it + 1) * 128)
            g = it % 2
            p.op("SP", lambda e, sl=sl, g=g: e.dma_start(
                out=PV_xsrc[g][:], in_=x_d[b, sl, :]),
                writes=[f"pvx{g}"], dma=True)
            for h in range(2):
                bk = 2 * g + h
                for jt in range(it + 1):
                    p.op("PE", lambda e, it=it, jt=jt, h=h, bk=bk: e.matmul(
                        out=B[bk][:],
                        lhsT=A_P[jt][:, it * 128 - jt * 128:
                                      it * 128 - jt * 128 + 128],
                        rhs=A_v[jt][:, h * 512:(h + 1) * 512],
                        start=(jt == 0), stop=(jt == it)),
                        reads=[f"P{jt}", f"v{jt}"], writes=[f"B{bk}"])
                p.op("DVE", lambda e, h=h, bk=bk, g=g: e.tensor_tensor(
                    out=PV_xres[:, h * 512:(h + 1) * 512], in0=B[bk][:],
                    in1=PV_xsrc[g][:, h * 512:(h + 1) * 512], op=ALU.add),
                    reads=[f"B{bk}", f"pvx{g}"], writes=["xres"])
            if pv_pend:
                pv_xpose_flush()
            p.op("DVE", lambda e: e.tensor_reduce(
                out=mc(MS), in_=PV_xres[:], axis=AX, op=ALU.add),
                reads=["xres"], writes=["ms"])
            p.op("DVE", lambda e: e.tensor_scalar(
                out=mc(NEGM), in0=mc(MS), scalar1=-1.0 / E, scalar2=None,
                op0=ALU.mult), reads=["ms"], writes=["negm"])
            for h, vs in ((0, VS1), (1, VS2)):
                bk = 4 + h
                p.op("ACT", lambda e, h=h, vs=vs, bk=bk: e.activation(
                    out=B[bk][:], in_=PV_xres[:, h * 512:(h + 1) * 512],
                    func=AF.Square, bias=mc(NEGM), scale=1.0,
                    accum_out=misc[:, vs:vs + 1]),
                    reads=["xres", "negm"], writes=[f"B{bk}", f"vs{h}"])
            p.op("DVE", lambda e: e.tensor_tensor(
                out=mc(VT), in0=mc(VS1), in1=mc(VS2), op=ALU.add),
                reads=["vs0", "vs1"], writes=["vt"])
            p.op("DVE", lambda e: e.tensor_scalar(
                out=mc(VT), in0=mc(VT), scalar1=1.0 / E, scalar2=LNEPS,
                op0=ALU.mult, op1=ALU.add), reads=["vt"], writes=["vt"])
            p.op("POOL", lambda e: e.tensor_tensor(
                out=mc(RSTD), in0=mc(VT), in1=neghalf[:], op=ALU.pow),
                reads=["vt", "neghalf"], writes=["rstd"])
            p.op("DVE", lambda e: e.tensor_scalar(
                out=mc(NEGMR), in0=mc(NEGM), scalar1=mc(RSTD), scalar2=None,
                op0=ALU.mult), reads=["negm", "rstd"], writes=["negmr"])
            p.op("ACT", lambda e, g=g: e.activation(
                out=PV_xln[g][:], in_=PV_xres[:], func=AF.Identity,
                scale=mc(RSTD), bias=mc(NEGMR)),
                reads=["xres", "rstd", "negmr"], writes=[f"xln{g}"])
            p.op("POOL", lambda e, sl=sl, g=g: e.dma_start(
                out=s_xln[b, sl, :], in_=PV_xln[g][:]),
                reads=[f"xln{g}"], writes=[f"sxln{b}_{it}"], dma=True)
            p.op("DVE", lambda e, g=g: e.tensor_reduce(
                out=mc(M1), in_=PV_xln[g][:], axis=AX, op=ALU.max,
                apply_absolute_value=True),
                reads=[f"xln{g}"], writes=["m1"])
            p.op("DVE", lambda e: e.tensor_scalar(
                out=mc(M1C), in0=mc(M1), scalar1=QCLIP, scalar2=None,
                op0=ALU.max), reads=["m1"], writes=["m1c"])
            p.op("DVE", lambda e, it=it: e.tensor_copy(
                out=m1ct[:, it:it + 1], in_=mc(M1C)),
                reads=["m1c"], writes=["m1ct"])
            p.op("DVE", lambda e: e.reciprocal(out=mc(R1), in_=mc(M1C)),
                 reads=["m1c"], writes=["r1"])
            p.op("DVE", lambda e: e.tensor_scalar(
                out=mc(S1V), in0=mc(R1), scalar1=127.0, scalar2=None,
                op0=ALU.mult), reads=["r1"], writes=["s1v"])
            p.op("DVE", lambda e, g=g: e.tensor_scalar(
                out=PV_xres[:], in0=PV_xln[g][:], scalar1=mc(S1V),
                scalar2=MAGIC, op0=ALU.mult, op1=ALU.add),
                reads=[f"xln{g}", "s1v"], writes=["xres"])
            p.op("DVE", lambda e: e.tensor_scalar(
                out=PV_xqbf[:], in0=PV_xres[:], scalar1=MAGIC, scalar2=None,
                op0=ALU.subtract), reads=["xres"], writes=["xqbf"])
            pv_pend.append(it)
        while pv_pend:
            pv_xpose_flush()
        p.op("PE", lambda e: e.transpose(
            out=B[5][0:NT, 0:128], in_=m1ct[:], identity=iden_f[:]),
            reads=["m1ct", "idenf"], writes=["B5"])
        p.op("DVE", lambda e: e.tensor_copy(out=mrow[:], in_=B[5][0:NT, 0:128]),
             reads=["B5"], writes=["mrow"])
        p.op("POOL", lambda e: e.dma_start(
            out=s_m1[b, 1:T + 1].rearrange("(i p) -> i p", p=128),
            in_=mrow[:]),
            reads=["mrow"], writes=[f"sm1_{b}"], dma=True)

    RWB1, RWB2 = 44, 45

    def prep():
        p.fence([f"pin{i}" for i in range(4)] + ["pob0", "pob1"]
                + [f"pap{u}" for u in range(32)],
                [f"P{jt}" for jt in range(NT)] + [f"v{t}" for t in range(NT)]
                + [f"qT{e}" for e in range(ECH)] + [f"kT{e}" for e in range(ECH)]
                + [f"WqT{e}" for e in range(ECH)] + [f"WkT{e}" for e in range(ECH)]
                + [f"WvT{e}" for e in range(ECH)]
                + [f"xT{e}_{t}" for e in range(ECH) for t in range(NT)]
                + ["xres", "xln0", "xln1", "xqstg", "xqbf", "pvx0", "pvx1",
                   "BVbc", "bvrow", "brow", "x32_0", "x32_1", "xbf0", "xbf1"])
        tiles = []
        for i in range(16):
            tiles.append(("w1", i))
        for i in range(16):
            tiles.append(("w2", i))

        def t_ap(which, i):
            if which == "w1":
                src_ap = w1_d[i * 256:(i + 1) * 256, :].rearrange(
                    "(p a) e -> p (a e)", p=128)
                dst_ap = s_w1i[i * 256:(i + 1) * 256, :].rearrange(
                    "(p a) e -> p (a e)", p=128)
                return src_ap, dst_ap, f"sw1ib{i}"
            sl = slice((i // 2) * 128, (i // 2 + 1) * 128)
            hc = i % 2
            return (w2_d[sl, hc * 2048:(hc + 1) * 2048],
                    s_w2i[sl, hc * 2048:(hc + 1) * 2048], f"sw2ib{i}")

        def load(u, which, i):
            src_ap, _, _ = t_ap(which, i)
            eng = "SP" if u % 2 == 0 else "ACT"
            p.op(eng, lambda e, u=u, src_ap=src_ap: e.dma_start(
                out=PR_in[u % 4][:], in_=src_ap),
                writes=[f"pin{u % 4}"], dma=True)

        def mean_chain(m):
            base = 16 * m
            p.op("DVE", lambda e, base=base: e.tensor_reduce(
                out=mc(MS), in_=PR_ap[:, base:base + 16], axis=AX, op=ALU.add),
                reads=[f"pap{base + u}" for u in range(16)], writes=["ms"])
            p.op("PE", lambda e: e.matmul(out=B[5][0:1, 0:1], lhsT=mc(MS),
                                          rhs=onescol[:], start=True, stop=True),
                 reads=["ms", "onescol"], writes=["B5"])
            p.op("DVE", lambda e, m=m: e.tensor_scalar(
                out=mw[:, 2 * m:2 * m + 1], in0=B[5][0:1, 0:1],
                scalar1=1.0 / (H * E), scalar2=QCLIP, op0=ALU.mult, op1=ALU.max),
                reads=["B5"], writes=[f"mw{m}"])
            p.op("DVE", lambda e, m=m: e.reciprocal(
                out=mw[:, 2 * m + 1:2 * m + 2], in_=mw[:, 2 * m:2 * m + 1]),
                reads=[f"mw{m}"], writes=[f"rw{m}"])
            p.op("PE", lambda e, m=m: e.matmul(
                out=B[5][:, 0:2], lhsT=ones1[:], rhs=mw[:, 2 * m:2 * m + 2],
                start=True, stop=True),
                reads=[f"mw{m}", f"rw{m}", "ones1"], writes=["B5"])
            p.op("DVE", lambda e, m=m: e.tensor_copy(
                out=mwb[:, m:m + 1], in_=B[5][:, 0:1]),
                reads=["B5"], writes=[f"mwb{m}"])
            p.op("DVE", lambda e, m=m: e.tensor_copy(
                out=misc[:, RWB1 + m:RWB1 + m + 1], in_=B[5][:, 1:2]),
                reads=["B5"], writes=[f"rwb{m + 1}"])

        for u, (which, i) in enumerate(tiles):
            load(u, which, i)
            if u % 2 == 0:
                p.op("DVE", lambda e, u=u: e.tensor_reduce(
                    out=PR_ap[:, u:u + 1], in_=PR_in[u % 4][:], axis=AX,
                    op=ALU.add, apply_absolute_value=True),
                    reads=[f"pin{u % 4}"], writes=[f"pap{u}"])
            else:
                p.op("ACT", lambda e, u=u: e.activation(
                    out=PR_in[u % 4][:], in_=PR_in[u % 4][:], func=AF.Abs,
                    accum_out=PR_ap[:, u:u + 1]),
                    reads=[f"pin{u % 4}"], writes=[f"pin{u % 4}", f"pap{u}"])
            if u == 15:
                mean_chain(0)
            elif u == 31:
                mean_chain(1)
        for u, (which, i) in enumerate(tiles):
            m = 0 if which == "w1" else 1
            _, dst_ap, skey = t_ap(which, i)
            load(32 + u, which, i)
            up = u % 4
            ob = u % 2
            p.op("ACT", lambda e, up=up, m=m: e.activation(
                out=PR_in[up][:], in_=PR_in[up][:], func=AF.Identity,
                scale=misc[:, RWB1 + m:RWB1 + m + 1], bias=misc[:, 47:48]),
                reads=[f"pin{up}", f"rwb{m + 1}", "magc"], writes=[f"pin{up}"])
            p.op("DVE", lambda e, up=up: e.tensor_scalar(
                out=PR_in[up][:], in0=PR_in[up][:], scalar1=MAGIC, scalar2=1.0,
                op0=ALU.subtract, op1=ALU.min),
                reads=[f"pin{up}"], writes=[f"pin{up}"])
            p.op("DVE", lambda e, up=up, ob=ob: e.tensor_scalar(
                out=PR_ob[ob][:], in0=PR_in[up][:], scalar1=-1.0, scalar2=None,
                op0=ALU.max),
                reads=[f"pin{up}"], writes=[f"pob{ob}"])
            p.op("POOL", lambda e, ob=ob, dst_ap=dst_ap: e.dma_start(
                out=dst_ap, in_=PR_ob[ob][:]),
                reads=[f"pob{ob}"], writes=[skey], dma=True)

    NGC = NB * NCHUNK

    def ffn_entry():
        newk = ([f"W1qT{e}" for e in range(ECH)]
                + [f"W2qT{c}" for c in range(HCH)]
                + [f"Fg{pr}_{s}" for pr in range(2) for s in range(HCH)]
                + [f"F_xq{pr}_{e}" for pr in range(2) for e in range(ECH)]
                + [f"fxlnt{pr}_{t}" for pr in range(2) for t in range(2)]
                + [f"fw{pr}" for pr in range(2)] + [f"fbc{pr}" for pr in range(2)]
                + ["sbc", "fs2seg", "Mx", "Mn", "flnt"]
                + [f"hs{i}" for i in range(2)] + [f"u{i}" for i in range(2)]
                + [f"fev{i}" for i in range(2)]
                + [f"D{k}_{pr}" for k in range(3) for pr in range(2)]
                + ["fcm0", "fcm1", "fcn0", "fcn1", "fm2", "fr2", "fs2v", "frs2"])
        oldk = ([f"P{jt}" for jt in range(NT)] + [f"v{t}" for t in range(NT)]
                + [f"qT{e}" for e in range(ECH)] + [f"kT{e}" for e in range(ECH)]
                + [f"xT{e}_{t}" for e in range(ECH) for t in range(NT)]
                + [f"WqT{e}" for e in range(ECH)]
                + [f"WkT{e}" for e in range(ECH)] + [f"WvT{e}" for e in range(ECH)]
                + ["xres", "xln0", "xln1", "xqstg", "xqbf", "pvx0", "pvx1",
                   "BVbc", "bvrow", "brow", "x32_0", "x32_1", "xbf0", "xbf1"]
                + [f"pin{i}" for i in range(4)] + [f"pap{u}" for u in range(32)]
                + ["pob0", "pob1", "m1ct", "mrow"]
                + [f"cmax{i}" for i in range(4)] + [f"csum{i}" for i in range(4)]
                + ["colsum", "msczero", "Mraw"])
        p.fence(newk, oldk)
        for e8 in range(ECH):
            p.op("SP", lambda e, e8=e8: e.dma_start(
                out=F_W1qT[e8][:], in_=s_w1i[:, e8 * 128:(e8 + 1) * 128],
                transpose=True),
                reads=[f"sw1ib{i}" for i in range(16)],
                writes=[f"W1qT{e8}"], dma=True, xbar=True)
        for c in range(HCH):
            p.op("SP", lambda e, c=c: e.dma_start(
                out=F_W2qT[c][:], in_=s_w2i[:, c * 128:(c + 1) * 128],
                transpose=True),
                reads=[f"sw2ib{i}" for i in range(16)],
                writes=[f"W2qT{c}"], dma=True, xbar=True)

    def ffn_loads(gc):
        b, ct = divmod(gc, NCHUNK)
        t0 = ct * 256
        par = gc % 2
        for e8 in range(ECH):
            p.op("SP", lambda e, e8=e8, b=b, t0=t0, par=par: e.dma_start(
                out=F_xq[par][e8][:],
                in_=s_xqT[b, e8, :, 8 + t0 - 1:8 + t0 + 257]),
                reads=[f"sxq{b}_{it}" for it in range(max(0, t0 // 128 - 1),
                                                      min(NT, t0 // 128 + 3))]
                + [f"sxq_zh{b}_{e8}", f"sxq_zt{b}_{e8}"],
                writes=[f"F_xq{par}_{e8}"], dma=True)
        p.op("SP", lambda e, b=b, t0=t0, par=par: e.dma_start(
            out=F_fw[par][:],
            in_=s_m1[b, t0:t0 + 258].rearrange("(o a) -> o a", o=1)),
            reads=[f"sm1_{b}", f"sm1z{b}", f"sm1z2{b}"],
            writes=[f"fw{par}"], dma=True)

    def ffn_loads_xlnt(gc):
        b, ct = divmod(gc, NCHUNK)
        t0 = ct * 256
        par = gc % 2
        for tb in range(2):
            sl = slice(t0 + tb * 128, t0 + (tb + 1) * 128)
            p.op("SP", lambda e, sl=sl, tb=tb, b=b, par=par: e.dma_start(
                out=F_xlnt[par][tb][:], in_=s_xln[b, sl, :]),
                reads=[f"sxln{b}_{sl.start // 128}"],
                writes=[f"fxlnt{par}_{tb}"], dma=True)

    def ffn_fbc(gc):
        par = gc % 2
        p.op("DVE", lambda e, par=par: e.tensor_scalar(
            out=F_fw[par][:], in0=F_fw[par][:], scalar1=mw[0:1, 0:1],
            scalar2=1.0 / 127.0, op0=ALU.mult, op1=ALU.mult),
            reads=[f"fw{par}", "mw0"], writes=[f"fw{par}"])
        p.op("PE", lambda e, par=par: e.matmul(
            out=B[3][:, 0:258], lhsT=ones1[:], rhs=F_fw[par][:],
            start=True, stop=True),
            reads=[f"fw{par}", "ones1"], writes=["B3s"])
        p.op("DVE", lambda e, par=par: e.tensor_copy(
            out=F_fbc[par][:], in_=B[3][:, 0:258]),
            reads=["B3s"], writes=[f"fbc{par}"])

    def ffn_w1(gc, s):
        par = gc % 2
        bk = s % 2
        for e8 in range(ECH):
            p.op("PE", lambda e, s=s, e8=e8, bk=bk, par=par: e.matmul(
                out=B[bk][:, 0:258],
                lhsT=F_W1qT[e8][:, s * 128:(s + 1) * 128],
                rhs=F_xq[par][e8][:],
                start=(e8 == 0), stop=(e8 == ECH - 1)),
                reads=[f"W1qT{e8}", f"F_xq{par}_{e8}"], writes=[f"B{bk}"])
        p.op("DVE", lambda e, s=s, bk=bk, par=par: e.tensor_tensor(
            out=F_hs[s % 2][:], in0=B[bk][:, 0:258], in1=F_fbc[par][:],
            op=ALU.mult),
            reads=[f"B{bk}", f"fbc{par}"], writes=[f"hs{s % 2}"])

    def ffn_dbuild(s):
        for k, wc, wk in ((0, w0c, "w0c"), (2, w2c, "w2c")):
            p.op("DVE", lambda e, k=k, s=s, wc=wc: e.tensor_scalar(
                out=F_D[k][s % 2][:], in0=iden_f[:], scalar1=wc[:, s:s + 1],
                scalar2=None, op0=ALU.mult),
                reads=["idenf", wk], writes=[f"D{k}_{s % 2}"])
        p.op("ACT", lambda e, s=s: e.activation(
            out=F_D[1][s % 2][:], in_=iden_f[:], func=AF.Identity,
            scale=w1c[:, s:s + 1]),
            reads=["idenf", "w1c"], writes=[f"D1_{s % 2}"])

    def ffn_conv(gc, s):
        par = gc % 2
        for k in range(3):
            p.op("PE", lambda e, k=k, s=s: e.matmul(
                out=B[2][:, 0:256], lhsT=F_D[k][s % 2][:],
                rhs=F_hs[s % 2][:, k:k + 256],
                start=(k == 0), stop=(k == 2)),
                reads=[f"D{k}_{s % 2}", f"hs{s % 2}"], writes=["B2c"])
        p.op("ACT", lambda e, s=s, par=par: e.activation(
            out=F_g[par][s][:], in_=B[2][:, 0:256], func=AF.Silu,
            bias=cbc[:, s:s + 1], scale=1.0),
            reads=["B2c", "cbc"], writes=[f"Fg{par}_{s}"])
        if s == 0:
            p.op("DVE", lambda e, par=par: e.tensor_copy(
                out=F_Mx[:], in_=F_g[par][0][:]),
                reads=[f"Fg{par}_0"], writes=["Mx"])
            p.op("DVE", lambda e, par=par: e.tensor_copy(
                out=F_Mn[:], in_=F_g[par][0][:]),
                reads=[f"Fg{par}_0"], writes=["Mn"])
        else:
            p.op("DVE", lambda e, s=s, par=par: e.tensor_tensor(
                out=F_Mx[:], in0=F_Mx[:], in1=F_g[par][s][:], op=ALU.max),
                reads=["Mx", f"Fg{par}_{s}"], writes=["Mx"])
            p.op("DVE", lambda e, s=s, par=par: e.tensor_tensor(
                out=F_Mn[:], in0=F_Mn[:], in1=F_g[par][s][:], op=ALU.min),
                reads=["Mn", f"Fg{par}_{s}"], writes=["Mn"])

    def ffn_scales_a(pgc):
        for i, key in ((0, "Mx"), (1, "Mn")):
            for tb in range(2):
                src = F_Mx if i == 0 else F_Mn
                p.op("PE", lambda e, i=i, tb=tb, src=src, key=key: e.transpose(
                    out=X6[:, (i * 2 + tb) * 128:(i * 2 + tb + 1) * 128],
                    in_=src[:, tb * 128:(tb + 1) * 128], identity=iden_bf[:]),
                    reads=[key, "idenb"], writes=["X6"])
        for tb in range(2):
            p.op("DVE", lambda e, tb=tb: e.tensor_reduce(
                out=misc[:, 16 + tb:17 + tb],
                in_=X6[:, tb * 128:(tb + 1) * 128], axis=AX, op=ALU.max),
                reads=["X6"], writes=[f"fcm{tb}"])
            p.op("DVE", lambda e, tb=tb: e.tensor_reduce(
                out=misc[:, 18 + tb:19 + tb],
                in_=X6[:, (2 + tb) * 128:(3 + tb) * 128], axis=AX, op=ALU.min,
                negate=True),
                reads=["X6"], writes=[f"fcn{tb}"])
        p.op("DVE", lambda e: e.tensor_tensor(
            out=misc[:, 20:22], in0=misc[:, 16:18], in1=misc[:, 18:20],
            op=ALU.max),
            reads=["fcm0", "fcm1", "fcn0", "fcn1"], writes=["fm2"])
        p.op("DVE", lambda e: e.tensor_scalar(
            out=misc[:, 20:22], in0=misc[:, 20:22], scalar1=QCLIP, scalar2=None,
            op0=ALU.max), reads=["fm2"], writes=["fm2"])
        p.op("DVE", lambda e: e.tensor_scalar(
            out=misc[:, 24:26], in0=misc[:, 20:22], scalar1=mwb[:, 1:2],
            scalar2=1.0 / 127.0, op0=ALU.mult, op1=ALU.mult),
            reads=["fm2", "mwb1"], writes=["frs2"])
        p.op("DVE", lambda e: e.reciprocal(out=misc[:, 22:24], in_=misc[:, 20:22]),
             reads=["fm2"], writes=["fr2"])
        p.op("DVE", lambda e: e.tensor_scalar(
            out=misc[:, 30:31], in0=misc[:, 22:23], scalar1=127.0, scalar2=None,
            op0=ALU.mult), reads=["fr2"], writes=["fs2v"])
        p.op("DVE", lambda e: e.tensor_scalar(
            out=misc[:, 62:63], in0=misc[:, 23:24], scalar1=127.0, scalar2=None,
            op0=ALU.mult), reads=["fr2"], writes=["fs2v"])

    def ffn_scales_b(pgc):
        import os as _os2
        _ksb = _os2.environ.get("KSB", "tcc")
        p.op("PE", lambda e: e.matmul(
            out=B[3][0:33, 384:512], lhsT=misc[:, 30:63], rhs=iden_f[:],
            start=True, stop=True, is_transpose=True),
            reads=["fs2v", "idenf"], writes=["B3s"])
        for half, row in ((0, 0), (1, 32))[:_ksb.count("c")]:
            p.op("DVE", lambda e, half=half, row=row: e.tensor_copy(
                out=F_s2seg[:, half * 128:(half + 1) * 128],
                in_=B[3][row:row + 1, 384:512]),
                reads=["B3s"], writes=["fs2seg"])

    def ffn_scales_c(pgc):
        p.op("PE", lambda e: e.matmul(
            out=B[3][:, 0:256], lhsT=ones1[:], rhs=F_s2seg[:],
            start=True, stop=True),
            reads=["fs2seg", "ones1"], writes=["B3s"])
        p.op("DVE", lambda e: e.tensor_copy(out=F_sbc[:], in_=B[3][:, 0:256]),
             reads=["B3s"], writes=["sbc"])

    def ffn_round(pgc, cs):
        par = pgc % 2
        up = cs % 2
        p.op("DVE", lambda e, cs=cs, par=par, up=up: e.tensor_tensor(
            out=F_u[up][:], in0=F_g[par][cs][:], in1=F_sbc[:], op=ALU.mult),
            reads=[f"Fg{par}_{cs}", "sbc"], writes=[f"u{up}"])
        p.op("DVE", lambda e, cs=cs, par=par, up=up: e.tensor_scalar(
            out=F_g[par][cs][:], in0=F_u[up][:], scalar1=MAGIC, scalar2=MAGIC,
            op0=ALU.add, op1=ALU.subtract),
            reads=[f"u{up}"], writes=[f"Fg{par}_{cs}"])

    def ffn_w2mm(pgc, grp, cs):
        par = pgc % 2
        tb, h = grp // 2, grp % 2
        bk = 4 + (grp % 2)
        p.op("PE", lambda e, par=par, tb=tb, h=h, cs=cs, bk=bk: e.matmul(
            out=B[bk][:],
            lhsT=F_g[par][cs][:, tb * 128:(tb + 1) * 128],
            rhs=F_W2qT[cs][:, h * 512:(h + 1) * 512],
            start=(cs == 0), stop=(cs == HCH - 1)),
            reads=[f"Fg{par}_{cs}", f"W2qT{cs}"], writes=[f"B{bk}"])

    def ffn_w2evac(pgc, grp):
        par = pgc % 2
        tb, h = grp // 2, grp % 2
        bk = 4 + (grp % 2)
        ev = grp % 2
        p.op("ACT", lambda e, bk=bk, ev=ev, tb=tb: e.activation(
            out=F_ev[ev][:], in_=B[bk][:], func=AF.Identity,
            scale=misc[:, 24 + tb:25 + tb]),
            reads=[f"B{bk}", "frs2"], writes=[f"fev{ev}"])
        p.op("DVE", lambda e, par=par, tb=tb, h=h, ev=ev: e.tensor_tensor(
            out=F_xlnt[par][tb][:, h * 512:(h + 1) * 512],
            in0=F_ev[ev][:], in1=F_xlnt[par][tb][:, h * 512:(h + 1) * 512],
            op=ALU.add),
            reads=[f"fev{ev}", f"fxlnt{par}_{tb}"], writes=[f"fxlnt{par}_{tb}"])

    def ffn_ln2(pgc, tb):
        b, ct = divmod(pgc, NCHUNK)
        par = pgc % 2
        t0 = ct * 256
        sl = slice(t0 + tb * 128, t0 + (tb + 1) * 128)
        xt = F_xlnt[par][tb]
        xk = f"fxlnt{par}_{tb}"
        p.op("DVE", lambda e, xt=xt: e.tensor_reduce(
            out=mc(MS), in_=xt[:], axis=AX, op=ALU.add),
            reads=[xk], writes=["ms"])
        p.op("DVE", lambda e: e.tensor_scalar(
            out=mc(NEGM), in0=mc(MS), scalar1=-1.0 / E, scalar2=None,
            op0=ALU.mult), reads=["ms"], writes=["negm"])
        for h, vs in ((0, VS1), (1, VS2)):
            p.op("ACT", lambda e, xt=xt, h=h, vs=vs: e.activation(
                out=F_lnt[:], in_=xt[:, h * 512:(h + 1) * 512],
                func=AF.Square, bias=mc(NEGM), scale=1.0,
                accum_out=misc[:, vs:vs + 1]),
                reads=[xk, "negm"], writes=["flnt", f"vs{h}"])
        p.op("DVE", lambda e: e.tensor_tensor(
            out=mc(VT), in0=mc(VS1), in1=mc(VS2), op=ALU.add),
            reads=["vs0", "vs1"], writes=["vt"])
        p.op("DVE", lambda e: e.tensor_scalar(
            out=mc(VT), in0=mc(VT), scalar1=1.0 / E, scalar2=LNEPS,
            op0=ALU.mult, op1=ALU.add), reads=["vt"], writes=["vt"])
        p.op("POOL", lambda e: e.tensor_tensor(
            out=mc(RSTD), in0=mc(VT), in1=neghalf[:], op=ALU.pow),
            reads=["vt", "neghalf"], writes=["rstd"])
        p.op("DVE", lambda e: e.tensor_scalar(
            out=mc(NEGMR), in0=mc(NEGM), scalar1=mc(RSTD), scalar2=None,
            op0=ALU.mult), reads=["negm", "rstd"], writes=["negmr"])
        p.op("ACT", lambda e, xt=xt: e.activation(
            out=xt[:], in_=xt[:], func=AF.Identity,
            scale=mc(RSTD), bias=mc(NEGMR)),
            reads=[xk, "rstd", "negmr"], writes=[xk])
        st = p.op("POOL", lambda e, sl=sl, xt=xt, b=b: e.dma_start(
            out=out_d[b, sl, :], in_=xt[:]),
            reads=[xk], writes=[f"outst{b}_{sl.start}"], dma=True)
        out_stores.append(st)

    def ffn_chunk(gc):
        if gc + 1 < NGC:
            ffn_loads(gc + 1)
        ffn_fbc(gc)
        ffn_dbuild(0)
        have_prev = gc > 0
        pgc = gc - 1
        import os as _os
        _kd = _os.environ.get("KDEF", "full")
        _lv = {"none": 0, "scales": 1, "round": 2, "w2": 3, "evac": 4,
               "full": 9}[_kd]
        W2S = {0: 12, 1: 14, 2: 20, 3: 22}
        for s in range(HCH):
            ffn_w1(gc, s)
            if s >= 1:
                ffn_conv(gc, s - 1)
            if s + 1 < HCH:
                ffn_dbuild(s + 1)
            if have_prev:
                if _lv >= 1:
                    _ksc = _os.environ.get("KSC", "abc")
                    if s == 0 and "a" in _ksc:
                        ffn_scales_a(pgc)
                    elif s == 2 and "b" in _ksc:
                        ffn_scales_b(pgc)
                    elif s == 3 and "c" in _ksc:
                        ffn_scales_c(pgc)
                if _lv >= 2 and 4 <= s <= 11:
                    for cs in range(4 * (s - 4), 4 * (s - 3)):
                        ffn_round(pgc, cs)
                if _lv >= 4:
                    if s == 20:
                        ffn_w2evac(pgc, 0)
                    elif s == 22:
                        ffn_w2evac(pgc, 1)
                    elif s == 28:
                        ffn_w2evac(pgc, 2)
                    elif s == 30:
                        ffn_w2evac(pgc, 3)
                if _lv >= 5:
                    if s == 23:
                        ffn_ln2(pgc, 0)
                    elif s == 31:
                        ffn_ln2(pgc, 1)
                if s == 31 and gc + 1 < NGC:
                    ffn_loads_xlnt(gc + 1)
                if _lv >= 3:
                    for grp, s0 in W2S.items():
                        if s0 <= s < s0 + 8:
                            for cs in range(4 * (s - s0), 4 * (s - s0 + 1)):
                                ffn_w2mm(pgc, grp, cs)
        if not have_prev and gc + 1 < NGC:
            ffn_loads_xlnt(gc + 1)
        ffn_conv(gc, HCH - 1)

    def ffn_tail():
        pgc = NGC - 1
        ffn_scales_a(pgc)
        ffn_scales_b(pgc)
        ffn_scales_c(pgc)
        for cs in range(HCH):
            ffn_round(pgc, cs)
        for grp in range(4):
            for cs in range(HCH):
                ffn_w2mm(pgc, grp, cs)
            ffn_w2evac(pgc, grp)
            if grp == 1:
                ffn_ln2(pgc, 0)
        ffn_ln2(pgc, 1)

    out_stores = []
    import os as _os
    _cut = _os.environ.get("KCUT", "full")
    for b in range(NB):
        attn(b)
    if _cut != "attn":
        if _cut != "noprep":
            prep()
        if _cut not in ("prep", "noprep"):
            ffn_entry()
            ffn_loads(0)
            ffn_loads_xlnt(0)
            if _cut != "entry":
                nch = NGC if _cut == "full" else min(NGC, int(_cut[1:]))
                for gc in range(nch):
                    ffn_chunk(gc)
                if _cut == "full":
                    ffn_tail()
    p.op("POOL", lambda e: e.nop(), deps=out_stores)

    import contextlib as _ctx
    with _ctx.ExitStack() as st:
        block = st.enter_context(nc.Block())
        sems = {e: st.enter_context(nc.semaphore("sem" + e)) for e in Prog.ENGINES}
        p.build(nc, block, sems)
    pstack.close()
    return nc


def kernel(L0_attn_X, L0_attn_Wq, L0_attn_bq, L0_attn_Wk, L0_attn_bk,
           L0_attn_Wv, L0_attn_bv, L0_ffn_w1, L0_ffn_conv_w, L0_ffn_conv_b,
           L0_ffn_w2):
    X = np.ascontiguousarray(np.asarray(L0_attn_X, dtype=np.float32))
    cw = np.asarray(L0_ffn_conv_w, dtype=np.float32).reshape(4096, 3)
    common = {
        "wq": np.ascontiguousarray(np.asarray(L0_attn_Wq, np.float32)),
        "wk": np.ascontiguousarray(np.asarray(L0_attn_Wk, np.float32)),
        "wv": np.ascontiguousarray(np.asarray(L0_attn_Wv, np.float32)),
        "bq": np.ascontiguousarray(np.asarray(L0_attn_bq, np.float32)),
        "bk": np.ascontiguousarray(np.asarray(L0_attn_bk, np.float32)),
        "bv": np.ascontiguousarray(np.asarray(L0_attn_bv, np.float32)),
        "w1": np.ascontiguousarray(np.asarray(L0_ffn_w1, np.float32)),
        "w2": np.ascontiguousarray(np.asarray(L0_ffn_w2, np.float32)),
        "cw0": np.ascontiguousarray(cw[:, 0]),
        "cw1": np.ascontiguousarray(cw[:, 1]),
        "cw2": np.ascontiguousarray(cw[:, 2]),
        "cb": np.ascontiguousarray(np.asarray(L0_ffn_conv_b, np.float32)),
    }
    n_cores = 8
    nb = X.shape[0] // n_cores
    nc = build_program(T=X.shape[1], NB=nb)
    in_maps = []
    for c in range(n_cores):
        m = dict(common)
        m["x"] = np.ascontiguousarray(X[c * nb:(c + 1) * nb])
        in_maps.append(m)
    res = run_bass_kernel_spmd(nc, in_maps, core_ids=list(range(n_cores)))
    out = np.concatenate([np.asarray(r["out"]) for r in res.results], axis=0)
    return out.astype(np.float32)
